# version 11
# baseline (speedup 1.0000x reference)
import contextlib
import numpy as np
import concourse.bass as bass
import concourse.mybir as mybir
from concourse.bass_utils import run_bass_kernel_spmd

F32 = mybir.dt.float32
BF16 = mybir.dt.bfloat16
AF = mybir.ActivationFunctionType
ALU = mybir.AluOpType

NCORES = 8
B, L, D = 2, 8192, 1024
SEG = 1024
TOK = 2 * SEG
T = 512
NCH = 4
TE = T + 3
EPS = 1e-5
RG = [list(range(8))]



class Ctx:
    def __init__(self, nc, stack):
        self.nc = nc
        self.stacks = [stack]
        self.sems = {}
        self.scope_id = 0

    @property
    def stack(self):
        return self.stacks[-1]

    @contextlib.contextmanager
    def scope(self):
        self.scope_id += 1
        es = contextlib.ExitStack()
        self.stacks.append(es)
        try:
            yield
        finally:
            self.stacks.pop()
            es.close()

    def sem(self, name):
        if name not in self.sems:
            self.sems[name] = [self.stacks[0].enter_context(self.nc.semaphore(name)), 0]
        return self.sems[name]


def sb(ctx, name, shape, dtype):
    return ctx.stack.enter_context(
        ctx.nc.sbuf_tensor(f"s{ctx.scope_id}_{name}", shape, dtype))


def ps(ctx, name, shape, dtype=F32):
    return ctx.stack.enter_context(
        ctx.nc.psum_tensor(f"s{ctx.scope_id}_{name}", shape, dtype))


ENGINES = ("sync", "tensor", "vector", "scalar", "gpsimd")
NDMASLOT = 8


class P:

    def __init__(self, ctx):
        self.ctx = ctx
        self.ops = []
        self.cnt = {e: 0 for e in ENGINES}
        self.dcnt = {}
        self.didx = {e: 0 for e in ENGINES}
        self.dlast = {}
        self.ccnt = 0
        self.cclast = None

    def comp(self, eng, fn, deps=()):
        self.cnt[eng] += 1
        key = "E_" + eng
        tok = (key, self.cnt[eng])
        self.ops.append((eng, fn, list(deps), 1, key))
        return tok

    def dma(self, eng, out, in_, deps=()):
        slot = self.didx[eng] % NDMASLOT
        self.didx[eng] += 1
        key = f"D_{eng}{slot}"
        deps = list(deps) + ([self.dlast[key]] if key in self.dlast else [])
        self.dcnt[key] = self.dcnt.get(key, 0) + 16
        tok = (key, self.dcnt[key])
        self.dlast[key] = tok
        self.ops.append((eng, lambda E, o=out, i=in_: E.dma_start(out=o, in_=i),
                         deps, 16, key))
        return tok

    def coll(self, fn, deps=()):
        self.ccnt += 1
        key = "CC"
        deps = list(deps) + ([self.cclast] if self.cclast else [])
        tok = (key, self.ccnt)
        self.cclast = tok
        self.ops.append(("gpsimd", fn, deps, 1, key))
        return tok

    def barrier(self):
        toks = [("E_" + e, self.cnt[e]) for e in ENGINES if self.cnt[e] > 0]
        toks += [(k, v) for k, v in self.dcnt.items()]
        if self.ccnt:
            toks.append(("CC", self.ccnt))
        return toks

    def emit(self):
        nc = self.ctx.nc
        semof = {}
        for e in ENGINES:
            semof["E_" + e] = self.ctx.sem("E_" + e)[0]
        for k in self.dcnt:
            semof[k] = self.ctx.sem(k)[0]
        if self.ccnt:
            semof["CC"] = self.ctx.sem("CC")[0]
        ops = self.ops
        final_dma = dict(self.dcnt)
        final_cnt = dict(self.cnt)
        final_cc = self.ccnt

        def make(eng_name):
            def run(E):
                mykey = "E_" + eng_name
                waited = {}
                for (eng, fn, deps, inc, key) in ops:
                    if eng != eng_name:
                        continue
                    for (dk, dv) in deps:
                        if dv <= waited.get(dk, 0):
                            continue
                        E.wait_ge(semof[dk], dv)
                        waited[dk] = dv
                    instr = fn(E)
                    if isinstance(instr, list):
                        instr = instr[-1]
                    instr.then_inc(semof[key], inc)
                if final_cnt[eng_name] > 0:
                    E.wait_ge(semof[mykey], final_cnt[eng_name])
                for s in range(NDMASLOT):
                    k = f"D_{eng_name}{s}"
                    if k in final_dma:
                        E.wait_ge(semof[k], final_dma[k])
                if eng_name == "gpsimd" and final_cc:
                    E.wait_ge(semof["CC"], final_cc)
            return run

        with nc.Block("prog") as blk:
            blk.sync(make("sync"))
            blk.tensor(make("tensor"))
            blk.vector(make("vector"))
            blk.scalar(make("scalar"))
            blk.gpsimd(make("gpsimd"))


class Konst:
    def __init__(self, ctx, p, n=24):
        self.tile = sb(ctx, "konst", [128, n], F32)
        self.p = p
        self.vals = {}
        self.toks = []
        self.n = n

    def ap(self, v):
        v = float(v)
        if v not in self.vals:
            i = len(self.vals)
            assert i < self.n
            self.vals[v] = i
            self.toks.append(self.p.comp(
                "vector", lambda E, i=i, v=v: E.memset(self.tile[:, i:i + 1], v)))
        i = self.vals[v]
        return self.tile[:, i:i + 1]

    def row(self, v):
        return self.ap(v)[0:1, :]

    def dep(self):
        return list(self.toks)


class LnTmp:
    def __init__(self, ctx, T_, tag=""):
        self.T = T_
        self.sq = sb(ctx, f"ln_sq{tag}", [128, T_], F32)
        self.r_mean = sb(ctx, f"ln_rm{tag}", [1, T_], F32)
        self.r_msq = sb(ctx, f"ln_rq{tag}", [1, T_], F32)
        self.r_tmp = sb(ctx, f"ln_rt{tag}", [1, T_], F32)
        self.r_rstd = sb(ctx, f"ln_rr{tag}", [1, T_], F32)
        self.t1 = sb(ctx, f"ln_t1{tag}", [128, T_], F32)
        self.ps_s = ps(ctx, f"ln_ps_s{tag}", [1, T_], F32)
        self.ps_q = ps(ctx, f"ln_ps_q{tag}", [1, T_], F32)
        self.ps_mb = ps(ctx, f"ln_ps_mb{tag}", [128, T_], F32)
        self.ps_rb = ps(ctx, f"ln_ps_rb{tag}", [128, T_], F32)


def emit_ln(p, ln, xt, nt, ones_col, ones_row, out16, dep_x, kn, T_=None,
            out_f32=None, col0=0, ocol0=0):
    T_ = T_ or ln.T
    inv = 1.0 / (nt * 128)
    mm_toks = []
    sq_toks = []
    last_qmm = []
    for i in range(nt):
        tsq = p.comp("scalar", lambda E, i=i: E.activation(
            out=ln.sq[:, 0:T_], in_=xt[:, i, col0:col0 + T_], func=AF.Square),
            deps=dep_x + last_qmm + sq_toks[-1:])
        sq_toks.append(tsq)
        mm_toks.append(p.comp("tensor", lambda E, i=i: E.matmul(
            out=ln.ps_s[:, 0:T_], lhsT=ones_col, rhs=xt[:, i, col0:col0 + T_],
            start=(i == 0), stop=(i == nt - 1)), deps=dep_x))
        last_qmm = [p.comp("tensor", lambda E, i=i: E.matmul(
            out=ln.ps_q[:, 0:T_], lhsT=ones_col, rhs=ln.sq[:, 0:T_],
            start=(i == 0), stop=(i == nt - 1)), deps=[tsq])]
        mm_toks.append(last_qmm[0])
    mm = [mm_toks[-1]]
    a_inv = kn.row(inv)
    a_eps = kn.row(EPS)
    a_mh = kn.row(-0.5)
    kd = kn.dep()
    t_mean = p.comp("scalar", lambda E: E.activation(
        out=ln.r_mean[:, 0:T_], in_=ln.ps_s[:, 0:T_], func=AF.Copy,
        scale=a_inv), deps=mm + kd)
    t_msq = p.comp("scalar", lambda E: E.activation(
        out=ln.r_msq[:, 0:T_], in_=ln.ps_q[:, 0:T_], func=AF.Copy,
        scale=a_inv), deps=mm + kd)
    t_m2 = p.comp("scalar", lambda E: E.activation(
        out=ln.r_tmp[:, 0:T_], in_=ln.r_mean[:, 0:T_], func=AF.Square), deps=[t_mean])
    t_var = p.comp("vector", lambda E: E.tensor_sub(
        out=ln.r_msq[:, 0:T_], in0=ln.r_msq[:, 0:T_], in1=ln.r_tmp[:, 0:T_]),
        deps=[t_msq, t_m2])
    t_ln = p.comp("scalar", lambda E: E.activation(
        out=ln.r_tmp[:, 0:T_], in_=ln.r_msq[:, 0:T_], func=AF.Ln,
        bias=a_eps), deps=[t_var] + kd)
    t_rstd = p.comp("scalar", lambda E: E.activation(
        out=ln.r_rstd[:, 0:T_], in_=ln.r_tmp[:, 0:T_], func=AF.Exp,
        scale=a_mh), deps=[t_ln] + kd)
    t_mb = p.comp("tensor", lambda E: E.matmul(
        out=ln.ps_mb[:, 0:T_], lhsT=ones_row, rhs=ln.r_mean[:, 0:T_],
        start=True, stop=True), deps=[t_mean, mm[0]])
    t_rb = p.comp("tensor", lambda E: E.matmul(
        out=ln.ps_rb[:, 0:T_], lhsT=ones_row, rhs=ln.r_rstd[:, 0:T_],
        start=True, stop=True), deps=[t_rstd])
    outdeps = []
    for i in range(nt):
        ta = p.comp("vector", lambda E, i=i: E.tensor_sub(
            out=ln.t1[:, 0:T_], in0=xt[:, i, col0:col0 + T_], in1=ln.ps_mb[:, 0:T_]),
            deps=dep_x + [t_mb] + outdeps[-1:])
        tb = p.comp("vector", lambda E, i=i: E.tensor_mul(
            out=out16[:, i, ocol0:ocol0 + T_], in0=ln.t1[:, 0:T_], in1=ln.ps_rb[:, 0:T_]),
            deps=[ta, t_rb])
        outdeps.append(tb)
        if out_f32 is not None:
            outdeps.append(p.comp("vector", lambda E, i=i: E.tensor_mul(
                out=out_f32[:, i, 0:T_], in0=ln.t1[:, 0:T_], in1=ln.ps_rb[:, 0:T_]),
                deps=[ta, t_rb]))
    return outdeps, (t_mb, t_rb)


class MS2:

    def __init__(self):
        self.bank_rd = {0: [], 1: []}


def mm_stream2(p, st, stg, w16, w_ap, nkt, nmt, rhs_ap, evac, banks, dep_rhs,
               bar, ext=None):
    queues = ("sync", "scalar")
    if not hasattr(p, "_ms2cv"):
        p._ms2cv = []
        p._ms2rd = {0: [], 1: []}
    cv = p._ms2cv
    out_toks = []
    for mt in range(nmt):
        g = len(cv)
        q = g % 2
        sl = g % 2
        tld = p.dma(queues[q], stg[sl][:, 0:nkt, :], w_ap(mt),
                    deps=([cv[-2]] if len(cv) >= 2 else bar))
        tcv = p.comp("scalar", lambda E, sl=sl, nkt=nkt: E.activation(
            out=w16[sl][:, 0:nkt, :], in_=stg[sl][:, 0:nkt, :], func=AF.Copy),
            deps=[tld] + p._ms2rd[sl])
        cv.append(tcv)
        bank = banks[mt % 2]
        tmm = None
        for kt in range(nkt):
            deps = [tcv]
            if kt == 0:
                deps = deps + dep_rhs + st.bank_rd[mt % 2]
            tmm = p.comp("tensor", lambda E, sl=sl, kt=kt, bank=bank: E.matmul(
                out=bank[:, :], lhsT=w16[sl][:, kt, :], rhs=rhs_ap(kt),
                start=(kt == 0), stop=(kt == nkt - 1)), deps=deps)
        tev = evac(mt, bank, [tmm])
        out_toks.append(tev)
        st.bank_rd[mt % 2] = [tev]
        p._ms2rd[sl] = [tmm]
        if ext is not None and mt < ext[4]:
            rhs2_ap, evac2, n2, dep_rhs2 = ext[0], ext[1], ext[2], ext[3]
            tm2 = None
            for kt in range(nkt):
                deps = [tcv]
                if kt == 0:
                    deps = deps + list(dep_rhs2) + [tev]
                tm2 = p.comp("tensor", lambda E, sl=sl, kt=kt, bank=bank, n2=n2: E.matmul(
                    out=bank[:, 0:n2], lhsT=w16[sl][:, kt, :], rhs=rhs2_ap(kt),
                    start=(kt == 0), stop=(kt == nkt - 1)), deps=deps)
            tev2 = evac2(mt, bank, [tm2])
            out_toks.append(tev2)
            st.bank_rd[mt % 2] = [tev, tev2]
            p._ms2rd[sl] = [tm2]
    return out_toks


def mm_stream(p, stages, w16s, w_ap, nkt, nmt, rhs_ap, evac, banks, dep_rhs,
              bar, ktp=128, nout=None):
    queues = ("sync", "gpsimd")
    ev_hist = []
    if not hasattr(p, "_mm_cv"):
        p._mm_cv = []
        p._mm_mm = []
    cv_hist = p._mm_cv
    mm_hist = p._mm_mm
    out_toks = []
    for mt in range(nmt):
        bank = banks[mt % 2]
        tmm = None
        for kt in range(nkt):
            g = len(cv_hist)
            q = g % 2
            s = g % 4
            tld = p.dma(queues[q], stages[q][:], w_ap(kt, mt),
                        deps=(cv_hist[-2:-1] if len(cv_hist) >= 2 else bar))
            tcv = p.comp("vector", lambda E, s=s, q=q: E.tensor_copy(
                out=w16s[:, s, :], in_=stages[q][:]),
                deps=[tld] + (mm_hist[-4:-3] if len(mm_hist) >= 4 else []))
            cv_hist.append(tcv)
            deps = [tcv]
            if kt == 0:
                deps += dep_rhs + ev_hist[-2:-1]
            tmm = p.comp("tensor", lambda E, s=s, kt=kt, bank=bank: E.matmul(
                out=bank[:, :] if nout is None else bank[:, 0:nout],
                lhsT=w16s[:, s, :], rhs=rhs_ap(kt),
                start=(kt == 0), stop=(kt == nkt - 1)), deps=deps)
            mm_hist.append(tmm)
        tev = evac(mt, bank, [tmm])
        ev_hist.append(tev)
        out_toks.append(tev)
    return out_toks




def emit_gather(p, ctx, G, src_ap, cc_in, cc_out, ncols, deps):
    slot = [sb(ctx, f"agslot{i}", [128, ncols], F32) for i in range(2)]
    slot_rd = {0: [], 1: []}
    dts = []
    for r in range(8):
        sl = r % 2
        tv = p.comp("vector", lambda E, r=r, sl=sl: E.tensor_scalar(
            out=slot[sl][:, :], in0=src_ap, scalar1=G.ownm[:, r:r + 1],
            scalar2=None, op0=ALU.mult), deps=deps + slot_rd[sl])
        td = p.dma("gpsimd", cc_in[r * 128:(r + 1) * 128, :], slot[sl][:, :],
                   deps=[tv])
        slot_rd[sl] = [td]
        dts.append(td)
    tc = p.coll(lambda E: E.collective_compute(
        "AllReduce", ALU.add, replica_groups=RG,
        ins=[cc_in.opt()], outs=[cc_out.opt()]), deps=dts)
    return tc


def emit_fold(p, ctx, G, cc_out, Htile, tc):
    fi = [sb(ctx, f"fold{i}", [128, 1024], F32) for i in range(2)]
    pmt = sb(ctx, "pmt", [128, 512], F32)
    t0 = p.comp("vector", lambda E: E.memset(Htile[:, :], 0.0), deps=[tc])
    fi_rd = {0: [], 1: []}
    th2 = t0
    for r in range(8):
        sl = r % 2
        tl = p.dma("sync", fi[sl][:, :], cc_out[r * 128:(r + 1) * 128, :],
                   deps=[tc] + fi_rd[sl])
        tpm = p.comp("vector", lambda E, r=r, sl=sl: E.tensor_scalar(
            out=pmt[:, :], in0=fi[sl][:, 512:1024], scalar1=G.pm1[:, r:r + 1],
            scalar2=G.pm0[:, r:r + 1], op0=ALU.mult, op1=ALU.add),
            deps=[tl, th2])
        th1 = p.comp("vector", lambda E: E.tensor_mul(
            out=Htile[:, :], in0=Htile[:, :], in1=pmt[:, :]), deps=[tpm])
        th2 = p.comp("vector", lambda E, r=r, sl=sl: E.scalar_tensor_tensor(
            out=Htile[:, :], in0=fi[sl][:, 0:512], scalar=G.pm1[:, r:r + 1],
            in1=Htile[:, :], op0=ALU.mult, op1=ALU.add), deps=[th1])
        fi_rd[sl] = [th2]
    return [th2]




NR = 3


def scan_A(p, ctx, mb, ch, delta16, xi_t, bbc16, cbc16, deltad, bc_d, conv_d, wc):
    b, j = ch // 2, ch % 2
    st_toks = []
    for dt in range(16):
        dsl = dt % 2
        init_ds = mb.kzero if j == 0 else mb.dstate[:, dt:dt + 1]
        t_ds = p.comp("vector", lambda E, dt=dt, dsl=dsl, init_ds=init_ds: E.tensor_tensor_scan(
            out=mb.Ds[:, dsl, :], data0=delta16[:, dt, :], data1=mb.z16zero[:],
            initial=init_ds, op0=ALU.add, op1=ALU.add),
            deps=deltad + wc + mb.ds_rd[dsl])
        t_dss = p.comp("vector", lambda E, dt=dt, dsl=dsl: E.tensor_copy(
            out=mb.dstate[:, dt:dt + 1], in_=mb.Ds[:, dsl, T - 1:T]), deps=[t_ds])
        t_sds = p.dma("sync", mb.scr_ds[ch, dt], mb.Ds[:, dsl, :], deps=[t_ds])
        st_toks.append(t_sds)
        mb.ds_rd[dsl] = [t_dss, t_sds]
        t_du = p.comp("vector", lambda E, dt=dt, dsl=dsl: E.tensor_mul(
            out=mb.du16[:, dsl, :], in0=delta16[:, dt, :], in1=xi_t[:, dt, :]),
            deps=deltad + conv_d + mb.du_rd[dsl])
        t_h = None
        t_hv = {0: None, 1: None}
        for n in range(16):
            sl = mb.g % NR
            s2 = mb.g % 2
            mb.g += 1
            acol_ap = mb.acol_fn(dt, n)
            t_dA = p.comp("scalar", lambda E, dt=dt, sl=sl, acol_ap=acol_ap: E.activation(
                out=mb.dA[:, sl, :], in_=delta16[:, dt, :], func=AF.Exp,
                scale=acol_ap), deps=deltad + wc + mb.da_rd[sl])
            t_u = p.comp("vector", lambda E, n=n, s2=s2, dsl=dsl: E.tensor_mul(
                out=mb.u16[:, s2, :], in0=mb.du16[:, dsl, :], in1=bbc16[:, n, :]),
                deps=[t_du] + bc_d + mb.u_rd[s2])
            scol = mb.spS(b, dt, n)
            init_h = mb.kzero if j == 0 else scol
            eng_h = "vector"
            t_h = p.comp(eng_h, lambda E, sl=sl, s2=s2, init_h=init_h: E.tensor_tensor_scan(
                out=mb.hS[:, s2, :], data0=mb.dA[:, sl, :], data1=mb.u16[:, s2, :],
                initial=init_h, op0=ALU.mult, op1=ALU.add),
                deps=[t_dA, t_u] + mb.h_rd[s2])
            t_hv[s2] = t_h
            mb.da_rd[sl] = [t_h]
            mb.u_rd[s2] = [t_h]
            t_cp = p.comp("gpsimd", lambda E, s2=s2, scol=scol: E.tensor_copy(
                out=scol, in_=mb.hS[:, s2, T - 1:T]), deps=[t_h])
            eng_t = "gpsimd"
            t_tmp = p.comp(eng_t, lambda E, n=n, sl=sl, s2=s2: E.tensor_mul(
                out=mb.tmp16[:, sl, :], in0=mb.hS[:, s2, :], in1=cbc16[:, n, :]),
                deps=[t_h] + bc_d + mb.tmp_rd[sl])
            mb.h_rd[s2] = [t_cp, t_tmp]
            ymm = p.comp("tensor", lambda E, n=n, sl=sl: E.matmul(
                out=mb.ybank[:, :], lhsT=mb.eye16[:], rhs=mb.tmp16[:, sl, :],
                start=(n == 0), stop=(n == 15)),
                deps=[t_tmp] + wc + (mb.last_ya if n == 0 else []))
            mb.tmp_rd[sl] = [ymm]
            mb.cbc_rd = [t_tmp]
        mb.du_rd[dsl] = [t for t in t_hv.values() if t is not None]
        ysl = dt % 2
        t_ev = p.comp("scalar", lambda E, ysl=ysl: E.activation(
            out=mb.ysb[:, ysl, :], in_=mb.ybank[:, :], func=AF.Copy),
            deps=mb.tmp_rd[(mb.g - 1) % NR] + mb.ysb_rd[ysl])
        mb.last_ya = [t_ev]
        t_sy = p.dma("scalar", mb.scr_y[ch, dt], mb.ysb[:, ysl, :], deps=[t_ev])
        mb.ysb_rd[ysl] = [t_sy]
        st_toks.append(t_sy)
        if j == 1:
            pts = []
            for n in range(16):
                pcol = mb.spP(b, dt, n)
                pts.append(p.comp("scalar", lambda E, pcol=pcol, dsl=dsl,
                                  acol_ap=mb.acol_fn(dt, n): E.activation(
                    out=pcol, in_=mb.Ds[:, dsl, T - 1:T], func=AF.Exp,
                    scale=acol_ap), deps=[t_ds] + wc))
            mb.ds_rd[dsl] = mb.ds_rd[dsl] + pts
    return st_toks


def corr_B(p, ctx, mb, ch, cbc16, y2, bar, wcH, bc_d):
    b = ch // 2
    y2d = []
    for dt in range(16):
        s2 = dt % 2
        tld_ds = p.dma("sync", mb.dsb[:, s2, :], mb.scr_ds[ch, dt],
                       deps=bar + mb.dsb_rd[s2])
        tld_y = p.dma("scalar", mb.yacc[:, s2, :], mb.scr_y[ch, dt],
                      deps=bar + mb.yacc_rd[s2])
        fma_last = [tld_y]
        t_dec = None
        eng = "vector" if s2 == 0 else "gpsimd"
        for n in range(16):
            sl = mb.g % NR
            mb.g += 1
            t_dec = p.comp("scalar", lambda E, s2=s2, sl=sl, acol_ap=mb.acol_fn(dt, n): E.activation(
                out=mb.dec[:, sl, :], in_=mb.dsb[:, s2, :], func=AF.Exp,
                scale=acol_ap), deps=[tld_ds] + mb.dec_rd[sl])
            st = s2 * 2 + (n % 2)
            eng_t2 = "vector" if (s2 == 0 and n % 3 == 0) else "gpsimd"
            t_t2 = p.comp(eng_t2, lambda E, n=n, sl=sl, st=st: E.tensor_mul(
                out=mb.tmpc2[:, st, :], in0=mb.dec[:, sl, :], in1=cbc16[:, n, :]),
                deps=[t_dec] + bc_d + mb.tmpc_rd[st])
            mb.dec_rd[sl] = [t_t2]
            hcol = mb.Hcol(b, dt, n)
            t_fma = p.comp("vector", lambda E, s2=s2, st=st, hcol=hcol: E.scalar_tensor_tensor(
                out=mb.yacc[:, s2, :], in0=mb.tmpc2[:, st, :], scalar=hcol,
                in1=mb.yacc[:, s2, :], op0=ALU.mult, op1=ALU.add),
                deps=[t_t2] + fma_last + wcH)
            fma_last = [t_fma]
            mb.tmpc_rd[st] = [t_fma]
        mb.dsb_rd[s2] = [t_dec]
        if mb.is_l0:
            tld_xi = p.dma("sync", mb.xib[:, s2, :], mb.scr_xi[ch, dt],
                           deps=bar + mb.xib_rd[s2])
            tld_z = p.dma("scalar", mb.zb[:, s2, :], mb.scr_z[ch, dt],
                          deps=bar + mb.zb_rd[s2])
            t_sz = p.comp("scalar", lambda E, s2=s2: E.activation(
                out=mb.szb[:, s2, :], in_=mb.zb[:, s2, :], func=AF.Silu),
                deps=[tld_z] + mb.szb_rd[s2])
            t_ya = p.comp("vector", lambda E, s2=s2, dt=dt: E.scalar_tensor_tensor(
                out=mb.yacc[:, s2, :], in0=mb.xib[:, s2, :],
                scalar=mb.d_c[:, dt:dt + 1], in1=mb.yacc[:, s2, :],
                op0=ALU.mult, op1=ALU.add), deps=[tld_xi] + fma_last)
            t_y2 = p.comp(eng, lambda E, s2=s2, dt=dt: E.tensor_mul(
                out=y2[:, dt, :], in0=mb.yacc[:, s2, :], in1=mb.szb[:, s2, :]),
                deps=[t_ya, t_sz])
            mb.xib_rd[s2] = [t_ya]
            mb.zb_rd[s2] = [t_sz]
            mb.szb_rd[s2] = [t_y2]
            mb.yacc_rd[s2] = [t_y2]
        else:
            tld_g = p.dma("sync", mb.xib[:, s2, :], mb.scr_z[ch, dt],
                          deps=bar + mb.xib_rd[s2])
            t_y2 = p.comp(eng, lambda E, s2=s2, dt=dt: E.tensor_mul(
                out=y2[:, dt, :], in0=mb.yacc[:, s2, :], in1=mb.xib[:, s2, :]),
                deps=[tld_g] + fma_last)
            mb.xib_rd[s2] = [t_y2]
            mb.yacc_rd[s2] = [t_y2]
        y2d.append(t_y2)
    return y2d


def emit_cbc(p, ctx, mb, ch, cbc16, bar, extra=()):
    bc_d = []
    bmm = []
    for n in range(16):
        tmv = p.dma("sync", mb.bc1[0:1, n % 2, 0:T], mb.scr_cr[ch, n],
                    deps=bar + bmm[-2:-1] + list(extra))
        tb = p.comp("tensor", lambda E, n=n: E.matmul(
            out=mb.bcbank[:, :], lhsT=mb.ones_row[:], rhs=mb.bc1[0:1, n % 2, 0:T],
            start=True, stop=True), deps=[tmv] + bc_d[-1:])
        bmm.append(tb)
        bc_d.append(p.comp("scalar", lambda E, n=n: E.activation(
            out=cbc16[:, n, :], in_=mb.bcbank[:, :], func=AF.Copy),
            deps=[tb]))
    return bc_d


class MB:
    pass



_CACHE = {}


def build_fused(stop_after=99):
    nc = bass.Bass("TRN2", target_bir_lowering=False)

    def din(name, shape, dtype=F32):
        return nc.dram_tensor(name, shape, dtype, kind="ExternalInput").ap()

    x0e = din("x0e", [D, 2 * (SEG + 3)])
    eye_in = din("eye", [128, 128])
    ownm_in = din("ownm", [128, 8])
    pm1_in = din("pm1", [128, 8])
    hsel_in = din("hsel", [128, 8])
    xt_out = nc.dram_tensor("xt_new", [D, TOK], F32, kind="ExternalOutput").ap()
    dbg = (nc.dram_tensor("dbg", [D, TOK], F32, kind="ExternalOutput").ap()
           if stop_after < 99 else None)
    ip_in = din("ip2", [32, 128, 8, 128])
    op_in = din("op2", [8, 128, 16, 128])
    xp_in = din("xp_l", [128, 16, 96])
    dtw_in = din("dtw_l", [64, 16, 128])
    rms_in = din("rms_c", [128, 8])
    convw_in = din("convw_c", [128, 16, 4])
    convb_in = din("convb_c", [128, 16])
    dtb_in = din("dtb_c", [128, 16])
    alog_in = din("alog_c", [128, 16, 16])
    d_in = din("d_c", [128, 16])
    win_in = din("win_l", [128, 8, 64])
    bin_in = din("bin_c", [64, 1])
    w1_in = din("w1", [64, 16])
    b1_in = din("b1", [64, 16])
    w2_in = din("w2", [64, 16])
    b2_in = din("b2", [64, 1])
    wout_in = din("wout_l", [64, 8, 128])
    bout_in = din("bout_c", [128, 8])
    wq_in = din("wq_t", [128, 8, D])
    wk_in = din("wk_t", [128, 8, D])
    wv_in = din("wv_t", [128, 8, D])
    bq_in = din("bq_c", [128, 8])
    bk_in = din("bk_c", [128, 8])
    bv_in = din("bv_r", [1, D])
    wqkvb_in = din("wqkvb2", [25, 128, 8, 128])
    woA_in = din("woA2", [8, 128, 8, 128])
    lngA_in = din("lng_c", [128, 8])
    lnbA_in = din("lnb_c", [128, 8])
    up_in = din("up_t", [8, 16, 128, 8, 128])
    dn_in = din("dn_t", [8, 8, 128, 8, 128])
    rw_in = din("rw_l", [8, 128, 8])
    wl_in = din("wl2", [16, 128, 8, 128])
    wd_in = din("wd2", [16, 128, 16, 128])
    wr_in = din("wr2", [16, 128, 8, 128])
    wo2_in = din("wo22", [8, 128, 16, 128])
    wb_in = din("wb_l", [16, 128, 16])
    wcc_in = din("wc_l", [16, 128, 16])
    convw2_in = din("convw2_c", [128, 16, 4])
    convb2_in = din("convb2_c", [128, 16])
    bdel_in = din("bdel_c", [128, 16])
    alog2_in = din("alog_r", [1, 16])
    lng2_in = din("lng2_c", [128, 16])
    lnb2_in = din("lnb2_c", [128, 16])

    sA = nc.dram_tensor("sA", [D, TOK], F32).ap()
    sB = nc.dram_tensor("sB", [D, TOK], F32).ap()
    scr_y = nc.dram_tensor("scr_y", [NCH, 16, 128, T], F32).ap()
    scr_ds = nc.dram_tensor("scr_ds", [NCH, 16, 128, T], F32).ap()
    scr_xi = nc.dram_tensor("scr_xi", [NCH, 16, 128, T], BF16).ap()
    scr_z = nc.dram_tensor("scr_z", [NCH, 16, 128, T], BF16).ap()
    scr_cr = nc.dram_tensor("scr_cr", [NCH, 16, 1, T], F32).ap()
    cc1_in = nc.dram_tensor("cc1_in", [1024, 1024], F32).ap()
    cc1_out = nc.dram_tensor("cc1_out", [1024, 1024], F32).ap()
    cc2_in = nc.dram_tensor("cc2_in", [1024, 1024], F32).ap()
    cc2_out = nc.dram_tensor("cc2_out", [1024, 1024], F32).ap()
    cc3_in = nc.dram_tensor("cc3_in", [1024, 48], F32).ap()
    cc3_out = nc.dram_tensor("cc3_out", [1024, 48], F32).ap()

    with contextlib.ExitStack() as stack:
        ctx = Ctx(nc, stack)
        p = P(ctx)
        kn = Konst(ctx, p)
        G = MB()
        G.ownm = sb(ctx, "ownm", [128, 8], F32)
        G.pm1 = sb(ctx, "pm1", [128, 8], F32)
        G.pm0 = sb(ctx, "pm0", [128, 8], F32)
        G.hsel = sb(ctx, "hsel", [128, 8], F32)
        eyef = sb(ctx, "eyef", [128, 128], F32)
        eye16 = sb(ctx, "eye16", [128, 128], BF16)
        ones_col = sb(ctx, "ones_col", [128, 1], F32)
        ones_row = sb(ctx, "ones_row", [1, 128], F32)
        sp_all = sb(ctx, "sp_all", [128, 1024], F32)
        Htile = sb(ctx, "Htile", [128, 512], F32)
        bsel = sb(ctx, "bsel", [128, 8, 6], F32)
        stg2 = [sb(ctx, f"stg2_{i}", [128, 8, 128], F32) for i in range(2)]
        w16b = [sb(ctx, f"w16b_{i}", [128, 8, 128], BF16) for i in range(2)]

        wl0 = [p.dma("sync", G.ownm[:], ownm_in),
               p.dma("sync", G.pm1[:], pm1_in),
               p.dma("sync", G.hsel[:], hsel_in),
               p.dma("sync", eyef[:], eye_in)]
        wc0 = [p.comp("vector", lambda E: E.tensor_copy(out=eye16[:], in_=eyef[:]), deps=wl0),
               p.comp("vector", lambda E: E.tensor_scalar(
                   out=G.pm0[:], in0=G.pm1[:], scalar1=-1.0, scalar2=1.0,
                   op0=ALU.mult, op1=ALU.add), deps=wl0),
               p.comp("vector", lambda E: E.memset(ones_col[:], 1.0)),
               p.comp("vector", lambda E: E.memset(ones_row[:], 1.0))]
        kzero = kn.ap(0.0)

        def spcol(q, dt, n):
            return sp_all[:, q * 256 + dt * 16 + n: q * 256 + dt * 16 + n + 1]

        def emit_l0_mamba():
            mb = MB()
            mb.is_l0 = True
            mb.kzero = kzero
            mb.eye16 = eye16
            mb.ones_row = ones_row
            mb.scr_y, mb.scr_ds = scr_y, scr_ds
            mb.scr_xi, mb.scr_z, mb.scr_cr = scr_xi, scr_z, scr_cr
            mb.spS = lambda b, dt, n: spcol(b, dt, n)
            mb.spP = lambda b, dt, n: spcol(2 + b, dt, n)
            mb.Hcol = lambda b, dt, n: Htile[:, b * 256 + dt * 16 + n:
                                             b * 256 + dt * 16 + n + 1]
            with ctx.scope():
                bar0 = p.barrier()
                xpst = sb(ctx, "xpst", [128, 16, 96], F32)
                xp16 = sb(ctx, "xp16", [128, 16, 96], BF16)
                dtwst = sb(ctx, "dtwst", [64, 16, 128], F32)
                dtw16 = sb(ctx, "dtw16", [64, 16, 128], BF16)
                rms_c = sb(ctx, "rms_c", [128, 8], F32)
                convw = sb(ctx, "convw", [128, 16, 4], F32)
                convb = sb(ctx, "convb", [128, 16], F32)
                dtb = sb(ctx, "dtb", [128, 16], F32)
                acol = sb(ctx, "acol", [128, 16, 16], F32)
                mb.d_c = sb(ctx, "d_c", [128, 16], F32)
                mb.acol_fn = lambda dt, n: acol[:, dt, n:n + 1]
                wl = [p.dma("sync", dst, src, deps=bar0) for dst, src in [
                    (xpst[:], xp_in), (dtwst[:], dtw_in), (rms_c[:], rms_in),
                    (convw[:], convw_in), (convb[:], convb_in), (dtb[:], dtb_in),
                    (acol[:], alog_in), (mb.d_c[:], d_in)]]
                wc = [p.comp("scalar", lambda E: E.activation(out=xp16[:], in_=xpst[:], func=AF.Copy), deps=wl),
                      p.comp("scalar", lambda E: E.activation(out=dtw16[:], in_=dtwst[:], func=AF.Copy), deps=wl)]
                t_a = p.comp("scalar", lambda E: E.activation(
                    out=acol[:], in_=acol[:], func=AF.Exp), deps=wl)
                t_a = p.comp("vector", lambda E: E.tensor_scalar(
                    out=acol[:], in0=acol[:], scalar1=-1.0, scalar2=None,
                    op0=ALU.mult), deps=[t_a])
                wc.append(t_a)
                wc += wc0

                def _phaseA():
                    xtc = sb(ctx, "xtc", [128, 8, TE], F32)
                    xin16 = sb(ctx, "xin16", [128, 8, T], BF16)
                    xn2 = sb(ctx, "xn2", [128, 8, T], BF16)
                    xn2e = sb(ctx, "xn2e", [128, 8, 3], BF16)
                    xipre = sb(ctx, "xipre", [128, 16, TE], BF16)
                    xi2 = sb(ctx, "xi2", [128, 16, T], BF16)
                    z16 = sb(ctx, "z16", [128, 2, T], BF16)
                    delta16 = sb(ctx, "delta16", [128, 16, T], BF16)
                    bbc16 = sb(ctx, "bbc16", [128, 16, T], BF16)
                    cbc16 = sb(ctx, "cbc16", [128, 16, T], BF16)
                    dtT16 = sb(ctx, "dtT16", [64, T], BF16)
                    bc1 = sb(ctx, "bc1", [1, 2, T], F32)
                    mb.bc1 = bc1
                    dsc = sb(ctx, "dsc", [128, 1, T], F32)
                    mb.du16 = sb(ctx, "du16", [128, 2, T], BF16)
                    mb.u16 = sb(ctx, "u16", [128, 2, T], BF16)
                    mb.tmp16 = sb(ctx, "tmp16", [128, NR, T], BF16)
                    mb.dA = sb(ctx, "dA", [128, NR, T], F32)
                    mb.hS = sb(ctx, "hS", [128, 2, T], F32)
                    mb.Ds = sb(ctx, "Ds", [128, 2, T], F32)
                    mb.z16zero = sb(ctx, "zz", [128, T], BF16)
                    mb.dstate = sb(ctx, "dstate", [128, 16], F32)
                    mb.ysb = sb(ctx, "ysb", [128, 2, T], F32)
                    ln = LnTmp(ctx, T)
                    cacc = ln.t1
                    bcst = ln.t1
                    wkps = [ps(ctx, f"wkps{i}", [128, T], F32) for i in range(2)]
                    mb.ybank = ps(ctx, "ybank", [128, T], F32)
                    mb.bcbank = ps(ctx, "bcbank", [128, T], F32)
                    mb.g = 0
                    mb.last_ya = []
                    mb.ysb_rd = {0: [], 1: []}
                    mb.ds_rd = {0: [], 1: []}
                    mb.du_rd = {0: [], 1: []}
                    mb.da_rd = {s: [] for s in range(NR)}
                    mb.u_rd = {0: [], 1: []}
                    mb.h_rd = {0: [], 1: []}
                    mb.tmp_rd = {s: [] for s in range(NR)}
                    ms = MS2()
                    wc2 = wc + [
                        p.comp("vector", lambda E: E.memset(mb.z16zero[:], 0.0)),
                        p.comp("vector", lambda E: E.memset(xipre[:, :, 0:3], 0.0))]

                    a_eps = kn.row(EPS)
                    a_mh = kn.row(-0.5)
                    a_i1024 = kn.row(1.0 / 1024.0)
                    kd = kn.dep()

                    def rms_pass(src16, nt, Tr, dst16, deps0):
                        sq_t, last_q = [], []
                        for i in range(nt):
                            tq = p.comp("scalar", lambda E, i=i: E.activation(
                                out=ln.sq[:, 0:Tr], in_=src16[:, i, 0:Tr], func=AF.Square),
                                deps=deps0 + last_q + sq_t[-1:])
                            sq_t.append(tq)
                            last_q = [p.comp("tensor", lambda E, i=i: E.matmul(
                                out=ln.ps_q[:, 0:Tr], lhsT=ones_col[:], rhs=ln.sq[:, 0:Tr],
                                start=(i == 0), stop=(i == nt - 1)), deps=[tq])]
                        tms = p.comp("scalar", lambda E: E.activation(
                            out=ln.r_msq[:, 0:Tr], in_=ln.ps_q[:, 0:Tr], func=AF.Copy,
                            scale=a_i1024), deps=last_q + kd)
                        tl2 = p.comp("scalar", lambda E: E.activation(
                            out=ln.r_tmp[:, 0:Tr], in_=ln.r_msq[:, 0:Tr], func=AF.Ln,
                            bias=a_eps), deps=[tms])
                        tr2 = p.comp("scalar", lambda E: E.activation(
                            out=ln.r_rstd[:, 0:Tr], in_=ln.r_tmp[:, 0:Tr], func=AF.Exp,
                            scale=a_mh), deps=[tl2])
                        trb = p.comp("tensor", lambda E: E.matmul(
                            out=ln.ps_rb[:, 0:Tr], lhsT=ones_row[:], rhs=ln.r_rstd[:, 0:Tr],
                            start=True, stop=True), deps=[tr2])
                        outd = []
                        for i in range(nt):
                            outd.append(p.comp("vector", lambda E, i=i: E.scalar_tensor_tensor(
                                out=dst16[:, i, 0:Tr], in0=src16[:, i, 0:Tr],
                                scalar=rms_c[:, i:i + 1], in1=ln.ps_rb[:, 0:Tr],
                                op0=ALU.mult, op1=ALU.mult), deps=[trb] + deps0 + wl + outd[-1:]))
                        return outd

                    zrd = {0: [], 1: []}
                    xist = {}
                    mb.t1_rd = []
                    mb.cbc_rd = []
                    mb.xtc_free = []

                    def frontA(ch):
                        b, j = ch // 2, ch % 2
                        cb = b * SEG + j * T
                        eb = b * (SEG + 3) + j * T
                        bar = p.barrier() if ch == 0 else []
                        ldx = [p.dma("sync", xtc[:], x0e[:, eb:eb + TE].rearrange(
                            "(a p) t -> p a t", p=128), deps=bar + mb.xtc_free)]
                        lnd, _ = emit_ln(p, ln, xtc, 8, ones_col[:], ones_row[:],
                                         xin16, ldx + mb.t1_rd, kn, T_=T, col0=3)
                        rdeps = []
                        stt_ = []
                        for i in range(8):
                            tr_ = p.comp("vector", lambda E, i=i: E.tensor_add(
                                out=xtc[:, i, 3:3 + T], in0=xtc[:, i, 3:3 + T],
                                in1=xin16[:, i, :]), deps=lnd)
                            rdeps.append(tr_)
                            stt_.append(p.dma("gpsimd",
                                              sA[i * 128:(i + 1) * 128, cb:cb + T],
                                              xtc[:, i, 3:3 + T], deps=[tr_]))
                        xnd = rms_pass(xin16, 8, T, xn2, lnd)
                        xne = None
                        lnde = []
                        if j == 0:
                            lnde, _ = emit_ln(p, ln, xtc, 8, ones_col[:], ones_row[:],
                                              xn2e, ldx + xnd + rdeps + mb.t1_rd,
                                              kn, T_=3, col0=0)
                            xne = rms_pass(xn2e, 8, 3, xn2e, lnde)
                        mb.xtc_free = rdeps + stt_ + lnd[-1:] + lnde[-1:]

                        def ip_evac(mt, bank, deps, ch=ch):
                            if mt < 16:
                                return p.comp("vector", lambda E, mt=mt, bank=bank: E.tensor_copy(
                                    out=xipre[:, mt, 3:3 + T], in_=bank[:, :]), deps=deps)
                            zi = mt - 16
                            t = p.comp("scalar", lambda E, zi=zi, bank=bank: E.activation(
                                out=z16[:, zi % 2, :], in_=bank[:, :], func=AF.Copy),
                                deps=deps + zrd[zi % 2])
                            td = p.dma("scalar", scr_z[ch, zi], z16[:, zi % 2, :],
                                       deps=[t])
                            zrd[zi % 2] = [td]
                            return t

                        def ipe_evac(mt, bank, deps):
                            return p.comp("scalar", lambda E, mt=mt, bank=bank: E.activation(
                                out=xipre[:, mt, 0:3], in_=bank[:, 0:3], func=AF.Copy), deps=deps)

                        ipd = mm_stream2(p, ms, stg2, w16b, lambda mt: ip_in[mt],
                                         8, 32, lambda kt: xn2[:, kt, :], ip_evac,
                                         wkps, xnd, bar,
                                         ext=((lambda kt: xn2e[:, kt, :]), ipe_evac,
                                              3, xne, 16) if j == 0 else None)
                        return dict(ch=ch, b=b, j=j, cb=cb, ipd=ipd)

                    def midA(fa):
                        ch = fa["ch"]
                        ipd = fa["ipd"]
                        conv_d = []
                        for dt in range(16):
                            cap = cacc[:, 0:T] if dt % 2 == 0 else dsc[:, 0, 0:T]
                            tc_ = p.comp("vector", lambda E, dt=dt, cap=cap: E.tensor_scalar(
                                out=cap, in0=xipre[:, dt, 0:T],
                                scalar1=convw[:, dt, 0:1], scalar2=convb[:, dt:dt + 1],
                                op0=ALU.mult, op1=ALU.add), deps=ipd + conv_d[-2:])
                            for k in range(1, 4):
                                tc_ = p.comp("vector", lambda E, dt=dt, k=k, cap=cap: E.scalar_tensor_tensor(
                                    out=cap, in0=xipre[:, dt, k:k + T],
                                    scalar=convw[:, dt, k:k + 1], in1=cap,
                                    op0=ALU.mult, op1=ALU.add), deps=[tc_])
                            tsi = p.comp("scalar", lambda E, dt=dt, cap=cap: E.activation(
                                out=xi2[:, dt, :], in_=cap, func=AF.Silu),
                                deps=[tc_] + xist.get(dt, []))
                            conv_d.append(tsi)
                            xist[dt] = [p.dma("scalar", scr_xi[ch, dt], xi2[:, dt, :],
                                              deps=[tsi])]
                        t_carry = p.comp("vector", lambda E: E.tensor_copy(
                            out=xipre[:, :, 0:3], in_=xipre[:, :, T:T + 3]), deps=conv_d)
                        t_dbc = None
                        for kt in range(16):
                            t_dbc = p.comp("tensor", lambda E, kt=kt: E.matmul(
                                out=wkps[0][0:96, :], lhsT=xp16[:, kt, :], rhs=xi2[:, kt, :],
                                start=(kt == 0), stop=(kt == 15)),
                                deps=conv_d + wc2 + ipd + (ms.bank_rd[0] if kt == 0 else []))
                        t_dt = p.comp("vector", lambda E: E.tensor_copy(
                            out=dtT16[:, :], in_=wkps[0][0:64, :]), deps=[t_dbc])
                        t_bc = p.comp("vector", lambda E: E.tensor_copy(
                            out=bcst[64:96, 0:T], in_=wkps[0][64:96, :]), deps=[t_dbc])
                        t_crs = p.dma("sync", scr_cr[ch].rearrange("n o t -> (n o) t"),
                                      bcst[80:96, 0:T], deps=[t_bc])
                        del_d = []
                        bank_rd = {0: [t_bc, t_crs], 1: list(ms.bank_rd[1])}
                        dsc_rd = {0: []}
                        for mt in range(16):
                            bnk = wkps[mt % 2]
                            tm = p.comp("tensor", lambda E, mt=mt, bnk=bnk: E.matmul(
                                out=bnk[:, :], lhsT=dtw16[:, mt, :], rhs=dtT16[:, :],
                                start=True, stop=True), deps=[t_dt] + bank_rd[mt % 2])
                            ta1 = p.comp("scalar", lambda E, mt=mt, bnk=bnk: E.activation(
                                out=ln.sq[:, 0:T], in_=bnk[:, :], func=AF.Abs,
                                bias=dtb[:, mt:mt + 1]), deps=[tm] + del_d[-1:])
                            ta2 = p.comp("scalar", lambda E: E.activation(
                                out=ln.sq[:, 0:T], in_=ln.sq[:, 0:T], func=AF.Exp,
                                scale=-1.0), deps=[ta1])
                            ta3 = p.comp("scalar", lambda E: E.activation(
                                out=ln.sq[:, 0:T], in_=ln.sq[:, 0:T], func=AF.Ln,
                                bias=1.0), deps=[ta2])
                            ta4 = p.comp("scalar", lambda E, mt=mt, bnk=bnk: E.activation(
                                out=dsc[:, 0, :], in_=bnk[:, :], func=AF.Relu,
                                bias=dtb[:, mt:mt + 1]), deps=[ta3] + dsc_rd[0])
                            bank_rd[mt % 2] = [ta4]
                            t_dl = p.comp("vector", lambda E, mt=mt: E.tensor_add(
                                out=delta16[:, mt, :], in0=ln.sq[:, 0:T],
                                in1=dsc[:, 0, :]), deps=[ta4])
                            dsc_rd[0] = [t_dl]
                            del_d.append(t_dl)
                        ms.bank_rd[0] = bank_rd[0]
                        ms.bank_rd[1] = bank_rd[1]
                        bc_d = []
                        bmm = []
                        tmv = None
                        for n in range(32):
                            tmv = p.dma("sync", bc1[0:1, n % 2, 0:T],
                                        bcst[64 + n:65 + n, 0:T],
                                        deps=[t_bc] + bmm[-2:-1] + del_d[-1:])
                            tb = p.comp("tensor", lambda E, n=n: E.matmul(
                                out=mb.bcbank[:, :], lhsT=ones_row[:],
                                rhs=bc1[0:1, n % 2, 0:T], start=True, stop=True),
                                deps=[tmv] + bc_d[-1:] + (del_d[-1:] if n == 0 else []))
                            bmm.append(tb)
                            dst = bbc16 if n < 16 else cbc16
                            bc_d.append(p.comp("scalar", lambda E, n=n, dst=dst: E.activation(
                                out=dst[:, n % 16, :], in_=mb.bcbank[:, :], func=AF.Copy),
                                deps=[tb] + mb.cbc_rd))
                        mb.t1_rd = [t_crs, tmv]
                        return dict(ch=ch, del_d=del_d, t_carry=t_carry,
                                    bc_d=bc_d, conv_d=conv_d)

                    fas = [None] * NCH
                    mds = [None] * NCH
                    fas[0] = frontA(0)
                    mds[0] = midA(fas[0])
                    for ch in range(NCH):
                        if ch + 1 < NCH:
                            fas[ch + 1] = frontA(ch + 1)
                        md = mds[ch]
                        scan_A(p, ctx, mb, ch, delta16, xi2, bbc16, cbc16,
                               md["del_d"] + [md["t_carry"]], md["bc_d"],
                               md["conv_d"], wc2)
                        if ch + 1 < NCH:
                            mds[ch + 1] = midA(fas[ch + 1])
                with ctx.scope():
                    _phaseA()
                barx = p.barrier()
                tc1 = emit_gather(p, ctx, G, sp_all[:, :], cc1_in, cc1_out, 1024, barx)
                wcH = emit_fold(p, ctx, G, cc1_out, Htile, tc1)

                def _phaseB():
                    xtc = sb(ctx, "xtcB", [128, 8, T], F32)
                    cbc16 = sb(ctx, "cbcB", [128, 16, T], BF16)
                    y2 = sb(ctx, "y2B", [128, 16, T], BF16)
                    mb.dsb = sb(ctx, "dsb", [128, 2, T], F32)
                    mb.yacc = sb(ctx, "yacc", [128, 2, T], F32)
                    mb.xib = sb(ctx, "xib", [128, 2, T], BF16)
                    mb.zb = sb(ctx, "zbB", [128, 2, T], BF16)
                    mb.szb = sb(ctx, "szb", [128, 2, T], F32)
                    mb.dec = sb(ctx, "dec", [128, NR, T], F32)
                    mb.tmpc2 = sb(ctx, "tmpc2", [128, 4, T], F32)
                    bc1 = sb(ctx, "bc1B", [1, 2, T], F32)
                    mb.bc1 = bc1
                    wkps = [ps(ctx, f"wkpsB{i}", [128, T], F32) for i in range(2)]
                    mb.bcbank = ps(ctx, "bcbankB", [128, T], F32)
                    mb.g = 0
                    mb.dsb_rd = {0: [], 1: []}
                    mb.yacc_rd = {0: [], 1: []}
                    mb.xib_rd = {0: [], 1: []}
                    mb.zb_rd = {0: [], 1: []}
                    mb.szb_rd = {0: [], 1: []}
                    mb.dec_rd = {s: [] for s in range(NR)}
                    mb.tmpc_rd = {i: [] for i in range(4)}
                    ms = MS2()
                    stgB = [sb(ctx, f"stgB_{i}", [128, 16, 128], F32) for i in range(2)]
                    w16B = [sb(ctx, f"w16B_{i}", [128, 16, 128], BF16) for i in range(2)]
                    for ch in range(NCH):
                        b, j = ch // 2, ch % 2
                        cb = b * SEG + j * T
                        bar = p.barrier()
                        bc_d = emit_cbc(p, ctx, mb, ch, cbc16, bar)
                        y2d = corr_B(p, ctx, mb, ch, cbc16, y2, bar, wcH, bc_d)
                        ldx2 = [p.dma("sync", xtc[:], sA[:, cb:cb + T].rearrange(
                            "(a p) t -> p a t", p=128), deps=bar)]

                        def op_evac(mt, bank, deps):
                            t = p.comp("vector", lambda E, mt=mt, bank=bank: E.tensor_add(
                                out=xtc[:, mt, :], in0=bank[:, :], in1=xtc[:, mt, :]),
                                deps=deps + ldx2)
                            p.dma("gpsimd", sA[mt * 128:(mt + 1) * 128, cb:cb + T],
                                  xtc[:, mt, :], deps=[t])
                            return t

                        mm_stream2(p, ms, stgB, w16B, lambda mt: op_in[mt],
                                   16, 8, lambda kt: y2[:, kt, :], op_evac, wkps,
                                   y2d, bar)
                with ctx.scope():
                    _phaseB()

        def emit_l0ffn(src, dst):
            with ctx.scope():
                bar0 = p.barrier()
                winf = sb(ctx, "winf", [128, 8, 64], F32)
                win16 = sb(ctx, "win16", [128, 8, 64], BF16)
                woutf = sb(ctx, "woutf", [64, 8, 128], F32)
                wout16 = sb(ctx, "wout16", [64, 8, 128], BF16)
                bin_c = sb(ctx, "bin_c", [64, 1], F32)
                w1 = sb(ctx, "w1", [64, 16], F32)
                b1 = sb(ctx, "b1", [64, 16], F32)
                w2 = sb(ctx, "w2", [64, 16], F32)
                b2 = sb(ctx, "b2", [64, 1], F32)
                bout_c = sb(ctx, "bout", [128, 8], F32)
                xtc = sb(ctx, "xtc", [128, 8, T], F32)
                xn16 = sb(ctx, "xn16", [128, 8, T], BF16)
                s_sb = sb(ctx, "s_sb", [64, T], F32)
                h_sb = sb(ctx, "h_sb", [64, 2, T], F32)
                g_sb = sb(ctx, "g_sb", [64, T], F32)
                g16 = sb(ctx, "g16", [64, T], BF16)
                ln = LnTmp(ctx, T)
                s_ps = ps(ctx, "s_ps", [64, T], F32)
                out_ps = [ps(ctx, f"out_ps{i}", [128, T], F32) for i in range(2)]
                wl = [
                    p.dma("sync", winf[:], win_in, deps=bar0),
                    p.dma("sync", woutf[:], wout_in, deps=bar0),
                    p.dma("sync", bin_c[:], bin_in, deps=bar0),
                    p.dma("sync", w1[:], w1_in, deps=bar0),
                    p.dma("sync", b1[:], b1_in, deps=bar0),
                    p.dma("sync", w2[:], w2_in, deps=bar0),
                    p.dma("sync", b2[:], b2_in, deps=bar0),
                    p.dma("sync", bout_c[:], bout_in, deps=bar0),
                ]
                wc = [
                    p.comp("scalar", lambda E: E.activation(out=win16[:], in_=winf[:], func=AF.Copy), deps=wl),
                    p.comp("scalar", lambda E: E.activation(out=wout16[:], in_=woutf[:], func=AF.Copy), deps=wl),
                ] + wc0
                for ch in range(NCH):
                    c0 = ch * T
                    bar = p.barrier()
                    ldx = [p.dma("sync", xtc[:], src[:, c0:c0 + T].rearrange(
                        "(a p) t -> p a t", p=128), deps=bar)]
                    lnd, _ = emit_ln(p, ln, xtc, 8, ones_col[:], ones_row[:], xn16,
                                     ldx, kn)
                    t_mm = None
                    for k in range(8):
                        t_mm = p.comp("tensor", lambda E, k=k: E.matmul(
                            out=s_ps[:, :], lhsT=win16[:, k, :], rhs=xn16[:, k, :],
                            start=(k == 0), stop=(k == 7)), deps=lnd + wc)
                    t_s = p.comp("vector", lambda E: E.tensor_scalar(
                        out=s_sb[:], in0=s_ps[:, :], scalar1=bin_c[:, 0:1],
                        scalar2=None, op0=ALU.add), deps=[t_mm])
                    t_g = None
                    tg_hist = []
                    for i in range(16):
                        t_h = p.comp("scalar", lambda E, i=i: E.activation(
                            out=h_sb[:, i % 2, :], in_=s_sb[:], func=AF.Relu,
                            scale=w1[:, i:i + 1], bias=b1[:, i:i + 1]),
                            deps=[t_s] + tg_hist[-2:-1])
                        if i == 0:
                            t_g = p.comp("vector", lambda E: E.tensor_scalar(
                                out=g_sb[:], in0=h_sb[:, 0, :], scalar1=w2[:, 0:1],
                                op0=ALU.mult, scalar2=b2[:, 0:1], op1=ALU.add),
                                deps=[t_h])
                        else:
                            t_g = p.comp("vector", lambda E, i=i: E.scalar_tensor_tensor(
                                out=g_sb[:], in0=h_sb[:, i % 2, :], scalar=w2[:, i:i + 1],
                                in1=g_sb[:], op0=ALU.mult, op1=ALU.add), deps=[t_h, t_g])
                        tg_hist.append(t_g)
                    t_g16 = p.comp("vector", lambda E: E.tensor_copy(
                        out=g16[:], in_=g_sb[:]), deps=[t_g])
                    ev_toks = []
                    for dt in range(8):
                        t_o = p.comp("tensor", lambda E, dt=dt: E.matmul(
                            out=out_ps[dt % 2][:, :], lhsT=wout16[:, dt, :], rhs=g16[:],
                            start=True, stop=True), deps=[t_g16] + ev_toks[-2:-1])
                        t_ev = p.comp("vector", lambda E, dt=dt: E.scalar_tensor_tensor(
                            out=xtc[:, dt, :], in0=out_ps[dt % 2][:, :],
                            scalar=bout_c[:, dt:dt + 1], in1=xtc[:, dt, :],
                            op0=ALU.add, op1=ALU.add), deps=[t_o])
                        ev_toks.append(t_ev)
                        p.dma("gpsimd", dst[dt * 128:(dt + 1) * 128, c0:c0 + T],
                              xtc[:, dt, :], deps=[t_ev])

        def emit_l0attnb(src, dst):
            with ctx.scope():
                bar0 = p.barrier()
                wstage = sb(ctx, "wstage", [128, 8, D], F32)
                wq = sb(ctx, "wq16", [128, 8, D], BF16)
                wk = sb(ctx, "wk16", [128, 8, D], BF16)
                wv = sb(ctx, "wv16", [128, 8, D], BF16)
                bq = sb(ctx, "bq", [128, 8], F32)
                bk = sb(ctx, "bk", [128, 8], F32)
                bvf = sb(ctx, "bvf", [1, D], F32)
                bv = sb(ctx, "bv16", [1, D], BF16)
                ones_r16 = sb(ctx, "ones_r16", [1, 128], BF16)
                xtc = sb(ctx, "xtc", [128, 8, T], F32)
                xn16 = sb(ctx, "xn16", [128, 8, T], BF16)
                qT = sb(ctx, "qT", [128, 8, T], BF16)
                kT = sb(ctx, "kT", [128, 8, T], BF16)
                v16 = sb(ctx, "v16", [128, 4, D], BF16)
                pexp = sb(ctx, "pexp", [128, T], F32)
                p16 = sb(ctx, "p16", [128, T], BF16)
                pT16 = sb(ctx, "pT16", [128, 4, T], BF16)
                nmax = sb(ctx, "nmax", [128, 1], F32)
                ssum = sb(ctx, "ssum", [128, 1], F32)
                rsum = sb(ctx, "rsum", [128, 1], F32)
                ln = LnTmp(ctx, T)
                wkps = [ps(ctx, f"wkps{i}", [128, T], F32) for i in range(3)]
                tps = ps(ctx, "tps", [128, 128], BF16)
                wl = [p.dma("sync", bq[:], bq_in, deps=bar0),
                      p.dma("sync", bk[:], bk_in, deps=bar0),
                      p.dma("sync", bvf[:], bv_in, deps=bar0)]
                wc = [p.comp("scalar", lambda E: E.activation(out=bv[:], in_=bvf[:], func=AF.Copy), deps=wl),
                      p.comp("vector", lambda E: E.memset(ones_r16[:], 1.0))] + wc0
                prev_cv = []
                for win, wdst in [(wq_in, wq), (wk_in, wk), (wv_in, wv)]:
                    tl = p.dma("sync", wstage[:], win, deps=prev_cv + bar0)
                    prev_cv = [p.comp("scalar", lambda E, wdst=wdst: E.activation(
                        out=wdst[:], in_=wstage[:], func=AF.Copy), deps=[tl])]
                    wc += prev_cv
                for ch in range(NCH):
                    c0 = ch * T
                    bar = p.barrier()
                    ldx = [p.dma("sync", xtc[:], src[:, c0:c0 + T].rearrange(
                        "(a p) t -> p a t", p=128), deps=bar)]
                    lnd, _ = emit_ln(p, ln, xtc, 8, ones_col[:], ones_row[:], xn16,
                                     ldx, kn)
                    qk_toks = []
                    ev_hist = []
                    for which, (wt, bc, dstt) in enumerate([(wq, bq, qT), (wk, bk, kT)]):
                        for et in range(8):
                            bnk = wkps[(which * 8 + et) % 2]
                            tm = None
                            for k in range(8):
                                tm = p.comp("tensor", lambda E, wt=wt, et=et, k=k, bnk=bnk: E.matmul(
                                    out=bnk[:, :], lhsT=wt[:, k, et * 128:(et + 1) * 128],
                                    rhs=xn16[:, k, :], start=(k == 0), stop=(k == 7)),
                                    deps=lnd + wc + ev_hist[-2:-1])
                            tev = p.comp("vector", lambda E, dstt=dstt, et=et, bc=bc, bnk=bnk: E.tensor_scalar(
                                out=dstt[:, et, :], in0=bnk[:, :], scalar1=bc[:, et:et + 1],
                                scalar2=None, op0=ALU.add), deps=[tm])
                            ev_hist.append(tev)
                            qk_toks.append(tev)
                    v_toks = []
                    for ct in range(4):
                        for eh in range(2):
                            bnk = wkps[(ct * 2 + eh) % 2]
                            tm = None
                            for k in range(8):
                                tm = p.comp("tensor", lambda E, ct=ct, eh=eh, k=k, bnk=bnk: E.matmul(
                                    out=bnk[:, :], lhsT=xn16[:, k, ct * 128:(ct + 1) * 128],
                                    rhs=wv[:, k, eh * 512:(eh + 1) * 512], start=(k == 0),
                                    stop=False), deps=lnd + wc + ev_hist[-2:-1])
                            tm = p.comp("tensor", lambda E, eh=eh, bnk=bnk: E.matmul(
                                out=bnk[:, :], lhsT=ones_r16[:, :],
                                rhs=bv[:, eh * 512:(eh + 1) * 512],
                                start=False, stop=True), deps=[])
                            tev = p.comp("scalar", lambda E, ct=ct, eh=eh, bnk=bnk: E.activation(
                                out=v16[:, ct, eh * 512:(eh + 1) * 512], in_=bnk[:, :],
                                func=AF.Copy), deps=[tm])
                            ev_hist.append(tev)
                            v_toks.append(tev)
                    ot_deps = []
                    tr_hist = []
                    sb_rd = {0: v_toks[-2:-1], 1: v_toks[-1:]}
                    exp_prev = []
                    for cqt in range(4):
                        sbnk = wkps[cqt % 2]
                        tm = None
                        for et in range(8):
                            tm = p.comp("tensor", lambda E, cqt=cqt, et=et, sbnk=sbnk: E.matmul(
                                out=sbnk[:, :], lhsT=qT[:, et, cqt * 128:(cqt + 1) * 128],
                                rhs=kT[:, et, :], start=(et == 0), stop=(et == 7)),
                                deps=qk_toks + sb_rd[cqt % 2])
                        t_max = p.comp("vector", lambda E, sbnk=sbnk: E.reduce_max(
                            out=nmax[:], in_=sbnk[:, :], axis=mybir.AxisListType.X,
                            negate=True), deps=[tm] + exp_prev)
                        t_exp = p.comp("scalar", lambda E, sbnk=sbnk: E.activation(
                            out=pexp[:], in_=sbnk[:, :], func=AF.Exp, bias=nmax[:, 0:1],
                            accum_out=ssum[:]), deps=[t_max])
                        sb_rd[cqt % 2] = [t_exp]
                        exp_prev = [t_exp]
                        t_rec = p.comp("vector", lambda E: E.reciprocal(
                            out=rsum[:], in_=ssum[:]), deps=[t_exp])
                        t_p16 = p.comp("vector", lambda E: E.tensor_scalar(
                            out=p16[:], in0=pexp[:], scalar1=rsum[:, 0:1], scalar2=None,
                            op0=ALU.mult), deps=[t_rec] + tr_hist[-1:])
                        for ckt in range(4):
                            ttr = p.comp("tensor", lambda E, ckt=ckt: E.transpose(
                                out=tps[:, :], in_=p16[:, ckt * 128:(ckt + 1) * 128],
                                identity=eye16[:]), deps=[t_p16] + tr_hist[-1:])
                            tev = p.comp("vector", lambda E, ckt=ckt, cqt=cqt: E.tensor_copy(
                                out=pT16[:, ckt, cqt * 128:(cqt + 1) * 128], in_=tps[:, :]),
                                deps=[ttr])
                            tr_hist.append(tev)
                            ot_deps.append(tev)
                    oev = []
                    for et in range(8):
                        bnk = wkps[et % 2]
                        tm = None
                        for ckt in range(4):
                            tm = p.comp("tensor", lambda E, et=et, ckt=ckt, bnk=bnk: E.matmul(
                                out=bnk[:, :], lhsT=v16[:, ckt, et * 128:(et + 1) * 128],
                                rhs=pT16[:, ckt, :], start=(ckt == 0), stop=(ckt == 3)),
                                deps=ot_deps + v_toks + oev[-2:-1])
                        tev = p.comp("vector", lambda E, et=et, bnk=bnk: E.tensor_add(
                            out=xtc[:, et, :], in0=bnk[:, :], in1=xtc[:, et, :]),
                            deps=[tm])
                        oev.append(tev)
                        p.dma("gpsimd", dst[et * 128:(et + 1) * 128, c0:c0 + T],
                              xtc[:, et, :], deps=[tev])

        def emit_l1attna(src, dst):
            with ctx.scope():
                bar0 = p.barrier()
                lng = sb(ctx, "lng", [128, 8], F32)
                lnb = sb(ctx, "lnb", [128, 8], F32)
                seg = sb(ctx, "seg", [128, 8, 16], F32)
                xtc = sb(ctx, "xtc", [128, 8, T], F32)
                xn16 = sb(ctx, "xn16", [128, 8, 2, T], BF16)
                qT = sb(ctx, "qT", [128, 8, 2, T], BF16)
                kT = sb(ctx, "kT", [128, 8, 2, T], BF16)
                vT = sb(ctx, "vT", [128, 8, 2, T], BF16)
                o16 = sb(ctx, "o16", [128, 8, 2, T], BF16)
                eq = sb(ctx, "eq", [128, 8, T], F32)
                pr2 = sb(ctx, "pr2", [128, 8, T], F32)
                yv = sb(ctx, "yv", [128, 8, T], F32)
                betaR = sb(ctx, "betaR", [16, 2, T], F32)
                rows = sb(ctx, "rows", [16, 8, T], F32)
                bc1 = sb(ctx, "bc1", [1, 2, T], F32)
                ln = LnTmp(ctx, T)
                wkps = [ps(ctx, f"wkps{i}", [128, T], F32) for i in range(2)]
                segb = ps(ctx, "segb", [16, T], F32)
                cbank = ps(ctx, "cbank", [128, T], F32)
                wl = [p.dma("sync", lng[:], lngA_in, deps=bar0),
                      p.dma("sync", lnb[:], lnbA_in, deps=bar0)]
                wc = [p.comp("vector", lambda E: E.memset(seg[:], 0.0))]
                for kt in range(8):
                    wc.append(p.comp("vector", lambda E, kt=kt: E.memset(
                        seg[0:64, kt, 2 * kt:2 * kt + 1], 1.0), deps=wc[0:1]))
                    wc.append(p.comp("vector", lambda E, kt=kt: E.memset(
                        seg[64:128, kt, 2 * kt + 1:2 * kt + 2], 1.0), deps=wc[0:1]))
                ones_c = kn.ap(1.0)
                prev_s_done = []
                p._bc_mm = []
                for pc in range(2):
                    bar = p.barrier()
                    lnd = []
                    for s in range(2):
                        ldx = p.dma("sync", xtc[:],
                                    src[:, s * 1024 + pc * T: s * 1024 + pc * T + T]
                                    .rearrange("(a p) t -> p a t", p=128),
                                    deps=bar + lnd + prev_s_done)
                        d, _ = emit_ln(p, ln, xtc, 8, ones_c, ones_row[:],
                                       xn16[:, :, s, :], [ldx] + prev_s_done,
                                       kn, T_=T, col0=0)
                        lnd += d
                    msA = MS2()
                    qkv_d = []
                    beta_d = []
                    if not hasattr(p, "_ms2cv"):
                        p._ms2cv = []
                        p._ms2rd = {0: [], 1: []}
                    for mt in range(25):
                        g = len(p._ms2cv)
                        q = g % 2
                        sl = g % 2
                        tld = p.dma(["sync", "scalar"][q], stg2[sl][:, 0:8, :],
                                    wqkvb_in[mt],
                                    deps=([p._ms2cv[-2]] if len(p._ms2cv) >= 2 else bar))
                        tcv = p.comp("scalar", lambda E, sl=sl: E.activation(
                            out=w16b[sl][:, 0:8, :], in_=stg2[sl][:, 0:8, :], func=AF.Copy),
                            deps=[tld] + p._ms2rd[sl])
                        p._ms2cv.append(tcv)
                        tmm = {0: None, 1: None}
                        for kt in range(8):
                            for s in range(2):
                                deps = [tcv]
                                if kt == 0:
                                    deps = deps + lnd + msA.bank_rd[s]
                                tmm[s] = p.comp("tensor", lambda E, sl=sl, kt=kt, s=s: E.matmul(
                                    out=wkps[s][:, :], lhsT=w16b[sl][:, kt, :],
                                    rhs=xn16[:, kt, s, :], start=(kt == 0), stop=(kt == 7)),
                                    deps=deps)
                        p._ms2rd[sl] = [tmm[0], tmm[1]]
                        for s in range(2):
                            if mt < 8:
                                tev = p.comp("vector", lambda E, mt=mt, s=s: E.tensor_copy(
                                    out=qT[:, mt, s, :], in_=wkps[s][:, :]), deps=[tmm[s]])
                            elif mt < 16:
                                tev = p.comp("vector", lambda E, mt=mt, s=s: E.tensor_copy(
                                    out=kT[:, mt - 8, s, :], in_=wkps[s][:, :]), deps=[tmm[s]])
                            elif mt < 24:
                                tev = p.comp("vector", lambda E, mt=mt, s=s: E.tensor_copy(
                                    out=vT[:, mt - 16, s, :], in_=wkps[s][:, :]), deps=[tmm[s]])
                            else:
                                tev = p.comp("vector", lambda E, s=s: E.tensor_copy(
                                    out=betaR[:, s, :], in_=wkps[s][0:16, :]), deps=[tmm[s]])
                                beta_d.append(tev)
                            msA.bank_rd[s] = [tev]
                            qkv_d.append(tev)
                    norm_d = []
                    seg_rd = []
                    for (srcq, ridx) in ((qT, 0), (kT, 1)):
                        for s in range(2):
                            t_m = p.comp("vector", lambda E, srcq=srcq, s=s: E.tensor_scalar(
                                out=eq[:], in0=srcq[:, :, s, :], scalar1=0.0, scalar2=None,
                                op0=ALU.min), deps=qkv_d + norm_d[-1:])
                            t_e = p.comp("scalar", lambda E: E.activation(
                                out=eq[:], in_=eq[:], func=AF.Exp), deps=[t_m])
                            t_r = p.comp("vector", lambda E, srcq=srcq, s=s: E.tensor_relu(
                                out=pr2[:], in_=srcq[:, :, s, :]), deps=qkv_d + norm_d[-1:])
                            t_a = p.comp("vector", lambda E: E.tensor_add(
                                out=eq[:], in0=eq[:], in1=pr2[:]), deps=[t_e, t_r])
                            t_sm = None
                            for kt in range(8):
                                t_sm = p.comp("tensor", lambda E, kt=kt: E.matmul(
                                    out=segb[:, :], lhsT=seg[:, kt, :], rhs=eq[:, kt, :],
                                    start=(kt == 0), stop=(kt == 7)),
                                    deps=[t_a] + wc + seg_rd)
                            ri = ridx * 2 + s
                            t_rc = p.comp("vector", lambda E, ri=ri: E.reciprocal(
                                out=rows[:, ri, :], in_=segb[:, :]), deps=[t_sm])
                            seg_rd = [t_rc]
                            bcm = p._bc_mm
                            for kt in range(8):
                                for hh in range(2):
                                    h = 2 * kt + hh
                                    tmv = p.dma("gpsimd", bc1[0:1, h % 2, 0:T],
                                                rows[h:h + 1, ri, :], deps=[t_rc] + bcm[-2:])
                                    bcm.append(p.comp("tensor", lambda E, hh=hh, h=h: E.matmul(
                                        out=cbank[64 * hh:64 * (hh + 1), :],
                                        lhsT=ones_row[0:1, 0:64], rhs=bc1[0:1, h % 2, 0:T],
                                        start=True, stop=True),
                                        deps=[tmv] + norm_d[-1:]))
                                norm_d.append(p.comp("vector", lambda E, srcq=srcq, s=s, kt=kt: E.tensor_mul(
                                    out=srcq[:, kt, s, :], in0=eq[:, kt, :], in1=cbank[:, :]),
                                    deps=bcm[-2:] + [t_a]))
                    dot_d = []
                    for di, (a_, sa, b_, sbi) in enumerate((
                            (qT, 0, kT, 0), (kT, 0, kT, 1), (qT, 1, kT, 0), (qT, 1, kT, 1))):
                        t_p = p.comp("vector", lambda E, a_=a_, sa=sa, b_=b_, sbi=sbi: E.tensor_mul(
                            out=pr2[:], in0=a_[:, :, sa, :], in1=b_[:, :, sbi, :]),
                            deps=norm_d + dot_d[-1:])
                        t_sm = None
                        for kt in range(8):
                            t_sm = p.comp("tensor", lambda E, kt=kt: E.matmul(
                                out=segb[:, :], lhsT=seg[:, kt, :], rhs=pr2[:, kt, :],
                                start=(kt == 0), stop=(kt == 7)),
                                deps=[t_p] + seg_rd)
                        t_ev = p.comp("vector", lambda E, di=di: E.tensor_copy(
                            out=rows[:, 4 + di, :], in_=segb[:, :]), deps=[t_sm])
                        seg_rd = [t_ev]
                        dot_d.append(t_ev)
                    t_b0 = p.comp("scalar", lambda E: E.activation(
                        out=rows[:, 3, :], in_=betaR[:, 0, :], func=AF.Sigmoid),
                        deps=beta_d + norm_d)
                    t_b1 = p.comp("scalar", lambda E: E.activation(
                        out=betaR[:, 1, :], in_=betaR[:, 1, :], func=AF.Sigmoid),
                        deps=beta_d + [t_b0])
                    t_c1 = p.comp("vector", lambda E: E.tensor_mul(
                        out=rows[:, 0, :], in0=rows[:, 4, :], in1=rows[:, 3, :]),
                        deps=[t_b0] + dot_d + norm_d)
                    t_c1 = p.comp("vector", lambda E: E.tensor_scalar(
                        out=rows[:, 0, :], in0=rows[:, 0, :], scalar1=0.125,
                        scalar2=None, op0=ALU.mult), deps=[t_c1])
                    t_cb = p.comp("vector", lambda E: E.tensor_mul(
                        out=rows[:, 2, :], in0=rows[:, 7, :], in1=betaR[:, 1, :]),
                        deps=[t_b1] + dot_d)
                    t_t = p.comp("vector", lambda E: E.tensor_mul(
                        out=rows[:, 1, :], in0=rows[:, 2, :], in1=rows[:, 5, :]),
                        deps=[t_cb])
                    t_t = p.comp("vector", lambda E: E.tensor_sub(
                        out=rows[:, 1, :], in0=rows[:, 6, :], in1=rows[:, 1, :]), deps=[t_t])
                    t_t = p.comp("vector", lambda E: E.tensor_mul(
                        out=rows[:, 1, :], in0=rows[:, 1, :], in1=rows[:, 3, :]),
                        deps=[t_t, t_b0])
                    t_ca = p.comp("vector", lambda E: E.tensor_scalar(
                        out=rows[:, 1, :], in0=rows[:, 1, :], scalar1=0.125,
                        scalar2=None, op0=ALU.mult), deps=[t_t])
                    t_cb = p.comp("vector", lambda E: E.tensor_scalar(
                        out=rows[:, 2, :], in0=rows[:, 2, :], scalar1=0.125,
                        scalar2=None, op0=ALU.mult), deps=[t_cb])
                    coef_d = [t_c1, t_ca, t_cb]
                    out_d = []
                    bcm = p._bc_mm
                    for (ri, s, vsrc, acc) in ((0, 0, 0, False), (1, 1, 0, False), (2, 1, 1, True)):
                        for kt in range(8):
                            for hh in range(2):
                                h = 2 * kt + hh
                                tmv = p.dma("gpsimd", bc1[0:1, h % 2, 0:T],
                                            rows[h:h + 1, ri, :], deps=coef_d + bcm[-2:])
                                bcm.append(p.comp("tensor", lambda E, hh=hh: E.matmul(
                                    out=cbank[64 * hh:64 * (hh + 1), :],
                                    lhsT=ones_row[0:1, 0:64], rhs=bc1[0:1, h % 2, 0:T],
                                    start=True, stop=True),
                                    deps=[tmv] + out_d[-2:]))
                            if not acc:
                                out_d.append(p.comp("vector", lambda E, s=s, kt=kt, vsrc=vsrc: E.tensor_mul(
                                    out=o16[:, kt, s, :] if s == 0 else yv[:, kt, :],
                                    in0=vT[:, kt, vsrc, :], in1=cbank[:, :]),
                                    deps=bcm[-2:] + norm_d + dot_d))
                            else:
                                t_m2 = p.comp("vector", lambda E, kt=kt, vsrc=vsrc: E.tensor_mul(
                                    out=pr2[:, kt, :], in0=vT[:, kt, vsrc, :], in1=cbank[:, :]),
                                    deps=bcm[-2:] + dot_d)
                                out_d.append(p.comp("vector", lambda E, kt=kt: E.tensor_add(
                                    out=o16[:, kt, 1, :], in0=yv[:, kt, :], in1=pr2[:, kt, :]),
                                    deps=[t_m2] + out_d))
                    for s in range(2):
                        def wo_evac(mt, bank, deps, s=s):
                            return p.comp("vector", lambda E, mt=mt, bank=bank, s=s: E.tensor_add(
                                out=yv[:, mt, :], in0=bank[:, :], in1=xn16[:, mt, s, :]),
                                deps=deps + out_d + prev_s_done)
                        yd = mm_stream2(p, msA, stg2, w16b, lambda mt: woA_in[mt],
                                        8, 8, lambda kt, s=s: o16[:, kt, s, :],
                                        wo_evac, wkps, out_d + prev_s_done, bar)
                        sq_t, last_q, last_s = [], [], []
                        for i in range(8):
                            tq = p.comp("scalar", lambda E, i=i: E.activation(
                                out=ln.sq[:, 0:T], in_=yv[:, i, :], func=AF.Square),
                                deps=yd + last_q + sq_t[-1:])
                            sq_t.append(tq)
                            last_q = [p.comp("tensor", lambda E, i=i: E.matmul(
                                out=ln.ps_q[:, 0:T], lhsT=kn.ap(1.0), rhs=ln.sq[:, 0:T],
                                start=(i == 0), stop=(i == 7)), deps=[tq])]
                            last_s = [p.comp("tensor", lambda E, i=i: E.matmul(
                                out=ln.ps_s[:, 0:T], lhsT=kn.ap(1.0), rhs=yv[:, i, :],
                                start=(i == 0), stop=(i == 7)), deps=yd)]
                        a_inv = kn.row(1.0 / 1024.0)
                        t_mean = p.comp("scalar", lambda E: E.activation(
                            out=ln.r_mean[:, 0:T], in_=ln.ps_s[:, 0:T], func=AF.Copy,
                            scale=a_inv), deps=last_s + kn.dep())
                        t_msq = p.comp("scalar", lambda E: E.activation(
                            out=ln.r_msq[:, 0:T], in_=ln.ps_q[:, 0:T], func=AF.Copy,
                            scale=a_inv), deps=last_q + kn.dep())
                        t_m2_ = p.comp("scalar", lambda E: E.activation(
                            out=ln.r_tmp[:, 0:T], in_=ln.r_mean[:, 0:T], func=AF.Square),
                            deps=[t_mean])
                        t_var = p.comp("vector", lambda E: E.tensor_sub(
                            out=ln.r_msq[:, 0:T], in0=ln.r_msq[:, 0:T], in1=ln.r_tmp[:, 0:T]),
                            deps=[t_msq, t_m2_])
                        t_lnv = p.comp("scalar", lambda E: E.activation(
                            out=ln.r_tmp[:, 0:T], in_=ln.r_msq[:, 0:T], func=AF.Ln,
                            bias=kn.row(EPS)), deps=[t_var] + kn.dep())
                        t_rst = p.comp("scalar", lambda E: E.activation(
                            out=ln.r_rstd[:, 0:T], in_=ln.r_tmp[:, 0:T], func=AF.Exp,
                            scale=kn.row(-0.5)), deps=[t_lnv] + kn.dep())
                        t_mb = p.comp("tensor", lambda E: E.matmul(
                            out=ln.ps_mb[:, 0:T], lhsT=ones_row[:], rhs=ln.r_mean[:, 0:T],
                            start=True, stop=True), deps=[t_mean] + last_s)
                        t_rb = p.comp("tensor", lambda E: E.matmul(
                            out=ln.ps_rb[:, 0:T], lhsT=ones_row[:], rhs=ln.r_rstd[:, 0:T],
                            start=True, stop=True), deps=[t_rst])
                        ldr = [p.dma("sync", xtc[:],
                                     src[:, s * 1024 + pc * T: s * 1024 + pc * T + T]
                                     .rearrange("(a p) t -> p a t", p=128),
                                     deps=lnd + prev_s_done + [t_rb])]
                        st_d = []
                        for i in range(8):
                            ta = p.comp("vector", lambda E, i=i: E.tensor_sub(
                                out=ln.sq[:, 0:T], in0=yv[:, i, :], in1=ln.ps_mb[:, 0:T]),
                                deps=[t_mb] + sq_t + st_d[-1:])
                            tb2 = p.comp("vector", lambda E, i=i: E.tensor_mul(
                                out=ln.sq[:, 0:T], in0=ln.sq[:, 0:T], in1=ln.ps_rb[:, 0:T]),
                                deps=[t_rb, ta])
                            tc2 = p.comp("vector", lambda E, i=i: E.tensor_scalar(
                                out=ln.sq[:, 0:T], in0=ln.sq[:, 0:T], scalar1=lng[:, i:i + 1],
                                scalar2=lnb[:, i:i + 1], op0=ALU.mult, op1=ALU.add),
                                deps=[tb2] + wl)
                            tf = p.comp("vector", lambda E, i=i: E.tensor_add(
                                out=xtc[:, i, :], in0=xtc[:, i, :], in1=ln.sq[:, 0:T]),
                                deps=[tc2] + ldr)
                            st_d.append(tf)
                            st_d.append(p.dma("gpsimd", dst[i * 128:(i + 1) * 128,
                                                s * 1024 + pc * T: s * 1024 + pc * T + T],
                                  xtc[:, i, :], deps=[tf]))
                        prev_s_done = st_d + last_q
        def emit_l1ffn(src, dst):
            with ctx.scope():
                bar0 = p.barrier()
                rwst = sb(ctx, "rwst", [128, 8, 8], F32)
                rw16 = sb(ctx, "rw16", [128, 8, 8], BF16)
                wstg2 = [sb(ctx, f"wstg2_{i}", [128, 8, 128], F32) for i in range(2)]
                w16 = [sb(ctx, f"w16_{i}", [128, 8, 128], BF16) for i in range(2)]
                xtc = sb(ctx, "xtc", [128, 8, T], F32)
                xn16 = sb(ctx, "xn16", [128, 8, TOK], BF16)
                acc = sb(ctx, "acc", [128, 8, TOK], BF16)
                hid16 = sb(ctx, "hid16", [128, 8, TOK], BF16)
                s16 = sb(ctx, "s16", [128, TOK], BF16)
                logit = sb(ctx, "logit", [128, 16, 8], F32)
                prob = sb(ctx, "prob", [128, 16, 8], F32)
                gate = sb(ctx, "gate", [128, 16, 8], F32)
                gateT = sb(ctx, "gateT", [8, TOK], F32)
                gsb4 = sb(ctx, "gsb4", [128, NCH, T], BF16)
                t1sb = sb(ctx, "t1sb", [128, T], BF16)
                nmax = sb(ctx, "nmax", [128, 1], F32)
                ssum = sb(ctx, "ssum", [128, 1], F32)
                rsum = sb(ctx, "rsum", [128, 1], F32)
                m2p = sb(ctx, "m2p", [128, 1], F32)
                msk = sb(ctx, "msk", [128, 8], F32)
                bc1 = sb(ctx, "bc1", [1, 2, T], F32)
                ln = LnTmp(ctx, T)
                wkps = [ps(ctx, f"wkps{i}", [128, T], F32) for i in range(2)]
                gps = ps(ctx, "gps", [128, T], F32)
                tps = ps(ctx, "tps", [128, 128], F32)
                wl = [p.dma("sync", rwst[:], rw_in.rearrange("k p e -> p k e"), deps=bar0)]
                wc = [p.comp("scalar", lambda E: E.activation(out=rw16[:], in_=rwst[:], func=AF.Copy), deps=wl)] + wc0
                lnall = []
                ldxs = []
                for ch in range(NCH):
                    c0 = ch * T
                    ldx = [p.dma("sync", xtc[:], src[:, c0:c0 + T].rearrange(
                        "(a p) t -> p a t", p=128), deps=(lnall[-1:] or (wc + wl)) + bar0)]
                    ldxs.append(ldx)
                    lnd, _ = emit_ln(p, ln, xtc, 8, ones_col[:], ones_row[:], xn16, ldx,
                                     kn, T_=T, col0=0, ocol0=c0)
                    lnall += lnd
                gdone = []
                tT_rd = []
                for tt in range(16):
                    tm = None
                    for kt in range(8):
                        tm = p.comp("tensor", lambda E, tt=tt, kt=kt: E.matmul(
                            out=wkps[0][:, 0:8], lhsT=xn16[:, kt, tt * 128:(tt + 1) * 128],
                            rhs=rw16[:, kt, :], start=(kt == 0), stop=(kt == 7)),
                            deps=lnall + wc + gdone[-1:])
                    t_lg = p.comp("vector", lambda E, tt=tt: E.tensor_copy(
                        out=logit[:, tt, :], in_=wkps[0][:, 0:8]), deps=[tm])
                    t_mx = p.comp("vector", lambda E, tt=tt: E.reduce_max(
                        out=nmax[:], in_=logit[:, tt, :], axis=mybir.AxisListType.X,
                        negate=True), deps=[t_lg] + gdone[-1:])
                    t_ex = p.comp("scalar", lambda E, tt=tt: E.activation(
                        out=prob[:, tt, :], in_=logit[:, tt, :], func=AF.Exp,
                        bias=nmax[:, 0:1], accum_out=ssum[:]), deps=[t_mx])
                    t_rc = p.comp("vector", lambda E: E.reciprocal(
                        out=rsum[:], in_=ssum[:]), deps=[t_ex])
                    t_pr = p.comp("vector", lambda E, tt=tt: E.tensor_scalar(
                        out=prob[:, tt, :], in0=prob[:, tt, :], scalar1=rsum[:, 0:1],
                        scalar2=None, op0=ALU.mult), deps=[t_rc])
                    t_m1 = p.comp("vector", lambda E, tt=tt: E.reduce_max(
                        out=nmax[:], in_=prob[:, tt, :], axis=mybir.AxisListType.X),
                        deps=[t_pr])
                    t_eq = p.comp("vector", lambda E, tt=tt: E.tensor_scalar(
                        out=msk[:], in0=prob[:, tt, :], scalar1=nmax[:, 0:1],
                        scalar2=None, op0=ALU.is_lt), deps=[t_m1])
                    t_mk = p.comp("vector", lambda E, tt=tt: E.tensor_mul(
                        out=msk[:], in0=msk[:], in1=prob[:, tt, :]), deps=[t_eq])
                    t_m2 = p.comp("vector", lambda E: E.reduce_max(
                        out=m2p[:], in_=msk[:], axis=mybir.AxisListType.X), deps=[t_mk])
                    t_dn = p.comp("vector", lambda E: E.tensor_add(
                        out=ssum[:], in0=nmax[:], in1=m2p[:]), deps=[t_m2])
                    t_rc2 = p.comp("vector", lambda E: E.reciprocal(
                        out=rsum[:], in_=ssum[:]), deps=[t_dn])
                    t_ge = p.comp("vector", lambda E, tt=tt: E.tensor_scalar(
                        out=msk[:], in0=prob[:, tt, :], scalar1=m2p[:, 0:1],
                        scalar2=None, op0=ALU.is_ge), deps=[t_m2, t_mk])
                    t_g1 = p.comp("vector", lambda E, tt=tt: E.tensor_mul(
                        out=gate[:, tt, :], in0=msk[:], in1=prob[:, tt, :]), deps=[t_ge])
                    t_g2 = p.comp("vector", lambda E, tt=tt: E.tensor_scalar(
                        out=gate[:, tt, :], in0=gate[:, tt, :], scalar1=rsum[:, 0:1],
                        scalar2=None, op0=ALU.mult), deps=[t_g1, t_rc2])
                    t_tr = p.comp("tensor", lambda E, tt=tt: E.transpose(
                        out=tps[0:8, :], in_=gate[:, tt, 0:8], identity=eyef[:]),
                        deps=[t_g2] + tT_rd)
                    t_gt = p.comp("vector", lambda E, tt=tt: E.tensor_copy(
                        out=gateT[0:8, tt * 128:(tt + 1) * 128], in_=tps[0:8, :]),
                        deps=[t_tr])
                    tT_rd = [t_gt]
                    gdone.append(t_g2)
                    gdone.append(t_gt)
                accd = []
                ev_all = list(gdone)
                p._moe_cv = []
                p._moe_mm = []
                p._gsb_rd = []
                p._tgb_hist = []
                qg = [0]

                def load_w(src_ap, extra):
                    q = qg[0] % 2
                    qg[0] += 1
                    tld = p.dma(["sync", "gpsimd"][q], wstg2[q][:], src_ap,
                                deps=(p._moe_cv[-2:-1] or (wl + lnall)))
                    tcv = p.comp("scalar", lambda E, q=q: E.activation(
                        out=w16[q][:], in_=wstg2[q][:], func=AF.Copy),
                        deps=[tld] + p._moe_mm[-5:-4])
                    p._moe_cv.append(tcv)
                    return q, tcv

                for e in range(8):
                    hidd = []
                    sils = {}
                    for jj in range(8):
                        for half in range(2):
                            mt = jj + 8 * half
                            q, tcv = load_w(up_in[e, mt], accd[-1:])
                            for ch in range(NCH):
                                c0 = ch * T
                                tm = None
                                for kt in range(8):
                                    tm = p.comp("tensor", lambda E, q=q, kt=kt, ch=ch, c0=c0: E.matmul(
                                        out=wkps[ch % 2][:, :], lhsT=w16[q][:, kt, :],
                                        rhs=xn16[:, kt, c0:c0 + T], start=(kt == 0),
                                        stop=(kt == 7)),
                                        deps=[tcv] + lnall + wc + ev_all[-2:-1] + accd[-1:])
                                p._moe_mm.append(tm)
                                if half == 0:
                                    tev = p.comp("scalar", lambda E, ch=ch, c0=c0: E.activation(
                                        out=s16[:, c0:c0 + T], in_=wkps[ch % 2][:, :],
                                        func=AF.Silu), deps=[tm] + hidd[-1:])
                                    sils[ch] = tev
                                else:
                                    tev = p.comp("vector", lambda E, jj=jj, ch=ch, c0=c0: E.tensor_mul(
                                        out=hid16[:, jj, c0:c0 + T], in0=s16[:, c0:c0 + T],
                                        in1=wkps[ch % 2][:, :]),
                                        deps=[tm, sils[ch]] + hidd[-4:-3])
                                    hidd.append(tev)
                                ev_all.append(tev)
                    gds = []
                    for ch in range(NCH):
                        c0 = ch * T
                        tmv = p.dma("gpsimd", bc1[0:1, ch % 2, 0:T],
                                    gateT[e:e + 1, c0:c0 + T],
                                    deps=gdone + (gds[-2:-1] or p._tgb_hist[-2:]))
                        tgb = p.comp("tensor", lambda E, ch=ch: E.matmul(
                            out=gps[:, :], lhsT=ones_row[:], rhs=bc1[0:1, ch % 2, 0:T],
                            start=True, stop=True), deps=[tmv] + gds[-1:] + p._gsb_rd)
                        tgs = p.comp("scalar", lambda E, ch=ch: E.activation(
                            out=gsb4[:, ch, :], in_=gps[:, :], func=AF.Copy),
                            deps=[tgb] + p._gsb_rd)
                        gds.append(tgs)
                        p._tgb_hist.append(tgb)
                    p._gsb_rd = []
                    for mt in range(8):
                        q, tcv = load_w(dn_in[e, mt], [])
                        for ch in range(NCH):
                            c0 = ch * T
                            tm = None
                            for kt in range(8):
                                tm = p.comp("tensor", lambda E, q=q, kt=kt, ch=ch, c0=c0: E.matmul(
                                    out=wkps[ch % 2][:, :], lhsT=w16[q][:, kt, :],
                                    rhs=hid16[:, kt, c0:c0 + T], start=(kt == 0),
                                    stop=(kt == 7)), deps=[tcv] + hidd + ev_all[-2:-1])
                            p._moe_mm.append(tm)
                            t_t1 = p.comp("vector", lambda E, ch=ch: E.tensor_mul(
                                out=t1sb[:, :], in0=wkps[ch % 2][:, :], in1=gsb4[:, ch, :]),
                                deps=[tm] + gds + accd[-1:])
                            p._gsb_rd.append(t_t1)
                            ev_all.append(t_t1)
                            if e == 0:
                                ta = p.comp("vector", lambda E, mt=mt, c0=c0: E.tensor_copy(
                                    out=acc[:, mt, c0:c0 + T], in_=t1sb[:, :]), deps=[t_t1])
                            else:
                                ta = p.comp("vector", lambda E, mt=mt, c0=c0: E.tensor_add(
                                    out=acc[:, mt, c0:c0 + T], in0=acc[:, mt, c0:c0 + T],
                                    in1=t1sb[:, :]), deps=[t_t1] + accd[-32:-31])
                            accd.append(ta)
                st_prev = []
                for ch in range(NCH):
                    c0 = ch * T
                    ldx2 = p.dma("sync", xtc[:], src[:, c0:c0 + T].rearrange(
                        "(a p) t -> p a t", p=128), deps=accd + ldxs[-1] + st_prev)
                    st_prev = []
                    for mt in range(8):
                        tf = p.comp("vector", lambda E, mt=mt, c0=c0: E.tensor_add(
                            out=xtc[:, mt, :], in0=xtc[:, mt, :], in1=acc[:, mt, c0:c0 + T]),
                            deps=[ldx2] + accd)
                        st_prev.append(p.dma("gpsimd", dst[mt * 128:(mt + 1) * 128, c0:c0 + T],
                                             xtc[:, mt, :], deps=[tf]))

        def emit_l1_mamba(src, dst):
            mb = MB()
            mb.is_l0 = False
            mb.kzero = kzero
            mb.eye16 = eye16
            mb.ones_row = ones_row
            mb.scr_y, mb.scr_ds = scr_y, scr_ds
            mb.scr_xi, mb.scr_z, mb.scr_cr = scr_xi, scr_z, scr_cr
            mb.spS = lambda b, dt, n: spcol(b, dt, n)
            mb.spP = lambda b, dt, n: spcol(2 + b, dt, n)
            mb.Hcol = lambda b, dt, n: Htile[:, b * 256 + dt * 16 + n:
                                             b * 256 + dt * 16 + n + 1]
            with ctx.scope():
                bar0 = p.barrier()
                wbst = sb(ctx, "wbst", [128, 16, 32], F32)
                wb16 = sb(ctx, "wb16", [128, 16, 32], BF16)
                convw = sb(ctx, "convw", [128, 16, 4], F32)
                convb = sb(ctx, "convb", [128, 16], F32)
                bdel = sb(ctx, "bdel", [128, 16], F32)
                lng = sb(ctx, "lng", [128, 16], F32)
                lnb = sb(ctx, "lnb", [128, 16], F32)
                arow = sb(ctx, "arow", [1, 16], F32)
                ar1 = sb(ctx, "ar1", [1, 16], F32)
                ar2 = sb(ctx, "ar2", [1, 16], F32)
                acol = sb(ctx, "acol", [128, 16], F32)
                mb.acol_fn = lambda dt, n: acol[:, n:n + 1]
                wl = [p.dma("sync", dst_, src_, deps=bar0) for dst_, src_ in [
                    (convw[:], convw2_in), (convb[:], convb2_in), (bdel[:], bdel_in),
                    (arow[:], alog2_in), (lng[:], lng2_in), (lnb[:], lnb2_in),
                    (wbst[:, :, 0:16], wb_in.rearrange("k p n -> p k n")),
                    (wbst[:, :, 16:32], wcc_in.rearrange("k p n -> p k n"))]]
                wc = [p.comp("scalar", lambda E: E.activation(out=wb16[:], in_=wbst[:],
                             func=AF.Copy), deps=wl)] + wc0
                t_a = p.comp("scalar", lambda E: E.activation(
                    out=ar1[:], in_=arow[:], func=AF.Abs), deps=wl)
                t_a = p.comp("scalar", lambda E: E.activation(
                    out=ar1[:], in_=ar1[:], func=AF.Exp, scale=-1.0), deps=[t_a])
                t_a = p.comp("scalar", lambda E: E.activation(
                    out=ar1[:], in_=ar1[:], func=AF.Ln, bias=1.0), deps=[t_a])
                t_a2 = p.comp("scalar", lambda E: E.activation(
                    out=ar2[:], in_=arow[:], func=AF.Relu), deps=wl)
                t_arow = p.comp("vector", lambda E: E.tensor_add(
                    out=ar1[:], in0=ar1[:], in1=ar2[:]), deps=[t_a, t_a2])
                bnds = sb(ctx, "bnds", [128, 8, 6], F32)
                tb1 = p.dma("sync", bnds[:, :, 0:3],
                            src[:, SEG - 3:SEG].rearrange("(a p) t -> p a t", p=128),
                            deps=bar0)
                tb2 = p.dma("sync", bnds[:, :, 3:6],
                            src[:, TOK - 3:TOK].rearrange("(a p) t -> p a t", p=128),
                            deps=bar0)
                tc3 = emit_gather(p, ctx, G, bnds.rearrange("p a t -> p (a t)"),
                                  cc3_in, cc3_out, 48, [tb1, tb2])
                bsf = [sb(ctx, f"bsf{i}", [128, 8, 6], F32) for i in range(2)]
                tsel = p.comp("vector", lambda E: E.memset(bsel[:], 0.0), deps=[tc3])
                bsf_rd = {0: [], 1: []}
                for r in range(8):
                    sl = r % 2
                    tl = p.dma("sync", bsf[sl][:],
                               cc3_out[r * 128:(r + 1) * 128, :].rearrange(
                                   "p (a t) -> p a t", a=8), deps=[tc3] + bsf_rd[sl])
                    tsel = p.comp("vector", lambda E, r=r, sl=sl: E.scalar_tensor_tensor(
                        out=bsel[:], in0=bsf[sl][:], scalar=G.hsel[:, r:r + 1],
                        in1=bsel[:], op0=ALU.mult, op1=ALU.add), deps=[tl, tsel])
                    bsf_rd[sl] = [tsel]
                bseld = [tsel]

                def _phaseA():
                    xtc = sb(ctx, "xtc", [128, 8, TE], F32)
                    xn16 = sb(ctx, "xn16", [128, 8, T], BF16)
                    xn16e = sb(ctx, "xn16e", [128, 8, 3], BF16)
                    lpre16 = sb(ctx, "lpre16", [128, 16, TE], BF16)
                    left16 = sb(ctx, "left16", [128, 16, T], BF16)
                    g16 = sb(ctx, "g16", [128, 2, T], BF16)
                    delta16 = sb(ctx, "delta16", [128, 16, T], BF16)
                    bbc16 = sb(ctx, "bbc16", [128, 16, T], BF16)
                    cbc16 = sb(ctx, "cbc16", [128, 16, T], BF16)
                    bc1 = sb(ctx, "bc1", [1, 2, T], F32)
                    mb.bc1 = bc1
                    dsc = sb(ctx, "dsc", [128, 2, T], F32)
                    mb.du16 = sb(ctx, "du16", [128, 2, T], BF16)
                    mb.u16 = sb(ctx, "u16", [128, 2, T], BF16)
                    mb.tmp16 = sb(ctx, "tmp16", [128, NR, T], BF16)
                    mb.dA = sb(ctx, "dA", [128, NR, T], F32)
                    mb.hS = sb(ctx, "hS", [128, 2, T], F32)
                    mb.Ds = sb(ctx, "Ds", [128, 2, T], F32)
                    mb.z16zero = sb(ctx, "zz", [128, T], BF16)
                    mb.dstate = sb(ctx, "dstate", [128, 16], F32)
                    mb.ysb = sb(ctx, "ysb", [128, 2, T], F32)
                    ln = LnTmp(ctx, T)
                    cacc = ln.t1
                    bcst = ln.t1
                    wkps = [ps(ctx, f"wkps{i}", [128, T], F32) for i in range(2)]
                    mb.ybank = ps(ctx, "ybank", [128, T], F32)
                    mb.bcbank = ps(ctx, "bcbank", [128, T], F32)
                    mb.g = 0
                    mb.last_ya = []
                    mb.ysb_rd = {0: [], 1: []}
                    mb.ds_rd = {0: [], 1: []}
                    mb.du_rd = {0: [], 1: []}
                    mb.da_rd = {s: [] for s in range(NR)}
                    mb.u_rd = {0: [], 1: []}
                    mb.h_rd = {0: [], 1: []}
                    mb.tmp_rd = {s: [] for s in range(NR)}
                    ms = MS2()
                    stgL = [sb(ctx, f"stgL_{i}", [128, 16, 128], F32) for i in range(2)]
                    w16L = [sb(ctx, f"w16L_{i}", [128, 16, 128], BF16) for i in range(2)]
                    t_ab = p.comp("tensor", lambda E: E.matmul(
                        out=mb.bcbank[:, 0:16], lhsT=ones_row[:], rhs=ar1[:],
                        start=True, stop=True), deps=[t_arow])
                    t_ac = p.comp("vector", lambda E: E.tensor_scalar(
                        out=acol[:], in0=mb.bcbank[:, 0:16], scalar1=-1.0,
                        scalar2=None, op0=ALU.mult), deps=[t_ab])
                    wc2 = wc + [
                        t_ac,
                        p.comp("vector", lambda E: E.memset(mb.z16zero[:], 0.0)),
                        p.comp("vector", lambda E: E.memset(lpre16[:, :, 0:3], 0.0))]
                    mb.t1_rd = []
                    mb.cbc_rd = []
                    mb.xtc_free = []
                    grd = {0: [], 1: []}

                    def frontA(ch):
                        b, j = ch // 2, ch % 2
                        cb = b * SEG + j * T
                        bar = p.barrier() if ch == 0 else []
                        if j == 0:
                            ldx = [p.dma("sync", xtc[:, :, 3:3 + T],
                                         src[:, cb:cb + T].rearrange("(a p) t -> p a t", p=128),
                                         deps=bar + mb.xtc_free)]
                            tcp = p.comp("vector", lambda E, b=b: E.tensor_copy(
                                out=xtc[:, :, 0:3], in_=bsel[:, :, b * 3:(b + 1) * 3]),
                                deps=bseld + bar + mb.xtc_free)
                            ldx = ldx + [tcp]
                        else:
                            ldx = [p.dma("sync", xtc[:],
                                         src[:, cb - 3:cb + T].rearrange("(a p) t -> p a t", p=128),
                                         deps=bar + mb.xtc_free)]
                        lnd, _ = emit_ln(p, ln, xtc, 8, ones_col[:], ones_row[:], xn16,
                                         ldx + mb.t1_rd, kn, T_=T, col0=3)
                        if j == 0:
                            lnde, _ = emit_ln(p, ln, xtc, 8, ones_col[:], ones_row[:],
                                              xn16e, ldx + lnd + mb.t1_rd, kn, T_=3, col0=0)
                            lnd = lnd + lnde
                        mb.xtc_free = lnd[-1:]

                        def lp_evac(mt, bank, deps):
                            return p.comp("vector", lambda E, mt=mt, bank=bank: E.tensor_copy(
                                out=lpre16[:, mt, 3:3 + T], in_=bank[:, :]), deps=deps)

                        def lpe_evac(mt, bank, deps):
                            return p.comp("scalar", lambda E, mt=mt, bank=bank: E.activation(
                                out=lpre16[:, mt, 0:3], in_=bank[:, 0:3], func=AF.Copy), deps=deps)

                        lpd = mm_stream2(p, ms, stgL, w16L, lambda mt: wl_in[mt],
                                         8, 16, lambda kt: xn16[:, kt, :], lp_evac,
                                         wkps, lnd, bar,
                                         ext=((lambda kt: xn16e[:, kt, :]), lpe_evac,
                                              3, lnd, 16) if j == 0 else None)

                        def g_evac(mt, bank, deps, ch=ch):
                            t = p.comp("scalar", lambda E, mt=mt, bank=bank: E.activation(
                                out=g16[:, mt % 2, :], in_=bank[:, :], func=AF.Silu),
                                deps=deps + grd[mt % 2])
                            td = p.dma("scalar", scr_z[ch, mt], g16[:, mt % 2, :],
                                       deps=[t])
                            grd[mt % 2] = [td]
                            return t
                        gd = mm_stream2(p, ms, stgL, w16L, lambda mt: wr_in[mt],
                                        8, 16, lambda kt: xn16[:, kt, :], g_evac,
                                        wkps, lnd, bar + lpd)
                        return dict(ch=ch, b=b, j=j, cb=cb, lpd=lpd, gd=gd)

                    def midA(fa):
                        ch = fa["ch"]
                        lpd = fa["lpd"]
                        gd = fa["gd"]
                        conv_d = []
                        for dt in range(16):
                            cap = cacc[:, 0:T] if dt % 2 == 0 else dsc[:, 0, 0:T]
                            tc_ = p.comp("vector", lambda E, dt=dt, cap=cap: E.tensor_scalar(
                                out=cap, in0=lpre16[:, dt, 0:T],
                                scalar1=convw[:, dt, 0:1], scalar2=convb[:, dt:dt + 1],
                                op0=ALU.mult, op1=ALU.add), deps=lpd + conv_d[-2:])
                            for k in range(1, 4):
                                tc_ = p.comp("vector", lambda E, dt=dt, k=k, cap=cap: E.scalar_tensor_tensor(
                                    out=cap, in0=lpre16[:, dt, k:k + T],
                                    scalar=convw[:, dt, k:k + 1], in1=cap,
                                    op0=ALU.mult, op1=ALU.add), deps=[tc_])
                            conv_d.append(p.comp("scalar", lambda E, dt=dt, cap=cap: E.activation(
                                out=left16[:, dt, :], in_=cap, func=AF.Silu),
                                deps=[tc_]))
                        t_carry = p.comp("vector", lambda E: E.tensor_copy(
                            out=lpre16[:, :, 0:3], in_=lpre16[:, :, T:T + 3]), deps=conv_d)
                        t_b = None
                        t_c = None
                        for kt in range(16):
                            t_b = p.comp("tensor", lambda E, kt=kt: E.matmul(
                                out=wkps[0][0:16, :], lhsT=wb16[:, kt, 0:16],
                                rhs=left16[:, kt, :], start=(kt == 0), stop=(kt == 15)),
                                deps=conv_d + wc2 + lpd + [t_carry] +
                                (ms.bank_rd[0] if kt == 0 else []))
                        for kt in range(16):
                            t_c = p.comp("tensor", lambda E, kt=kt: E.matmul(
                                out=wkps[1][0:16, :], lhsT=wb16[:, kt, 16:32],
                                rhs=left16[:, kt, :], start=(kt == 0), stop=(kt == 15)),
                                deps=conv_d + wc2 + (ms.bank_rd[1] if kt == 0 else []))
                        t_bs = p.comp("vector", lambda E: E.tensor_copy(
                            out=bcst[0:16, 0:T], in_=wkps[0][0:16, :]), deps=[t_b])
                        t_cs = p.comp("vector", lambda E: E.tensor_copy(
                            out=bcst[32:48, 0:T], in_=wkps[1][0:16, :]), deps=[t_c])
                        ms.bank_rd[0] = ms.bank_rd[0] + [t_bs]
                        ms.bank_rd[1] = ms.bank_rd[1] + [t_cs]
                        t_crs = p.dma("sync", scr_cr[ch].rearrange("n o t -> (n o) t"),
                                      bcst[32:48, 0:T], deps=[t_cs])
                        deld = []
                        dsc_rd = {0: [], 1: []}

                        def d_evac(mt, bank, deps):
                            ds2 = mt % 2
                            ta1 = p.comp("scalar", lambda E, mt=mt, bank=bank: E.activation(
                                out=ln.sq[:, 0:T], in_=bank[:, :], func=AF.Abs,
                                bias=bdel[:, mt:mt + 1]), deps=deps + deld[-1:])
                            ta2 = p.comp("scalar", lambda E: E.activation(
                                out=ln.sq[:, 0:T], in_=ln.sq[:, 0:T], func=AF.Exp,
                                scale=-1.0), deps=[ta1])
                            ta3 = p.comp("scalar", lambda E: E.activation(
                                out=ln.sq[:, 0:T], in_=ln.sq[:, 0:T], func=AF.Ln,
                                bias=1.0), deps=[ta2])
                            ta4 = p.comp("scalar", lambda E, mt=mt, bank=bank, ds2=ds2: E.activation(
                                out=dsc[:, ds2, :], in_=bank[:, :], func=AF.Relu,
                                bias=bdel[:, mt:mt + 1]), deps=[ta3] + dsc_rd[ds2])
                            t5 = p.comp("vector", lambda E, ds2=ds2: E.tensor_add(
                                out=dsc[:, ds2, :], in0=ln.sq[:, 0:T],
                                in1=dsc[:, ds2, :]), deps=[ta4])
                            t6 = p.comp("vector", lambda E, mt=mt, ds2=ds2: E.tensor_scalar(
                                out=delta16[:, mt, :], in0=dsc[:, ds2, :], scalar1=1e-4,
                                scalar2=10.0, op0=ALU.max, op1=ALU.min), deps=[t5])
                            dsc_rd[ds2] = [t6]
                            deld.append(t6)
                            return ta4

                        mm_stream2(p, ms, stgL, w16L, lambda mt: wd_in[mt],
                                   16, 16, lambda kt: left16[:, kt, :], d_evac,
                                   wkps, conv_d + [t_bs, t_cs, t_crs], gd)
                        bc_d = []
                        bmm = []
                        tmv = None
                        for n in range(32):
                            row = n if n < 16 else n + 16
                            tmv = p.dma("sync", bc1[0:1, n % 2, 0:T], bcst[row:row + 1, 0:T],
                                        deps=[t_bs, t_cs] + bmm[-2:-1] + deld[-1:])
                            tb = p.comp("tensor", lambda E, n=n: E.matmul(
                                out=mb.bcbank[:, :], lhsT=ones_row[:],
                                rhs=bc1[0:1, n % 2, 0:T], start=True, stop=True),
                                deps=[tmv] + bc_d[-1:])
                            bmm.append(tb)
                            dstt = bbc16 if n < 16 else cbc16
                            bc_d.append(p.comp("scalar", lambda E, n=n, dstt=dstt: E.activation(
                                out=dstt[:, n % 16, :], in_=mb.bcbank[:, :], func=AF.Copy),
                                deps=[tb] + mb.cbc_rd))
                        mb.t1_rd = [t_crs, tmv]
                        return dict(ch=ch, deld=deld, t_carry=t_carry,
                                    bc_d=bc_d, conv_d=conv_d)

                    fas = [None] * NCH
                    mds = [None] * NCH
                    fas[0] = frontA(0)
                    mds[0] = midA(fas[0])
                    for ch in range(NCH):
                        if ch + 1 < NCH:
                            fas[ch + 1] = frontA(ch + 1)
                        md = mds[ch]
                        scan_A(p, ctx, mb, ch, delta16, left16, bbc16, cbc16,
                               md["deld"] + [md["t_carry"]], md["bc_d"],
                               md["conv_d"], wc2)
                        if ch + 1 < NCH:
                            mds[ch + 1] = midA(fas[ch + 1])
                with ctx.scope():
                    _phaseA()
                barx = p.barrier()
                tc2 = emit_gather(p, ctx, G, sp_all[:, :], cc2_in, cc2_out, 1024, barx)
                wcH = emit_fold(p, ctx, G, cc2_out, Htile, tc2)

                def _phaseB():
                    xtc = sb(ctx, "xtcB", [128, 8, T], F32)
                    cbc16 = sb(ctx, "cbcB", [128, 16, T], BF16)
                    y2 = sb(ctx, "y2B", [128, 16, T], BF16)
                    y3 = sb(ctx, "y3B", [128, 16, T], BF16)
                    mb.dsb = sb(ctx, "dsb", [128, 2, T], F32)
                    mb.yacc = sb(ctx, "yacc", [128, 2, T], F32)
                    mb.xib = sb(ctx, "xib", [128, 2, T], BF16)
                    mb.dec = sb(ctx, "dec", [128, NR, T], F32)
                    mb.tmpc2 = sb(ctx, "tmpc2", [128, 4, T], F32)
                    ones_c16 = sb(ctx, "ones_c16", [128, 1], BF16)
                    bc1 = sb(ctx, "bc1B", [1, 2, T], F32)
                    mb.bc1 = bc1
                    ln = LnTmp(ctx, T)
                    wkps = [ps(ctx, f"wkpsB{i}", [128, T], F32) for i in range(2)]
                    mb.bcbank = ps(ctx, "bcbankB", [128, T], F32)
                    mb.g = 0
                    mb.dsb_rd = {0: [], 1: []}
                    mb.yacc_rd = {0: [], 1: []}
                    mb.xib_rd = {0: [], 1: []}
                    mb.dec_rd = {s: [] for s in range(NR)}
                    mb.tmpc_rd = {i: [] for i in range(4)}
                    ms = MS2()
                    stgB2 = [sb(ctx, f"stgB2_{i}", [128, 16, 128], F32) for i in range(2)]
                    w16B2 = [sb(ctx, f"w16B2_{i}", [128, 16, 128], BF16) for i in range(2)]
                    wc3 = [p.comp("vector", lambda E: E.memset(ones_c16[:], 1.0))]
                    a_eps = kn.row(EPS)
                    a_mh = kn.row(-0.5)
                    for ch in range(NCH):
                        b, j = ch // 2, ch % 2
                        cb = b * SEG + j * T
                        bar = p.barrier()
                        bc_d = emit_cbc(p, ctx, mb, ch, cbc16, bar)
                        y2d = corr_B(p, ctx, mb, ch, cbc16, y2, bar, wcH, bc_d)
                        sq_t, last_q, last_s = [], [], []
                        for i in range(16):
                            tq = p.comp("scalar", lambda E, i=i: E.activation(
                                out=ln.sq[:, 0:T], in_=y2[:, i, :], func=AF.Square),
                                deps=y2d + last_q + sq_t[-1:])
                            sq_t.append(tq)
                            last_q = [p.comp("tensor", lambda E, i=i: E.matmul(
                                out=ln.ps_q[:, 0:T], lhsT=ones_col[:], rhs=ln.sq[:, 0:T],
                                start=(i == 0), stop=(i == 15)), deps=[tq])]
                            last_s = [p.comp("tensor", lambda E, i=i: E.matmul(
                                out=ln.ps_s[:, 0:T], lhsT=ones_c16[:], rhs=y2[:, i, :],
                                start=(i == 0), stop=(i == 15)), deps=y2d + wc3)]
                        a_inv2 = kn.row(1.0 / 2048.0)
                        t_mean = p.comp("scalar", lambda E: E.activation(
                            out=ln.r_mean[:, 0:T], in_=ln.ps_s[:, 0:T], func=AF.Copy,
                            scale=a_inv2), deps=last_s + kn.dep())
                        t_msq = p.comp("scalar", lambda E: E.activation(
                            out=ln.r_msq[:, 0:T], in_=ln.ps_q[:, 0:T], func=AF.Copy,
                            scale=a_inv2), deps=last_q + kn.dep())
                        t_m2 = p.comp("scalar", lambda E: E.activation(
                            out=ln.r_tmp[:, 0:T], in_=ln.r_mean[:, 0:T], func=AF.Square),
                            deps=[t_mean])
                        t_var = p.comp("vector", lambda E: E.tensor_sub(
                            out=ln.r_msq[:, 0:T], in0=ln.r_msq[:, 0:T],
                            in1=ln.r_tmp[:, 0:T]), deps=[t_msq, t_m2])
                        t_lnv = p.comp("scalar", lambda E: E.activation(
                            out=ln.r_tmp[:, 0:T], in_=ln.r_msq[:, 0:T], func=AF.Ln,
                            bias=a_eps), deps=[t_var])
                        t_rst = p.comp("scalar", lambda E: E.activation(
                            out=ln.r_rstd[:, 0:T], in_=ln.r_tmp[:, 0:T], func=AF.Exp,
                            scale=a_mh), deps=[t_lnv])
                        t_mb = p.comp("tensor", lambda E: E.matmul(
                            out=ln.ps_mb[:, 0:T], lhsT=ones_row[:], rhs=ln.r_mean[:, 0:T],
                            start=True, stop=True), deps=[t_mean] + last_s)
                        t_rb = p.comp("tensor", lambda E: E.matmul(
                            out=ln.ps_rb[:, 0:T], lhsT=ones_row[:], rhs=ln.r_rstd[:, 0:T],
                            start=True, stop=True), deps=[t_rst])
                        y3d = []
                        for i in range(16):
                            ta = p.comp("vector", lambda E, i=i: E.tensor_sub(
                                out=ln.sq[:, 0:T], in0=y2[:, i, :], in1=ln.ps_mb[:, 0:T]),
                                deps=[t_mb] + y2d + sq_t + y3d[-1:])
                            tb2 = p.comp("vector", lambda E, i=i: E.tensor_mul(
                                out=ln.sq[:, 0:T], in0=ln.sq[:, 0:T], in1=ln.ps_rb[:, 0:T]),
                                deps=[t_rb, ta])
                            y3d.append(p.comp("vector", lambda E, i=i: E.tensor_scalar(
                                out=y3[:, i, :], in0=ln.sq[:, 0:T], scalar1=lng[:, i:i + 1],
                                scalar2=lnb[:, i:i + 1], op0=ALU.mult, op1=ALU.add),
                                deps=[tb2] + wl))
                        ldx2 = [p.dma("sync", xtc[:], src[:, cb:cb + T].rearrange(
                            "(a p) t -> p a t", p=128), deps=bar)]

                        def o_evac(mt, bank, deps):
                            t = p.comp("vector", lambda E, mt=mt, bank=bank: E.tensor_add(
                                out=xtc[:, mt, :], in0=bank[:, :], in1=xtc[:, mt, :]),
                                deps=deps + ldx2)
                            p.dma("gpsimd", dst[mt * 128:(mt + 1) * 128, cb:cb + T],
                                  xtc[:, mt, :], deps=[t])
                            return t

                        mm_stream2(p, ms, stgB2, w16B2, lambda mt: wo2_in[mt],
                                   16, 8, lambda kt: y3[:, kt, :], o_evac, wkps,
                                   y3d, bar)
                with ctx.scope():
                    _phaseB()

        stages = [
            (1, lambda: emit_l0_mamba(), sA),
            (3, lambda: emit_l0ffn(sA, sB), sB),
            (4, lambda: emit_l0attnb(sB, sA), sA),
            (5, lambda: emit_l0ffn(sA, sB), sB),
            (6, lambda: emit_l1attna(sB, sA), sA),
            (7, lambda: emit_l1ffn(sA, sB), sB),
            (8, lambda: emit_l1_mamba(sB, sA), sA),
            (10, lambda: emit_l1ffn(sA, xt_out), xt_out),
        ]
        cur = None
        for (sid, fn, buf) in stages:
            if sid > stop_after:
                break
            fn()
            cur = buf
        if dbg is not None and cur is not None and cur is not xt_out:
            fin = p.barrier()
            p.dma("gpsimd", dbg, cur, deps=fin)
        p.emit()
    return nc



_EYE = np.eye(128, dtype=np.float32)


def prep_weights(inputs):
    f32 = np.float32
    A = np.asarray

    def pc(v, n):
        return np.ascontiguousarray(A(v, f32).reshape(n, 128).T)

    w = {}
    w["ip2"] = np.ascontiguousarray(
        A(inputs["L0_attn_a_in_proj"], f32).reshape(8, 128, 32, 128).transpose(2, 1, 0, 3))
    w["op2"] = np.ascontiguousarray(
        A(inputs["L0_attn_a_out_proj"], f32).reshape(16, 128, 8, 128).transpose(2, 1, 0, 3))
    w["xp_l"] = np.ascontiguousarray(A(inputs["L0_attn_a_x_proj"], f32).reshape(16, 128, 96).transpose(1, 0, 2))
    w["dtw_l"] = np.ascontiguousarray(A(inputs["L0_attn_a_dt_w"], f32).reshape(64, 16, 128))
    w["rms_c"] = pc(inputs["L0_attn_a_rms_w"], 8)
    w["convw_c"] = np.ascontiguousarray(A(inputs["L0_attn_a_conv_w"], f32).reshape(16, 128, 4).transpose(1, 0, 2))
    w["convb_c"] = pc(inputs["L0_attn_a_conv_b"], 16)
    w["dtb_c"] = pc(inputs["L0_attn_a_dt_b"], 16)
    w["alog_c"] = np.ascontiguousarray(A(inputs["L0_attn_a_A_log"], f32).reshape(16, 128, 16).transpose(1, 0, 2))
    w["d_c"] = pc(inputs["L0_attn_a_D"], 16)
    w["win_l"] = np.ascontiguousarray(A(inputs["L0_ffn_Win"], f32).reshape(8, 128, 64).transpose(1, 0, 2))
    w["bin_c"] = A(inputs["L0_ffn_bin_"], f32).reshape(64, 1)
    w["w1"] = A(inputs["L0_ffn_W1"], f32)
    w["b1"] = A(inputs["L0_ffn_b1"], f32)
    w["w2"] = A(inputs["L0_ffn_W2"], f32)
    w["b2"] = A(inputs["L0_ffn_b2"], f32).reshape(64, 1)
    w["wout_l"] = np.ascontiguousarray(A(inputs["L0_ffn_Wout"], f32).reshape(64, 8, 128))
    w["bout_c"] = np.ascontiguousarray(A(inputs["L0_ffn_bout"], f32).reshape(8, 128).T)
    def tiler(wt):
        return np.ascontiguousarray(A(wt, f32).T.reshape(8, 128, D).transpose(1, 0, 2))
    w["wq_t"] = tiler(inputs["L0_attn_b_Wq"])
    w["wk_t"] = tiler(inputs["L0_attn_b_Wk"])
    w["wv_t"] = tiler(inputs["L0_attn_b_Wv"])
    w["bq_c"] = np.ascontiguousarray(A(inputs["L0_attn_b_bq"], f32).reshape(8, 128).T)
    w["bk_c"] = np.ascontiguousarray(A(inputs["L0_attn_b_bk"], f32).reshape(8, 128).T)
    w["bv_r"] = A(inputs["L0_attn_b_bv"], f32).reshape(1, D)
    W = A(inputs["L1_attn_a_W_qkvb"], f32)
    perm = []
    for comp in range(3):
        for h in range(16):
            perm.extend(range(h * 193 + comp * 64, h * 193 + comp * 64 + 64))
    perm.extend(h * 193 + 192 for h in range(16))
    Wp = W[np.array(perm)]
    WpT = np.zeros((1024, 3200), f32)
    WpT[:, 0:3088] = Wp.T
    w["wqkvb2"] = np.ascontiguousarray(
        WpT.reshape(8, 128, 25, 128).transpose(2, 1, 0, 3))
    w["woA2"] = np.ascontiguousarray(
        A(inputs["L1_attn_a_W_o"], f32).T.reshape(8, 128, 8, 128).transpose(2, 1, 0, 3))
    w["lng_c"] = np.ascontiguousarray(A(inputs["L1_attn_a_ln_g"], f32).reshape(8, 128).T)
    w["lnb_c"] = np.ascontiguousarray(A(inputs["L1_attn_a_ln_b"], f32).reshape(8, 128).T)
    up = A(inputs["L1_ffn_up_proj"], f32)
    dn = A(inputs["L1_ffn_down_proj"], f32)
    w["up_t"] = np.ascontiguousarray(up.reshape(8, 8, 128, 16, 128).transpose(0, 3, 2, 1, 4))
    w["dn_t"] = np.ascontiguousarray(dn.reshape(8, 8, 128, 8, 128).transpose(0, 3, 2, 1, 4))
    w["rw_l"] = np.ascontiguousarray(A(inputs["L1_ffn_router_w"], f32).T.reshape(8, 128, 8))
    w["wl2"] = np.ascontiguousarray(
        A(inputs["L1_attn_b_W_left"], f32).T.reshape(8, 128, 16, 128).transpose(2, 1, 0, 3))
    w["wd2"] = np.ascontiguousarray(
        A(inputs["L1_attn_b_W_delta"], f32).T.reshape(16, 128, 16, 128).transpose(2, 1, 0, 3))
    w["wr2"] = np.ascontiguousarray(
        A(inputs["L1_attn_b_W_right"], f32).T.reshape(8, 128, 16, 128).transpose(2, 1, 0, 3))
    w["wo22"] = np.ascontiguousarray(
        A(inputs["L1_attn_b_W_out"], f32).T.reshape(16, 128, 8, 128).transpose(2, 1, 0, 3))
    w["wb_l"] = np.ascontiguousarray(A(inputs["L1_attn_b_W_B"], f32).T.reshape(16, 128, 16))
    w["wc_l"] = np.ascontiguousarray(A(inputs["L1_attn_b_W_C"], f32).T.reshape(16, 128, 16))
    w["convw2_c"] = np.ascontiguousarray(A(inputs["L1_attn_b_conv_w"], f32).reshape(16, 128, 4).transpose(1, 0, 2))
    w["convb2_c"] = pc(inputs["L1_attn_b_conv_b"], 16)
    w["bdel_c"] = pc(inputs["L1_attn_b_b_delta"], 16)
    w["alog_r"] = A(inputs["L1_attn_b_A_log"], f32).reshape(1, 16)
    w["lng2_c"] = pc(inputs["L1_attn_b_ln_g"], 16)
    w["lnb2_c"] = pc(inputs["L1_attn_b_ln_b"], 16)
    w["eye"] = _EYE
    return w


def make_inmaps(inputs):
    w = prep_weights(inputs)
    x = np.asarray(inputs["L0_attn_a_x"], np.float32)
    xT = np.ascontiguousarray(x.reshape(B * L, D).T)
    maps = []
    for c in range(NCORES):
        m = dict(w)
        x0e = np.zeros((D, 2 * (SEG + 3)), np.float32)
        for b in range(B):
            lo = b * L + c * SEG
            x0e[:, b * (SEG + 3) + 3: (b + 1) * (SEG + 3)] = xT[:, lo:lo + SEG]
            if c > 0:
                x0e[:, b * (SEG + 3): b * (SEG + 3) + 3] = xT[:, lo - 3:lo]
        m["x0e"] = x0e
        om = np.zeros((128, 8), np.float32)
        om[:, c] = 1.0
        m["ownm"] = om
        pm1 = np.zeros((128, 8), np.float32)
        pm1[:, :c] = 1.0
        m["pm1"] = pm1
        hs = np.zeros((128, 8), np.float32)
        if c > 0:
            hs[:, c - 1] = 1.0
        m["hsel"] = hs
        maps.append(m)
    return maps


def assemble(results, key="xt_new"):
    xT = np.empty((D, B * L), np.float32)
    for c in range(NCORES):
        r = results[c][key]
        for b in range(B):
            xT[:, b * L + c * SEG:(b * L) + (c + 1) * SEG] = \
                r[:, b * SEG:(b + 1) * SEG]
    return np.ascontiguousarray(xT.T).reshape(B, L, D)


def kernel(**inputs):
    if "nc" not in _CACHE:
        _CACHE["nc"] = build_fused()
    nc = _CACHE["nc"]
    maps = make_inmaps(inputs)
    res = run_bass_kernel_spmd(nc, maps, core_ids=list(range(NCORES)))
    return assemble(res.results)


# version 12
# speedup vs baseline: 1.1778x; 1.1778x over previous
import contextlib
import numpy as np
import concourse.bass as bass
import concourse.mybir as mybir
from concourse.bass_utils import run_bass_kernel_spmd

F32 = mybir.dt.float32
BF16 = mybir.dt.bfloat16
AF = mybir.ActivationFunctionType
ALU = mybir.AluOpType

NCORES = 8
B, L, D = 2, 8192, 1024
SEG = 1024
TOK = 2 * SEG
T = 512
NCH = 4
TE = T + 3
EPS = 1e-5
RG = [list(range(8))]



class Ctx:
    def __init__(self, nc, stack):
        self.nc = nc
        self.stacks = [stack]
        self.sems = {}
        self.scope_id = 0

    @property
    def stack(self):
        return self.stacks[-1]

    @contextlib.contextmanager
    def scope(self):
        self.scope_id += 1
        es = contextlib.ExitStack()
        self.stacks.append(es)
        try:
            yield
        finally:
            self.stacks.pop()
            es.close()

    def sem(self, name):
        if name not in self.sems:
            self.sems[name] = [self.stacks[0].enter_context(self.nc.semaphore(name)), 0]
        return self.sems[name]


def sb(ctx, name, shape, dtype):
    return ctx.stack.enter_context(
        ctx.nc.sbuf_tensor(f"s{ctx.scope_id}_{name}", shape, dtype))


def ps(ctx, name, shape, dtype=F32):
    return ctx.stack.enter_context(
        ctx.nc.psum_tensor(f"s{ctx.scope_id}_{name}", shape, dtype))


ENGINES = ("sync", "tensor", "vector", "scalar", "gpsimd")
NDMASLOT = 8


class P:

    def __init__(self, ctx):
        self.ctx = ctx
        self.ops = []
        self.cnt = {e: 0 for e in ENGINES}
        self.dcnt = {}
        self.didx = {e: 0 for e in ENGINES}
        self.dlast = {}
        self.ccnt = 0
        self.cclast = None

    def comp(self, eng, fn, deps=()):
        self.cnt[eng] += 1
        key = "E_" + eng
        tok = (key, self.cnt[eng])
        self.ops.append((eng, fn, list(deps), 1, key))
        return tok

    def dma(self, eng, out, in_, deps=()):
        slot = self.didx[eng] % NDMASLOT
        self.didx[eng] += 1
        key = f"D_{eng}{slot}"
        deps = list(deps) + ([self.dlast[key]] if key in self.dlast else [])
        self.dcnt[key] = self.dcnt.get(key, 0) + 16
        tok = (key, self.dcnt[key])
        self.dlast[key] = tok
        self.ops.append((eng, lambda E, o=out, i=in_: E.dma_start(out=o, in_=i),
                         deps, 16, key))
        return tok

    def coll(self, fn, deps=()):
        self.ccnt += 1
        key = "CC"
        deps = list(deps) + ([self.cclast] if self.cclast else [])
        tok = (key, self.ccnt)
        self.cclast = tok
        self.ops.append(("gpsimd", fn, deps, 1, key))
        return tok

    def barrier(self):
        toks = [("E_" + e, self.cnt[e]) for e in ENGINES if self.cnt[e] > 0]
        toks += [(k, v) for k, v in self.dcnt.items()]
        if self.ccnt:
            toks.append(("CC", self.ccnt))
        return toks

    def emit(self):
        nc = self.ctx.nc
        semof = {}
        for e in ENGINES:
            semof["E_" + e] = self.ctx.sem("E_" + e)[0]
        for k in self.dcnt:
            semof[k] = self.ctx.sem(k)[0]
        if self.ccnt:
            semof["CC"] = self.ctx.sem("CC")[0]
        ops = self.ops
        final_dma = dict(self.dcnt)
        final_cnt = dict(self.cnt)
        final_cc = self.ccnt

        def make(eng_name):
            def run(E):
                mykey = "E_" + eng_name
                waited = {}
                for (eng, fn, deps, inc, key) in ops:
                    if eng != eng_name:
                        continue
                    for (dk, dv) in deps:
                        if dv <= waited.get(dk, 0):
                            continue
                        E.wait_ge(semof[dk], dv)
                        waited[dk] = dv
                    instr = fn(E)
                    if isinstance(instr, list):
                        instr = instr[-1]
                    instr.then_inc(semof[key], inc)
                if final_cnt[eng_name] > 0:
                    E.wait_ge(semof[mykey], final_cnt[eng_name])
                for s in range(NDMASLOT):
                    k = f"D_{eng_name}{s}"
                    if k in final_dma:
                        E.wait_ge(semof[k], final_dma[k])
                if eng_name == "gpsimd" and final_cc:
                    E.wait_ge(semof["CC"], final_cc)
            return run

        with nc.Block("prog") as blk:
            blk.sync(make("sync"))
            blk.tensor(make("tensor"))
            blk.vector(make("vector"))
            blk.scalar(make("scalar"))
            blk.gpsimd(make("gpsimd"))


class Konst:
    def __init__(self, ctx, p, n=24):
        self.tile = sb(ctx, "konst", [128, n], F32)
        self.p = p
        self.vals = {}
        self.toks = []
        self.n = n

    def ap(self, v):
        v = float(v)
        if v not in self.vals:
            i = len(self.vals)
            assert i < self.n
            self.vals[v] = i
            self.toks.append(self.p.comp(
                "vector", lambda E, i=i, v=v: E.memset(self.tile[:, i:i + 1], v)))
        i = self.vals[v]
        return self.tile[:, i:i + 1]

    def row(self, v):
        return self.ap(v)[0:1, :]

    def dep(self):
        return list(self.toks)


class LnTmp:
    def __init__(self, ctx, T_, tag=""):
        self.T = T_
        self.sq = sb(ctx, f"ln_sq{tag}", [128, T_], F32)
        self.r_mean = sb(ctx, f"ln_rm{tag}", [1, T_], F32)
        self.r_msq = sb(ctx, f"ln_rq{tag}", [1, T_], F32)
        self.r_tmp = sb(ctx, f"ln_rt{tag}", [1, T_], F32)
        self.r_rstd = sb(ctx, f"ln_rr{tag}", [1, T_], F32)
        self.t1 = sb(ctx, f"ln_t1{tag}", [128, T_], F32)
        self.ps_s = ps(ctx, f"ln_ps_s{tag}", [1, T_], F32)
        self.ps_q = ps(ctx, f"ln_ps_q{tag}", [1, T_], F32)
        self.ps_mb = ps(ctx, f"ln_ps_mb{tag}", [128, T_], F32)
        self.ps_rb = ps(ctx, f"ln_ps_rb{tag}", [128, T_], F32)


def emit_ln(p, ln, xt, nt, ones_col, ones_row, out16, dep_x, kn, T_=None,
            out_f32=None, col0=0, ocol0=0):
    T_ = T_ or ln.T
    inv = 1.0 / (nt * 128)
    mm_toks = []
    sq_toks = []
    last_qmm = []
    for i in range(nt):
        tsq = p.comp("scalar", lambda E, i=i: E.activation(
            out=ln.sq[:, 0:T_], in_=xt[:, i, col0:col0 + T_], func=AF.Square),
            deps=dep_x + last_qmm + sq_toks[-1:])
        sq_toks.append(tsq)
        mm_toks.append(p.comp("tensor", lambda E, i=i: E.matmul(
            out=ln.ps_s[:, 0:T_], lhsT=ones_col, rhs=xt[:, i, col0:col0 + T_],
            start=(i == 0), stop=(i == nt - 1)), deps=dep_x))
        last_qmm = [p.comp("tensor", lambda E, i=i: E.matmul(
            out=ln.ps_q[:, 0:T_], lhsT=ones_col, rhs=ln.sq[:, 0:T_],
            start=(i == 0), stop=(i == nt - 1)), deps=[tsq])]
        mm_toks.append(last_qmm[0])
    mm = [mm_toks[-1]]
    a_inv = kn.row(inv)
    a_eps = kn.row(EPS)
    a_mh = kn.row(-0.5)
    kd = kn.dep()
    t_mean = p.comp("scalar", lambda E: E.activation(
        out=ln.r_mean[:, 0:T_], in_=ln.ps_s[:, 0:T_], func=AF.Copy,
        scale=a_inv), deps=mm + kd)
    t_msq = p.comp("scalar", lambda E: E.activation(
        out=ln.r_msq[:, 0:T_], in_=ln.ps_q[:, 0:T_], func=AF.Copy,
        scale=a_inv), deps=mm + kd)
    t_m2 = p.comp("scalar", lambda E: E.activation(
        out=ln.r_tmp[:, 0:T_], in_=ln.r_mean[:, 0:T_], func=AF.Square), deps=[t_mean])
    t_var = p.comp("vector", lambda E: E.tensor_sub(
        out=ln.r_msq[:, 0:T_], in0=ln.r_msq[:, 0:T_], in1=ln.r_tmp[:, 0:T_]),
        deps=[t_msq, t_m2])
    t_ln = p.comp("scalar", lambda E: E.activation(
        out=ln.r_tmp[:, 0:T_], in_=ln.r_msq[:, 0:T_], func=AF.Ln,
        bias=a_eps), deps=[t_var] + kd)
    t_rstd = p.comp("scalar", lambda E: E.activation(
        out=ln.r_rstd[:, 0:T_], in_=ln.r_tmp[:, 0:T_], func=AF.Exp,
        scale=a_mh), deps=[t_ln] + kd)
    t_mb = p.comp("tensor", lambda E: E.matmul(
        out=ln.ps_mb[:, 0:T_], lhsT=ones_row, rhs=ln.r_mean[:, 0:T_],
        start=True, stop=True), deps=[t_mean, mm[0]])
    t_rb = p.comp("tensor", lambda E: E.matmul(
        out=ln.ps_rb[:, 0:T_], lhsT=ones_row, rhs=ln.r_rstd[:, 0:T_],
        start=True, stop=True), deps=[t_rstd])
    outdeps = []
    for i in range(nt):
        ta = p.comp("vector", lambda E, i=i: E.tensor_sub(
            out=ln.t1[:, 0:T_], in0=xt[:, i, col0:col0 + T_], in1=ln.ps_mb[:, 0:T_]),
            deps=dep_x + [t_mb] + outdeps[-1:])
        tb = p.comp("vector", lambda E, i=i: E.tensor_mul(
            out=out16[:, i, ocol0:ocol0 + T_], in0=ln.t1[:, 0:T_], in1=ln.ps_rb[:, 0:T_]),
            deps=[ta, t_rb])
        outdeps.append(tb)
        if out_f32 is not None:
            outdeps.append(p.comp("vector", lambda E, i=i: E.tensor_mul(
                out=out_f32[:, i, 0:T_], in0=ln.t1[:, 0:T_], in1=ln.ps_rb[:, 0:T_]),
                deps=[ta, t_rb]))
    return outdeps, (t_mb, t_rb)


class MS2:

    def __init__(self):
        self.bank_rd = {0: [], 1: []}


def mm_stream2(p, st, stg, w16, w_ap, nkt, nmt, rhs_ap, evac, banks, dep_rhs,
               bar, ext=None, scr=None, first=True):
    queues = ("sync", "scalar")
    if not hasattr(p, "_ms2cv"):
        p._ms2cv = []
        p._ms2rd = {0: [], 1: []}
    if not hasattr(p, "_ms2st"):
        p._ms2st = {}
    cv = p._ms2cv
    out_toks = []
    for mt in range(nmt):
        g = len(cv)
        q = g % 2
        sl = g % 2
        if scr is not None and not first:
            skey = (id(scr), mt)
            tld = p.dma(queues[q], w16[sl][:, 0:nkt, :], scr[mt],
                        deps=p._ms2rd[sl] + p._ms2st.get(skey, []) +
                        ([cv[-2]] if len(cv) >= 2 else bar))
            tcv = tld
        else:
            tld = p.dma(queues[q], stg[sl][:, 0:nkt, :], w_ap(mt),
                        deps=([cv[-2]] if len(cv) >= 2 else bar))
            tcv = p.comp("vector", lambda E, sl=sl, nkt=nkt: E.tensor_copy(
                out=w16[sl][:, 0:nkt, :], in_=stg[sl][:, 0:nkt, :]),
                deps=[tld] + p._ms2rd[sl])
            if scr is not None:
                skey = (id(scr), mt)
                tst = p.dma(queues[1 - q], scr[mt], w16[sl][:, 0:nkt, :],
                            deps=[tcv])
                p._ms2st[skey] = [tst]
                p._ms2rd[sl] = p._ms2rd[sl] + [tst]
        cv.append(tcv)
        bank = banks[mt % 2]
        tmm = None
        for kt in range(nkt):
            deps = [tcv]
            if kt == 0:
                deps = deps + dep_rhs + st.bank_rd[mt % 2]
            tmm = p.comp("tensor", lambda E, sl=sl, kt=kt, bank=bank: E.matmul(
                out=bank[:, :], lhsT=w16[sl][:, kt, :], rhs=rhs_ap(kt),
                start=(kt == 0), stop=(kt == nkt - 1)), deps=deps)
        tev = evac(mt, bank, [tmm])
        out_toks.append(tev)
        st.bank_rd[mt % 2] = [tev]
        p._ms2rd[sl] = [tmm]
        if ext is not None and mt < ext[4]:
            rhs2_ap, evac2, n2, dep_rhs2 = ext[0], ext[1], ext[2], ext[3]
            tm2 = None
            for kt in range(nkt):
                deps = [tcv]
                if kt == 0:
                    deps = deps + list(dep_rhs2) + [tev]
                tm2 = p.comp("tensor", lambda E, sl=sl, kt=kt, bank=bank, n2=n2: E.matmul(
                    out=bank[:, 0:n2], lhsT=w16[sl][:, kt, :], rhs=rhs2_ap(kt),
                    start=(kt == 0), stop=(kt == nkt - 1)), deps=deps)
            tev2 = evac2(mt, bank, [tm2])
            out_toks.append(tev2)
            st.bank_rd[mt % 2] = [tev, tev2]
            p._ms2rd[sl] = [tm2]
    return out_toks


def mm_stream(p, stages, w16s, w_ap, nkt, nmt, rhs_ap, evac, banks, dep_rhs,
              bar, ktp=128, nout=None):
    queues = ("sync", "gpsimd")
    ev_hist = []
    if not hasattr(p, "_mm_cv"):
        p._mm_cv = []
        p._mm_mm = []
    cv_hist = p._mm_cv
    mm_hist = p._mm_mm
    out_toks = []
    for mt in range(nmt):
        bank = banks[mt % 2]
        tmm = None
        for kt in range(nkt):
            g = len(cv_hist)
            q = g % 2
            s = g % 4
            tld = p.dma(queues[q], stages[q][:], w_ap(kt, mt),
                        deps=(cv_hist[-2:-1] if len(cv_hist) >= 2 else bar))
            tcv = p.comp("vector", lambda E, s=s, q=q: E.tensor_copy(
                out=w16s[:, s, :], in_=stages[q][:]),
                deps=[tld] + (mm_hist[-4:-3] if len(mm_hist) >= 4 else []))
            cv_hist.append(tcv)
            deps = [tcv]
            if kt == 0:
                deps += dep_rhs + ev_hist[-2:-1]
            tmm = p.comp("tensor", lambda E, s=s, kt=kt, bank=bank: E.matmul(
                out=bank[:, :] if nout is None else bank[:, 0:nout],
                lhsT=w16s[:, s, :], rhs=rhs_ap(kt),
                start=(kt == 0), stop=(kt == nkt - 1)), deps=deps)
            mm_hist.append(tmm)
        tev = evac(mt, bank, [tmm])
        ev_hist.append(tev)
        out_toks.append(tev)
    return out_toks




def emit_gather(p, ctx, G, src_ap, cc_in, cc_out, ncols, deps):
    slot = [sb(ctx, f"agslot{i}", [128, ncols], F32) for i in range(2)]
    slot_rd = {0: [], 1: []}
    dts = []
    for r in range(8):
        sl = r % 2
        tv = p.comp("vector", lambda E, r=r, sl=sl: E.tensor_scalar(
            out=slot[sl][:, :], in0=src_ap, scalar1=G.ownm[:, r:r + 1],
            scalar2=None, op0=ALU.mult), deps=deps + slot_rd[sl])
        td = p.dma("gpsimd", cc_in[r * 128:(r + 1) * 128, :], slot[sl][:, :],
                   deps=[tv])
        slot_rd[sl] = [td]
        dts.append(td)
    tc = p.coll(lambda E: E.collective_compute(
        "AllReduce", ALU.add, replica_groups=RG,
        ins=[cc_in.opt()], outs=[cc_out.opt()]), deps=dts)
    return tc


def emit_fold(p, ctx, G, cc_out, Htile, tc):
    fi = [sb(ctx, f"fold{i}", [128, 1024], F32) for i in range(2)]
    pmt = sb(ctx, "pmt", [128, 512], F32)
    t0 = p.comp("vector", lambda E: E.memset(Htile[:, :], 0.0), deps=[tc])
    fi_rd = {0: [], 1: []}
    th2 = t0
    for r in range(8):
        sl = r % 2
        tl = p.dma("sync", fi[sl][:, :], cc_out[r * 128:(r + 1) * 128, :],
                   deps=[tc] + fi_rd[sl])
        tpm = p.comp("vector", lambda E, r=r, sl=sl: E.tensor_scalar(
            out=pmt[:, :], in0=fi[sl][:, 512:1024], scalar1=G.pm1[:, r:r + 1],
            scalar2=G.pm0[:, r:r + 1], op0=ALU.mult, op1=ALU.add),
            deps=[tl, th2])
        th1 = p.comp("vector", lambda E: E.tensor_mul(
            out=Htile[:, :], in0=Htile[:, :], in1=pmt[:, :]), deps=[tpm])
        th2 = p.comp("vector", lambda E, r=r, sl=sl: E.scalar_tensor_tensor(
            out=Htile[:, :], in0=fi[sl][:, 0:512], scalar=G.pm1[:, r:r + 1],
            in1=Htile[:, :], op0=ALU.mult, op1=ALU.add), deps=[th1])
        fi_rd[sl] = [th2]
    return [th2]




NR = 3


def scan_A(p, ctx, mb, ch, delta16, xi_t, bbc16, cbc16, deltad, bc_d, conv_d, wc):
    b, j = ch // 2, ch % 2
    st_toks = []
    for dt in range(16):
        dsl = dt % 2
        init_ds = mb.kzero if j == 0 else mb.dstate[:, dt:dt + 1]
        t_ds = p.comp("vector", lambda E, dt=dt, dsl=dsl, init_ds=init_ds: E.tensor_tensor_scan(
            out=mb.Ds[:, dsl, :], data0=delta16[:, dt, :], data1=mb.z16zero[:],
            initial=init_ds, op0=ALU.add, op1=ALU.add),
            deps=deltad + wc + mb.ds_rd[dsl])
        t_dss = p.comp("vector", lambda E, dt=dt, dsl=dsl: E.tensor_copy(
            out=mb.dstate[:, dt:dt + 1], in_=mb.Ds[:, dsl, T - 1:T]), deps=[t_ds])
        t_sds = p.dma("sync", mb.scr_ds[ch, dt], mb.Ds[:, dsl, :], deps=[t_ds])
        st_toks.append(t_sds)
        mb.ds_rd[dsl] = [t_dss, t_sds]
        t_du = p.comp("vector", lambda E, dt=dt, dsl=dsl: E.tensor_mul(
            out=mb.du16[:, dsl, :], in0=delta16[:, dt, :], in1=xi_t[:, dt, :]),
            deps=deltad + conv_d + mb.du_rd[dsl])
        t_h = None
        t_hv = {0: None, 1: None}
        for n in range(16):
            sl = mb.g % NR
            s2 = mb.g % 2
            mb.g += 1
            acol_ap = mb.acol_fn(dt, n)
            t_dA = p.comp("scalar", lambda E, dt=dt, sl=sl, acol_ap=acol_ap: E.activation(
                out=mb.dA[:, sl, :], in_=delta16[:, dt, :], func=AF.Exp,
                scale=acol_ap), deps=deltad + wc + mb.da_rd[sl])
            t_u = p.comp("vector", lambda E, n=n, s2=s2, dsl=dsl: E.tensor_mul(
                out=mb.u16[:, s2, :], in0=mb.du16[:, dsl, :], in1=bbc16[:, n, :]),
                deps=[t_du] + bc_d + mb.u_rd[s2])
            scol = mb.spS(b, dt, n)
            init_h = mb.kzero if j == 0 else scol
            eng_h = "vector"
            t_h = p.comp(eng_h, lambda E, sl=sl, s2=s2, init_h=init_h: E.tensor_tensor_scan(
                out=mb.hS[:, s2, :], data0=mb.dA[:, sl, :], data1=mb.u16[:, s2, :],
                initial=init_h, op0=ALU.mult, op1=ALU.add),
                deps=[t_dA, t_u] + mb.h_rd[s2])
            t_hv[s2] = t_h
            mb.da_rd[sl] = [t_h]
            mb.u_rd[s2] = [t_h]
            t_cp = p.comp("gpsimd", lambda E, s2=s2, scol=scol: E.tensor_copy(
                out=scol, in_=mb.hS[:, s2, T - 1:T]), deps=[t_h])
            eng_t = "gpsimd"
            t_tmp = p.comp(eng_t, lambda E, n=n, sl=sl, s2=s2: E.tensor_mul(
                out=mb.tmp16[:, sl, :], in0=mb.hS[:, s2, :], in1=cbc16[:, n, :]),
                deps=[t_h] + bc_d + mb.tmp_rd[sl])
            mb.h_rd[s2] = [t_cp, t_tmp]
            ymm = p.comp("tensor", lambda E, n=n, sl=sl: E.matmul(
                out=mb.ybank[:, :], lhsT=mb.eye16[:], rhs=mb.tmp16[:, sl, :],
                start=(n == 0), stop=(n == 15)),
                deps=[t_tmp] + wc + (mb.last_ya if n == 0 else []))
            mb.tmp_rd[sl] = [ymm]
            mb.cbc_rd = [t_tmp]
        mb.du_rd[dsl] = [t for t in t_hv.values() if t is not None]
        ysl = dt % 2
        t_ev = p.comp("scalar", lambda E, ysl=ysl: E.activation(
            out=mb.ysb[:, ysl, :], in_=mb.ybank[:, :], func=AF.Copy),
            deps=mb.tmp_rd[(mb.g - 1) % NR] + mb.ysb_rd[ysl])
        mb.last_ya = [t_ev]
        t_sy = p.dma("scalar", mb.scr_y[ch, dt], mb.ysb[:, ysl, :], deps=[t_ev])
        mb.ysb_rd[ysl] = [t_sy]
        st_toks.append(t_sy)
        if j == 1:
            pts = []
            for n in range(16):
                pcol = mb.spP(b, dt, n)
                pts.append(p.comp("scalar", lambda E, pcol=pcol, dsl=dsl,
                                  acol_ap=mb.acol_fn(dt, n): E.activation(
                    out=pcol, in_=mb.Ds[:, dsl, T - 1:T], func=AF.Exp,
                    scale=acol_ap), deps=[t_ds] + wc))
            mb.ds_rd[dsl] = mb.ds_rd[dsl] + pts
    return st_toks


def corr_B(p, ctx, mb, ch, cbc16, y2, bar, wcH, bc_d):
    b = ch // 2
    y2d = []
    for dt in range(16):
        s2 = dt % 2
        tld_ds = p.dma("sync", mb.dsb[:, s2, :], mb.scr_ds[ch, dt],
                       deps=bar + mb.dsb_rd[s2])
        tld_y = p.dma("scalar", mb.yacc[:, s2, :], mb.scr_y[ch, dt],
                      deps=bar + mb.yacc_rd[s2])
        fma_last = [tld_y]
        t_dec = None
        eng = "vector" if s2 == 0 else "gpsimd"
        for n in range(16):
            sl = mb.g % NR
            mb.g += 1
            t_dec = p.comp("scalar", lambda E, s2=s2, sl=sl, acol_ap=mb.acol_fn(dt, n): E.activation(
                out=mb.dec[:, sl, :], in_=mb.dsb[:, s2, :], func=AF.Exp,
                scale=acol_ap), deps=[tld_ds] + mb.dec_rd[sl])
            st = s2 * 2 + (n % 2)
            eng_t2 = "vector" if (s2 == 0 and n % 3 == 0) else "gpsimd"
            t_t2 = p.comp(eng_t2, lambda E, n=n, sl=sl, st=st: E.tensor_mul(
                out=mb.tmpc2[:, st, :], in0=mb.dec[:, sl, :], in1=cbc16[:, n, :]),
                deps=[t_dec] + bc_d + mb.tmpc_rd[st])
            mb.dec_rd[sl] = [t_t2]
            hcol = mb.Hcol(b, dt, n)
            t_fma = p.comp("vector", lambda E, s2=s2, st=st, hcol=hcol: E.scalar_tensor_tensor(
                out=mb.yacc[:, s2, :], in0=mb.tmpc2[:, st, :], scalar=hcol,
                in1=mb.yacc[:, s2, :], op0=ALU.mult, op1=ALU.add),
                deps=[t_t2] + fma_last + wcH)
            fma_last = [t_fma]
            mb.tmpc_rd[st] = [t_fma]
        mb.dsb_rd[s2] = [t_dec]
        if mb.is_l0:
            tld_xi = p.dma("sync", mb.xib[:, s2, :], mb.scr_xi[ch, dt],
                           deps=bar + mb.xib_rd[s2])
            tld_z = p.dma("scalar", mb.zb[:, s2, :], mb.scr_z[ch, dt],
                          deps=bar + mb.zb_rd[s2])
            t_sz = p.comp("scalar", lambda E, s2=s2: E.activation(
                out=mb.szb[:, s2, :], in_=mb.zb[:, s2, :], func=AF.Silu),
                deps=[tld_z] + mb.szb_rd[s2])
            t_ya = p.comp("vector", lambda E, s2=s2, dt=dt: E.scalar_tensor_tensor(
                out=mb.yacc[:, s2, :], in0=mb.xib[:, s2, :],
                scalar=mb.d_c[:, dt:dt + 1], in1=mb.yacc[:, s2, :],
                op0=ALU.mult, op1=ALU.add), deps=[tld_xi] + fma_last)
            t_y2 = p.comp(eng, lambda E, s2=s2, dt=dt: E.tensor_mul(
                out=y2[:, dt, :], in0=mb.yacc[:, s2, :], in1=mb.szb[:, s2, :]),
                deps=[t_ya, t_sz])
            mb.xib_rd[s2] = [t_ya]
            mb.zb_rd[s2] = [t_sz]
            mb.szb_rd[s2] = [t_y2]
            mb.yacc_rd[s2] = [t_y2]
        else:
            tld_g = p.dma("sync", mb.xib[:, s2, :], mb.scr_z[ch, dt],
                          deps=bar + mb.xib_rd[s2])
            t_y2 = p.comp(eng, lambda E, s2=s2, dt=dt: E.tensor_mul(
                out=y2[:, dt, :], in0=mb.yacc[:, s2, :], in1=mb.xib[:, s2, :]),
                deps=[tld_g] + fma_last)
            mb.xib_rd[s2] = [t_y2]
            mb.yacc_rd[s2] = [t_y2]
        y2d.append(t_y2)
    return y2d


def emit_cbc(p, ctx, mb, ch, cbc16, bar, extra=()):
    bc_d = []
    bmm = []
    for n in range(16):
        tmv = p.dma("sync", mb.bc1[0:1, n % 2, 0:T], mb.scr_cr[ch, n],
                    deps=bar + bmm[-2:-1] + list(extra))
        tb = p.comp("tensor", lambda E, n=n: E.matmul(
            out=mb.bcbank[:, :], lhsT=mb.ones_row[:], rhs=mb.bc1[0:1, n % 2, 0:T],
            start=True, stop=True), deps=[tmv] + bc_d[-1:])
        bmm.append(tb)
        bc_d.append(p.comp("scalar", lambda E, n=n: E.activation(
            out=cbc16[:, n, :], in_=mb.bcbank[:, :], func=AF.Copy),
            deps=[tb]))
    return bc_d


class MB:
    pass



_CACHE = {}


def build_fused(stop_after=99):
    nc = bass.Bass("TRN2", target_bir_lowering=False)

    def din(name, shape, dtype=F32):
        return nc.dram_tensor(name, shape, dtype, kind="ExternalInput").ap()

    x0e = din("x0e", [D, 2 * (SEG + 3)])
    eye_in = din("eye", [128, 128])
    ownm_in = din("ownm", [128, 8])
    pm1_in = din("pm1", [128, 8])
    hsel_in = din("hsel", [128, 8])
    xt_out = nc.dram_tensor("xt_new", [D, TOK], F32, kind="ExternalOutput").ap()
    dbg = (nc.dram_tensor("dbg", [D, TOK], F32, kind="ExternalOutput").ap()
           if stop_after < 99 else None)
    ip_in = din("ip2", [32, 128, 8, 128])
    op_in = din("op2", [8, 128, 16, 128])
    xp_in = din("xp_l", [128, 16, 96])
    dtw_in = din("dtw_l", [64, 16, 128])
    rms_in = din("rms_c", [128, 8])
    convw_in = din("convw_c", [128, 16, 4])
    convb_in = din("convb_c", [128, 16])
    dtb_in = din("dtb_c", [128, 16])
    alog_in = din("alog_c", [128, 16, 16])
    d_in = din("d_c", [128, 16])
    win_in = din("win_l", [128, 8, 64])
    bin_in = din("bin_c", [64, 1])
    w1_in = din("w1", [64, 16])
    b1_in = din("b1", [64, 16])
    w2_in = din("w2", [64, 16])
    b2_in = din("b2", [64, 1])
    wout_in = din("wout_l", [64, 8, 128])
    bout_in = din("bout_c", [128, 8])
    wq_in = din("wq_t", [128, 8, D])
    wk_in = din("wk_t", [128, 8, D])
    wv_in = din("wv_t", [128, 8, D])
    bq_in = din("bq_c", [128, 8])
    bk_in = din("bk_c", [128, 8])
    bv_in = din("bv_r", [1, D])
    wqkvb_in = din("wqkvb2", [25, 128, 8, 128])
    woA_in = din("woA2", [8, 128, 8, 128])
    lngA_in = din("lng_c", [128, 8])
    lnbA_in = din("lnb_c", [128, 8])
    up_in = din("up_t", [8, 16, 128, 8, 128])
    dn_in = din("dn_t", [8, 8, 128, 8, 128])
    rw_in = din("rw_l", [8, 128, 8])
    wl_in = din("wl2", [16, 128, 8, 128])
    wd_in = din("wd2", [16, 128, 16, 128])
    wr_in = din("wr2", [16, 128, 8, 128])
    wo2_in = din("wo22", [8, 128, 16, 128])
    wb_in = din("wb_l", [16, 128, 16])
    wcc_in = din("wc_l", [16, 128, 16])
    convw2_in = din("convw2_c", [128, 16, 4])
    convb2_in = din("convb2_c", [128, 16])
    bdel_in = din("bdel_c", [128, 16])
    alog2_in = din("alog_r", [1, 16])
    lng2_in = din("lng2_c", [128, 16])
    lnb2_in = din("lnb2_c", [128, 16])

    sA = nc.dram_tensor("sA", [D, TOK], F32).ap()
    sB = nc.dram_tensor("sB", [D, TOK], F32).ap()
    scr_y = nc.dram_tensor("scr_y", [NCH, 16, 128, T], F32).ap()
    scr_ds = nc.dram_tensor("scr_ds", [NCH, 16, 128, T], F32).ap()
    scr_xi = nc.dram_tensor("scr_xi", [NCH, 16, 128, T], BF16).ap()
    scr_z = nc.dram_tensor("scr_z", [NCH, 16, 128, T], BF16).ap()
    scr_cr = nc.dram_tensor("scr_cr", [NCH, 16, 1, T], F32).ap()
    cc1_in = nc.dram_tensor("cc1_in", [1024, 1024], F32).ap()
    cc1_out = nc.dram_tensor("cc1_out", [1024, 1024], F32).ap()
    cc2_in = nc.dram_tensor("cc2_in", [1024, 1024], F32).ap()
    cc2_out = nc.dram_tensor("cc2_out", [1024, 1024], F32).ap()
    cc3_in = nc.dram_tensor("cc3_in", [1024, 48], F32).ap()
    cc3_out = nc.dram_tensor("cc3_out", [1024, 48], F32).ap()
    ip16s = nc.dram_tensor("ip16s", [32, 128, 8, 128], BF16).ap()
    op16s = nc.dram_tensor("op16s", [8, 128, 16, 128], BF16).ap()
    woA16s = nc.dram_tensor("woA16s", [8, 128, 8, 128], BF16).ap()
    wl16s = nc.dram_tensor("wl16s", [16, 128, 8, 128], BF16).ap()
    wr16s = nc.dram_tensor("wr16s", [16, 128, 8, 128], BF16).ap()
    wd16s = nc.dram_tensor("wd16s", [16, 128, 16, 128], BF16).ap()
    wo216s = nc.dram_tensor("wo216s", [8, 128, 16, 128], BF16).ap()
    qkvb16s = nc.dram_tensor("qkvb16s", [25, 128, 8, 128], BF16).ap()
    up16s = nc.dram_tensor("up16s", [8, 16, 128, 8, 128], BF16).ap()
    dn16s = nc.dram_tensor("dn16s", [8, 8, 128, 8, 128], BF16).ap()

    with contextlib.ExitStack() as stack:
        ctx = Ctx(nc, stack)
        p = P(ctx)
        kn = Konst(ctx, p)
        G = MB()
        G.ownm = sb(ctx, "ownm", [128, 8], F32)
        G.pm1 = sb(ctx, "pm1", [128, 8], F32)
        G.pm0 = sb(ctx, "pm0", [128, 8], F32)
        G.hsel = sb(ctx, "hsel", [128, 8], F32)
        eyef = sb(ctx, "eyef", [128, 128], F32)
        eye16 = sb(ctx, "eye16", [128, 128], BF16)
        ones_col = sb(ctx, "ones_col", [128, 1], F32)
        ones_row = sb(ctx, "ones_row", [1, 128], F32)
        sp_all = sb(ctx, "sp_all", [128, 1024], F32)
        Htile = sb(ctx, "Htile", [128, 512], F32)
        bsel = sb(ctx, "bsel", [128, 8, 6], F32)
        stg2 = [sb(ctx, f"stg2_{i}", [128, 8, 128], F32) for i in range(2)]
        w16b = [sb(ctx, f"w16b_{i}", [128, 8, 128], BF16) for i in range(2)]

        wl0 = [p.dma("sync", G.ownm[:], ownm_in),
               p.dma("sync", G.pm1[:], pm1_in),
               p.dma("sync", G.hsel[:], hsel_in),
               p.dma("sync", eyef[:], eye_in)]
        wc0 = [p.comp("vector", lambda E: E.tensor_copy(out=eye16[:], in_=eyef[:]), deps=wl0),
               p.comp("vector", lambda E: E.tensor_scalar(
                   out=G.pm0[:], in0=G.pm1[:], scalar1=-1.0, scalar2=1.0,
                   op0=ALU.mult, op1=ALU.add), deps=wl0),
               p.comp("vector", lambda E: E.memset(ones_col[:], 1.0)),
               p.comp("vector", lambda E: E.memset(ones_row[:], 1.0))]
        kzero = kn.ap(0.0)

        def spcol(q, dt, n):
            return sp_all[:, q * 256 + dt * 16 + n: q * 256 + dt * 16 + n + 1]

        def emit_l0_mamba():
            mb = MB()
            mb.is_l0 = True
            mb.kzero = kzero
            mb.eye16 = eye16
            mb.ones_row = ones_row
            mb.scr_y, mb.scr_ds = scr_y, scr_ds
            mb.scr_xi, mb.scr_z, mb.scr_cr = scr_xi, scr_z, scr_cr
            mb.spS = lambda b, dt, n: spcol(b, dt, n)
            mb.spP = lambda b, dt, n: spcol(2 + b, dt, n)
            mb.Hcol = lambda b, dt, n: Htile[:, b * 256 + dt * 16 + n:
                                             b * 256 + dt * 16 + n + 1]
            with ctx.scope():
                bar0 = p.barrier()
                xpst = sb(ctx, "xpst", [128, 16, 96], F32)
                xp16 = sb(ctx, "xp16", [128, 16, 96], BF16)
                dtwst = sb(ctx, "dtwst", [64, 16, 128], F32)
                dtw16 = sb(ctx, "dtw16", [64, 16, 128], BF16)
                rms_c = sb(ctx, "rms_c", [128, 8], F32)
                convw = sb(ctx, "convw", [128, 16, 4], F32)
                convb = sb(ctx, "convb", [128, 16], F32)
                dtb = sb(ctx, "dtb", [128, 16], F32)
                acol = sb(ctx, "acol", [128, 16, 16], F32)
                mb.d_c = sb(ctx, "d_c", [128, 16], F32)
                mb.acol_fn = lambda dt, n: acol[:, dt, n:n + 1]
                wl = [p.dma("sync", dst, src, deps=bar0) for dst, src in [
                    (xpst[:], xp_in), (dtwst[:], dtw_in), (rms_c[:], rms_in),
                    (convw[:], convw_in), (convb[:], convb_in), (dtb[:], dtb_in),
                    (acol[:], alog_in), (mb.d_c[:], d_in)]]
                wc = [p.comp("scalar", lambda E: E.activation(out=xp16[:], in_=xpst[:], func=AF.Copy), deps=wl),
                      p.comp("scalar", lambda E: E.activation(out=dtw16[:], in_=dtwst[:], func=AF.Copy), deps=wl)]
                t_a = p.comp("scalar", lambda E: E.activation(
                    out=acol[:], in_=acol[:], func=AF.Exp), deps=wl)
                t_a = p.comp("vector", lambda E: E.tensor_scalar(
                    out=acol[:], in0=acol[:], scalar1=-1.0, scalar2=None,
                    op0=ALU.mult), deps=[t_a])
                wc.append(t_a)
                wc += wc0

                def _phaseA():
                    xtc = sb(ctx, "xtc", [128, 8, TE], F32)
                    xin16 = sb(ctx, "xin16", [128, 8, T], BF16)
                    xn2 = sb(ctx, "xn2", [128, 8, T], BF16)
                    xn2e = sb(ctx, "xn2e", [128, 8, 3], BF16)
                    xipre = sb(ctx, "xipre", [128, 16, TE], BF16)
                    xi2 = sb(ctx, "xi2", [128, 16, T], BF16)
                    z16 = sb(ctx, "z16", [128, 2, T], BF16)
                    delta16 = sb(ctx, "delta16", [128, 16, T], BF16)
                    bbc16 = sb(ctx, "bbc16", [128, 16, T], BF16)
                    cbc16 = sb(ctx, "cbc16", [128, 16, T], BF16)
                    dtT16 = sb(ctx, "dtT16", [64, T], BF16)
                    bc1 = sb(ctx, "bc1", [1, 2, T], F32)
                    mb.bc1 = bc1
                    dsc = sb(ctx, "dsc", [128, 1, T], F32)
                    mb.du16 = sb(ctx, "du16", [128, 2, T], BF16)
                    mb.u16 = sb(ctx, "u16", [128, 2, T], BF16)
                    mb.tmp16 = sb(ctx, "tmp16", [128, NR, T], BF16)
                    mb.dA = sb(ctx, "dA", [128, NR, T], F32)
                    mb.hS = sb(ctx, "hS", [128, 2, T], F32)
                    mb.Ds = sb(ctx, "Ds", [128, 2, T], F32)
                    mb.z16zero = sb(ctx, "zz", [128, T], BF16)
                    mb.dstate = sb(ctx, "dstate", [128, 16], F32)
                    mb.ysb = sb(ctx, "ysb", [128, 2, T], F32)
                    ln = LnTmp(ctx, T)
                    cacc = ln.t1
                    bcst = ln.t1
                    wkps = [ps(ctx, f"wkps{i}", [128, T], F32) for i in range(2)]
                    mb.ybank = ps(ctx, "ybank", [128, T], F32)
                    mb.bcbank = ps(ctx, "bcbank", [128, T], F32)
                    mb.g = 0
                    mb.last_ya = []
                    mb.ysb_rd = {0: [], 1: []}
                    mb.ds_rd = {0: [], 1: []}
                    mb.du_rd = {0: [], 1: []}
                    mb.da_rd = {s: [] for s in range(NR)}
                    mb.u_rd = {0: [], 1: []}
                    mb.h_rd = {0: [], 1: []}
                    mb.tmp_rd = {s: [] for s in range(NR)}
                    ms = MS2()
                    wc2 = wc + [
                        p.comp("vector", lambda E: E.memset(mb.z16zero[:], 0.0)),
                        p.comp("vector", lambda E: E.memset(xipre[:, :, 0:3], 0.0))]

                    a_eps = kn.row(EPS)
                    a_mh = kn.row(-0.5)
                    a_i1024 = kn.row(1.0 / 1024.0)
                    kd = kn.dep()

                    def rms_pass(src16, nt, Tr, dst16, deps0):
                        sq_t, last_q = [], []
                        for i in range(nt):
                            tq = p.comp("scalar", lambda E, i=i: E.activation(
                                out=ln.sq[:, 0:Tr], in_=src16[:, i, 0:Tr], func=AF.Square),
                                deps=deps0 + last_q + sq_t[-1:])
                            sq_t.append(tq)
                            last_q = [p.comp("tensor", lambda E, i=i: E.matmul(
                                out=ln.ps_q[:, 0:Tr], lhsT=ones_col[:], rhs=ln.sq[:, 0:Tr],
                                start=(i == 0), stop=(i == nt - 1)), deps=[tq])]
                        tms = p.comp("scalar", lambda E: E.activation(
                            out=ln.r_msq[:, 0:Tr], in_=ln.ps_q[:, 0:Tr], func=AF.Copy,
                            scale=a_i1024), deps=last_q + kd)
                        tl2 = p.comp("scalar", lambda E: E.activation(
                            out=ln.r_tmp[:, 0:Tr], in_=ln.r_msq[:, 0:Tr], func=AF.Ln,
                            bias=a_eps), deps=[tms])
                        tr2 = p.comp("scalar", lambda E: E.activation(
                            out=ln.r_rstd[:, 0:Tr], in_=ln.r_tmp[:, 0:Tr], func=AF.Exp,
                            scale=a_mh), deps=[tl2])
                        trb = p.comp("tensor", lambda E: E.matmul(
                            out=ln.ps_rb[:, 0:Tr], lhsT=ones_row[:], rhs=ln.r_rstd[:, 0:Tr],
                            start=True, stop=True), deps=[tr2])
                        outd = []
                        for i in range(nt):
                            outd.append(p.comp("vector", lambda E, i=i: E.scalar_tensor_tensor(
                                out=dst16[:, i, 0:Tr], in0=src16[:, i, 0:Tr],
                                scalar=rms_c[:, i:i + 1], in1=ln.ps_rb[:, 0:Tr],
                                op0=ALU.mult, op1=ALU.mult), deps=[trb] + deps0 + wl + outd[-1:]))
                        return outd

                    zrd = {0: [], 1: []}
                    xist = {}
                    mb.t1_rd = []
                    mb.cbc_rd = []
                    mb.xtc_free = []

                    def frontA(ch):
                        b, j = ch // 2, ch % 2
                        cb = b * SEG + j * T
                        eb = b * (SEG + 3) + j * T
                        bar = p.barrier() if ch == 0 else []
                        ldx = [p.dma("sync", xtc[:], x0e[:, eb:eb + TE].rearrange(
                            "(a p) t -> p a t", p=128), deps=bar + mb.xtc_free)]
                        lnd, _ = emit_ln(p, ln, xtc, 8, ones_col[:], ones_row[:],
                                         xin16, ldx + mb.t1_rd, kn, T_=T, col0=3)
                        rdeps = []
                        stt_ = []
                        for i in range(8):
                            tr_ = p.comp("vector", lambda E, i=i: E.tensor_add(
                                out=xtc[:, i, 3:3 + T], in0=xtc[:, i, 3:3 + T],
                                in1=xin16[:, i, :]), deps=lnd)
                            rdeps.append(tr_)
                            stt_.append(p.dma("gpsimd",
                                              sA[i * 128:(i + 1) * 128, cb:cb + T],
                                              xtc[:, i, 3:3 + T], deps=[tr_]))
                        xnd = rms_pass(xin16, 8, T, xn2, lnd)
                        xne = None
                        lnde = []
                        if j == 0:
                            lnde, _ = emit_ln(p, ln, xtc, 8, ones_col[:], ones_row[:],
                                              xn2e, ldx + xnd + rdeps + mb.t1_rd,
                                              kn, T_=3, col0=0)
                            xne = rms_pass(xn2e, 8, 3, xn2e, lnde)
                        mb.xtc_free = rdeps + stt_ + lnd[-1:] + lnde[-1:]

                        def ip_evac(mt, bank, deps, ch=ch):
                            if mt < 16:
                                return p.comp("vector", lambda E, mt=mt, bank=bank: E.tensor_copy(
                                    out=xipre[:, mt, 3:3 + T], in_=bank[:, :]), deps=deps)
                            zi = mt - 16
                            t = p.comp("scalar", lambda E, zi=zi, bank=bank: E.activation(
                                out=z16[:, zi % 2, :], in_=bank[:, :], func=AF.Copy),
                                deps=deps + zrd[zi % 2])
                            td = p.dma("scalar", scr_z[ch, zi], z16[:, zi % 2, :],
                                       deps=[t])
                            zrd[zi % 2] = [td]
                            return t

                        def ipe_evac(mt, bank, deps):
                            return p.comp("scalar", lambda E, mt=mt, bank=bank: E.activation(
                                out=xipre[:, mt, 0:3], in_=bank[:, 0:3], func=AF.Copy), deps=deps)

                        ipd = mm_stream2(p, ms, stg2, w16b, lambda mt: ip_in[mt],
                                         8, 32, lambda kt: xn2[:, kt, :], ip_evac,
                                         wkps, xnd, bar,
                                         ext=((lambda kt: xn2e[:, kt, :]), ipe_evac,
                                              3, xne, 16) if j == 0 else None,
                                         scr=ip16s, first=(ch == 0))
                        return dict(ch=ch, b=b, j=j, cb=cb, ipd=ipd)

                    def midA(fa):
                        ch = fa["ch"]
                        ipd = fa["ipd"]
                        conv_d = []
                        for dt in range(16):
                            cap = cacc[:, 0:T] if dt % 2 == 0 else dsc[:, 0, 0:T]
                            tc_ = p.comp("vector", lambda E, dt=dt, cap=cap: E.tensor_scalar(
                                out=cap, in0=xipre[:, dt, 0:T],
                                scalar1=convw[:, dt, 0:1], scalar2=convb[:, dt:dt + 1],
                                op0=ALU.mult, op1=ALU.add), deps=ipd + conv_d[-2:])
                            for k in range(1, 4):
                                tc_ = p.comp("vector", lambda E, dt=dt, k=k, cap=cap: E.scalar_tensor_tensor(
                                    out=cap, in0=xipre[:, dt, k:k + T],
                                    scalar=convw[:, dt, k:k + 1], in1=cap,
                                    op0=ALU.mult, op1=ALU.add), deps=[tc_])
                            tsi = p.comp("scalar", lambda E, dt=dt, cap=cap: E.activation(
                                out=xi2[:, dt, :], in_=cap, func=AF.Silu),
                                deps=[tc_] + xist.get(dt, []))
                            conv_d.append(tsi)
                            xist[dt] = [p.dma("scalar", scr_xi[ch, dt], xi2[:, dt, :],
                                              deps=[tsi])]
                        t_carry = p.comp("vector", lambda E: E.tensor_copy(
                            out=xipre[:, :, 0:3], in_=xipre[:, :, T:T + 3]), deps=conv_d)
                        t_dbc = None
                        for kt in range(16):
                            t_dbc = p.comp("tensor", lambda E, kt=kt: E.matmul(
                                out=wkps[0][0:96, :], lhsT=xp16[:, kt, :], rhs=xi2[:, kt, :],
                                start=(kt == 0), stop=(kt == 15)),
                                deps=conv_d + wc2 + ipd + (ms.bank_rd[0] if kt == 0 else []))
                        t_dt = p.comp("vector", lambda E: E.tensor_copy(
                            out=dtT16[:, :], in_=wkps[0][0:64, :]), deps=[t_dbc])
                        t_bc = p.comp("vector", lambda E: E.tensor_copy(
                            out=bcst[64:96, 0:T], in_=wkps[0][64:96, :]), deps=[t_dbc])
                        t_crs = p.dma("sync", scr_cr[ch].rearrange("n o t -> (n o) t"),
                                      bcst[80:96, 0:T], deps=[t_bc])
                        del_d = []
                        bank_rd = {0: [t_bc, t_crs], 1: list(ms.bank_rd[1])}
                        dsc_rd = {0: []}
                        for mt in range(16):
                            bnk = wkps[mt % 2]
                            tm = p.comp("tensor", lambda E, mt=mt, bnk=bnk: E.matmul(
                                out=bnk[:, :], lhsT=dtw16[:, mt, :], rhs=dtT16[:, :],
                                start=True, stop=True), deps=[t_dt] + bank_rd[mt % 2])
                            ta1 = p.comp("scalar", lambda E, mt=mt, bnk=bnk: E.activation(
                                out=ln.sq[:, 0:T], in_=bnk[:, :], func=AF.Abs,
                                bias=dtb[:, mt:mt + 1]), deps=[tm] + del_d[-1:])
                            ta2 = p.comp("scalar", lambda E: E.activation(
                                out=ln.sq[:, 0:T], in_=ln.sq[:, 0:T], func=AF.Exp,
                                scale=-1.0), deps=[ta1])
                            ta3 = p.comp("scalar", lambda E: E.activation(
                                out=ln.sq[:, 0:T], in_=ln.sq[:, 0:T], func=AF.Ln,
                                bias=1.0), deps=[ta2])
                            ta4 = p.comp("scalar", lambda E, mt=mt, bnk=bnk: E.activation(
                                out=dsc[:, 0, :], in_=bnk[:, :], func=AF.Relu,
                                bias=dtb[:, mt:mt + 1]), deps=[ta3] + dsc_rd[0])
                            bank_rd[mt % 2] = [ta4]
                            t_dl = p.comp("vector", lambda E, mt=mt: E.tensor_add(
                                out=delta16[:, mt, :], in0=ln.sq[:, 0:T],
                                in1=dsc[:, 0, :]), deps=[ta4])
                            dsc_rd[0] = [t_dl]
                            del_d.append(t_dl)
                        ms.bank_rd[0] = bank_rd[0]
                        ms.bank_rd[1] = bank_rd[1]
                        bc_d = []
                        bmm = []
                        tmv = None
                        for n in range(32):
                            tmv = p.dma("sync", bc1[0:1, n % 2, 0:T],
                                        bcst[64 + n:65 + n, 0:T],
                                        deps=[t_bc] + bmm[-2:-1] + del_d[-1:])
                            tb = p.comp("tensor", lambda E, n=n: E.matmul(
                                out=mb.bcbank[:, :], lhsT=ones_row[:],
                                rhs=bc1[0:1, n % 2, 0:T], start=True, stop=True),
                                deps=[tmv] + bc_d[-1:] + (del_d[-1:] if n == 0 else []))
                            bmm.append(tb)
                            dst = bbc16 if n < 16 else cbc16
                            bc_d.append(p.comp("scalar", lambda E, n=n, dst=dst: E.activation(
                                out=dst[:, n % 16, :], in_=mb.bcbank[:, :], func=AF.Copy),
                                deps=[tb] + mb.cbc_rd))
                        mb.t1_rd = [t_crs, tmv]
                        return dict(ch=ch, del_d=del_d, t_carry=t_carry,
                                    bc_d=bc_d, conv_d=conv_d)

                    fas = [None] * NCH
                    mds = [None] * NCH
                    fas[0] = frontA(0)
                    mds[0] = midA(fas[0])
                    for ch in range(NCH):
                        if ch + 1 < NCH:
                            fas[ch + 1] = frontA(ch + 1)
                        md = mds[ch]
                        scan_A(p, ctx, mb, ch, delta16, xi2, bbc16, cbc16,
                               md["del_d"] + [md["t_carry"]], md["bc_d"],
                               md["conv_d"], wc2)
                        if ch + 1 < NCH:
                            mds[ch + 1] = midA(fas[ch + 1])
                with ctx.scope():
                    _phaseA()
                barx = p.barrier()
                tc1 = emit_gather(p, ctx, G, sp_all[:, :], cc1_in, cc1_out, 1024, barx)
                wcH = emit_fold(p, ctx, G, cc1_out, Htile, tc1)

                def _phaseB():
                    xtc = sb(ctx, "xtcB", [128, 8, T], F32)
                    cbc16 = sb(ctx, "cbcB", [128, 16, T], BF16)
                    y2 = sb(ctx, "y2B", [128, 16, T], BF16)
                    mb.dsb = sb(ctx, "dsb", [128, 2, T], F32)
                    mb.yacc = sb(ctx, "yacc", [128, 2, T], F32)
                    mb.xib = sb(ctx, "xib", [128, 2, T], BF16)
                    mb.zb = sb(ctx, "zbB", [128, 2, T], BF16)
                    mb.szb = sb(ctx, "szb", [128, 2, T], F32)
                    mb.dec = sb(ctx, "dec", [128, NR, T], F32)
                    mb.tmpc2 = sb(ctx, "tmpc2", [128, 4, T], F32)
                    bc1 = sb(ctx, "bc1B", [1, 2, T], F32)
                    mb.bc1 = bc1
                    wkps = [ps(ctx, f"wkpsB{i}", [128, T], F32) for i in range(2)]
                    mb.bcbank = ps(ctx, "bcbankB", [128, T], F32)
                    mb.g = 0
                    mb.dsb_rd = {0: [], 1: []}
                    mb.yacc_rd = {0: [], 1: []}
                    mb.xib_rd = {0: [], 1: []}
                    mb.zb_rd = {0: [], 1: []}
                    mb.szb_rd = {0: [], 1: []}
                    mb.dec_rd = {s: [] for s in range(NR)}
                    mb.tmpc_rd = {i: [] for i in range(4)}
                    ms = MS2()
                    stgB = [sb(ctx, f"stgB_{i}", [128, 16, 128], F32) for i in range(2)]
                    w16B = [sb(ctx, f"w16B_{i}", [128, 16, 128], BF16) for i in range(2)]
                    for ch in range(NCH):
                        b, j = ch // 2, ch % 2
                        cb = b * SEG + j * T
                        bar = p.barrier()
                        bc_d = emit_cbc(p, ctx, mb, ch, cbc16, bar)
                        y2d = corr_B(p, ctx, mb, ch, cbc16, y2, bar, wcH, bc_d)
                        ldx2 = [p.dma("sync", xtc[:], sA[:, cb:cb + T].rearrange(
                            "(a p) t -> p a t", p=128), deps=bar)]

                        def op_evac(mt, bank, deps):
                            t = p.comp("vector", lambda E, mt=mt, bank=bank: E.tensor_add(
                                out=xtc[:, mt, :], in0=bank[:, :], in1=xtc[:, mt, :]),
                                deps=deps + ldx2)
                            p.dma("gpsimd", sA[mt * 128:(mt + 1) * 128, cb:cb + T],
                                  xtc[:, mt, :], deps=[t])
                            return t

                        mm_stream2(p, ms, stgB, w16B, lambda mt: op_in[mt],
                                   16, 8, lambda kt: y2[:, kt, :], op_evac, wkps,
                                   y2d, bar, scr=op16s, first=(ch == 0))
                with ctx.scope():
                    _phaseB()

        def emit_l0ffn(src, dst):
            with ctx.scope():
                bar0 = p.barrier()
                winf = sb(ctx, "winf", [128, 8, 64], F32)
                win16 = sb(ctx, "win16", [128, 8, 64], BF16)
                woutf = sb(ctx, "woutf", [64, 8, 128], F32)
                wout16 = sb(ctx, "wout16", [64, 8, 128], BF16)
                bin_c = sb(ctx, "bin_c", [64, 1], F32)
                w1 = sb(ctx, "w1", [64, 16], F32)
                b1 = sb(ctx, "b1", [64, 16], F32)
                w2 = sb(ctx, "w2", [64, 16], F32)
                b2 = sb(ctx, "b2", [64, 1], F32)
                bout_c = sb(ctx, "bout", [128, 8], F32)
                xtc = sb(ctx, "xtc", [128, 8, T], F32)
                xn16 = sb(ctx, "xn16", [128, 8, T], BF16)
                s_sb = sb(ctx, "s_sb", [64, T], F32)
                h_sb = sb(ctx, "h_sb", [64, 2, T], F32)
                g_sb = sb(ctx, "g_sb", [64, T], F32)
                g16 = sb(ctx, "g16", [64, T], BF16)
                ln = LnTmp(ctx, T)
                s_ps = ps(ctx, "s_ps", [64, T], F32)
                out_ps = [ps(ctx, f"out_ps{i}", [128, T], F32) for i in range(2)]
                wl = [
                    p.dma("sync", winf[:], win_in, deps=bar0),
                    p.dma("sync", woutf[:], wout_in, deps=bar0),
                    p.dma("sync", bin_c[:], bin_in, deps=bar0),
                    p.dma("sync", w1[:], w1_in, deps=bar0),
                    p.dma("sync", b1[:], b1_in, deps=bar0),
                    p.dma("sync", w2[:], w2_in, deps=bar0),
                    p.dma("sync", b2[:], b2_in, deps=bar0),
                    p.dma("sync", bout_c[:], bout_in, deps=bar0),
                ]
                wc = [
                    p.comp("scalar", lambda E: E.activation(out=win16[:], in_=winf[:], func=AF.Copy), deps=wl),
                    p.comp("scalar", lambda E: E.activation(out=wout16[:], in_=woutf[:], func=AF.Copy), deps=wl),
                ] + wc0
                for ch in range(NCH):
                    c0 = ch * T
                    bar = p.barrier()
                    ldx = [p.dma("sync", xtc[:], src[:, c0:c0 + T].rearrange(
                        "(a p) t -> p a t", p=128), deps=bar)]
                    lnd, _ = emit_ln(p, ln, xtc, 8, ones_col[:], ones_row[:], xn16,
                                     ldx, kn)
                    t_mm = None
                    for k in range(8):
                        t_mm = p.comp("tensor", lambda E, k=k: E.matmul(
                            out=s_ps[:, :], lhsT=win16[:, k, :], rhs=xn16[:, k, :],
                            start=(k == 0), stop=(k == 7)), deps=lnd + wc)
                    t_s = p.comp("vector", lambda E: E.tensor_scalar(
                        out=s_sb[:], in0=s_ps[:, :], scalar1=bin_c[:, 0:1],
                        scalar2=None, op0=ALU.add), deps=[t_mm])
                    t_g = None
                    tg_hist = []
                    for i in range(16):
                        t_h = p.comp("scalar", lambda E, i=i: E.activation(
                            out=h_sb[:, i % 2, :], in_=s_sb[:], func=AF.Relu,
                            scale=w1[:, i:i + 1], bias=b1[:, i:i + 1]),
                            deps=[t_s] + tg_hist[-2:-1])
                        if i == 0:
                            t_g = p.comp("vector", lambda E: E.tensor_scalar(
                                out=g_sb[:], in0=h_sb[:, 0, :], scalar1=w2[:, 0:1],
                                op0=ALU.mult, scalar2=b2[:, 0:1], op1=ALU.add),
                                deps=[t_h])
                        else:
                            t_g = p.comp("vector", lambda E, i=i: E.scalar_tensor_tensor(
                                out=g_sb[:], in0=h_sb[:, i % 2, :], scalar=w2[:, i:i + 1],
                                in1=g_sb[:], op0=ALU.mult, op1=ALU.add), deps=[t_h, t_g])
                        tg_hist.append(t_g)
                    t_g16 = p.comp("vector", lambda E: E.tensor_copy(
                        out=g16[:], in_=g_sb[:]), deps=[t_g])
                    ev_toks = []
                    for dt in range(8):
                        t_o = p.comp("tensor", lambda E, dt=dt: E.matmul(
                            out=out_ps[dt % 2][:, :], lhsT=wout16[:, dt, :], rhs=g16[:],
                            start=True, stop=True), deps=[t_g16] + ev_toks[-2:-1])
                        t_ev = p.comp("vector", lambda E, dt=dt: E.scalar_tensor_tensor(
                            out=xtc[:, dt, :], in0=out_ps[dt % 2][:, :],
                            scalar=bout_c[:, dt:dt + 1], in1=xtc[:, dt, :],
                            op0=ALU.add, op1=ALU.add), deps=[t_o])
                        ev_toks.append(t_ev)
                        p.dma("gpsimd", dst[dt * 128:(dt + 1) * 128, c0:c0 + T],
                              xtc[:, dt, :], deps=[t_ev])

        def emit_l0attnb(src, dst):
            with ctx.scope():
                bar0 = p.barrier()
                wstage = sb(ctx, "wstage", [128, 8, D], F32)
                wq = sb(ctx, "wq16", [128, 8, D], BF16)
                wk = sb(ctx, "wk16", [128, 8, D], BF16)
                wv = sb(ctx, "wv16", [128, 8, D], BF16)
                bq = sb(ctx, "bq", [128, 8], F32)
                bk = sb(ctx, "bk", [128, 8], F32)
                bvf = sb(ctx, "bvf", [1, D], F32)
                bv = sb(ctx, "bv16", [1, D], BF16)
                ones_r16 = sb(ctx, "ones_r16", [1, 128], BF16)
                xtc = sb(ctx, "xtc", [128, 8, T], F32)
                xn16 = sb(ctx, "xn16", [128, 8, T], BF16)
                qT = sb(ctx, "qT", [128, 8, T], BF16)
                kT = sb(ctx, "kT", [128, 8, T], BF16)
                v16 = sb(ctx, "v16", [128, 4, D], BF16)
                pexp = sb(ctx, "pexp", [128, T], F32)
                p16 = sb(ctx, "p16", [128, T], BF16)
                pT16 = sb(ctx, "pT16", [128, 4, T], BF16)
                nmax = sb(ctx, "nmax", [128, 1], F32)
                ssum = sb(ctx, "ssum", [128, 1], F32)
                rsum = sb(ctx, "rsum", [128, 1], F32)
                ln = LnTmp(ctx, T)
                wkps = [ps(ctx, f"wkps{i}", [128, T], F32) for i in range(3)]
                tps = ps(ctx, "tps", [128, 128], BF16)
                wl = [p.dma("sync", bq[:], bq_in, deps=bar0),
                      p.dma("sync", bk[:], bk_in, deps=bar0),
                      p.dma("sync", bvf[:], bv_in, deps=bar0)]
                wc = [p.comp("scalar", lambda E: E.activation(out=bv[:], in_=bvf[:], func=AF.Copy), deps=wl),
                      p.comp("vector", lambda E: E.memset(ones_r16[:], 1.0))] + wc0
                prev_cv = []
                for win, wdst in [(wq_in, wq), (wk_in, wk), (wv_in, wv)]:
                    tl = p.dma("sync", wstage[:], win, deps=prev_cv + bar0)
                    prev_cv = [p.comp("scalar", lambda E, wdst=wdst: E.activation(
                        out=wdst[:], in_=wstage[:], func=AF.Copy), deps=[tl])]
                    wc += prev_cv
                for ch in range(NCH):
                    c0 = ch * T
                    bar = p.barrier()
                    ldx = [p.dma("sync", xtc[:], src[:, c0:c0 + T].rearrange(
                        "(a p) t -> p a t", p=128), deps=bar)]
                    lnd, _ = emit_ln(p, ln, xtc, 8, ones_col[:], ones_row[:], xn16,
                                     ldx, kn)
                    qk_toks = []
                    ev_hist = []
                    for which, (wt, bc, dstt) in enumerate([(wq, bq, qT), (wk, bk, kT)]):
                        for et in range(8):
                            bnk = wkps[(which * 8 + et) % 2]
                            tm = None
                            for k in range(8):
                                tm = p.comp("tensor", lambda E, wt=wt, et=et, k=k, bnk=bnk: E.matmul(
                                    out=bnk[:, :], lhsT=wt[:, k, et * 128:(et + 1) * 128],
                                    rhs=xn16[:, k, :], start=(k == 0), stop=(k == 7)),
                                    deps=lnd + wc + ev_hist[-2:-1])
                            tev = p.comp("vector", lambda E, dstt=dstt, et=et, bc=bc, bnk=bnk: E.tensor_scalar(
                                out=dstt[:, et, :], in0=bnk[:, :], scalar1=bc[:, et:et + 1],
                                scalar2=None, op0=ALU.add), deps=[tm])
                            ev_hist.append(tev)
                            qk_toks.append(tev)
                    v_toks = []
                    for ct in range(4):
                        for eh in range(2):
                            bnk = wkps[(ct * 2 + eh) % 2]
                            tm = None
                            for k in range(8):
                                tm = p.comp("tensor", lambda E, ct=ct, eh=eh, k=k, bnk=bnk: E.matmul(
                                    out=bnk[:, :], lhsT=xn16[:, k, ct * 128:(ct + 1) * 128],
                                    rhs=wv[:, k, eh * 512:(eh + 1) * 512], start=(k == 0),
                                    stop=False), deps=lnd + wc + ev_hist[-2:-1])
                            tm = p.comp("tensor", lambda E, eh=eh, bnk=bnk: E.matmul(
                                out=bnk[:, :], lhsT=ones_r16[:, :],
                                rhs=bv[:, eh * 512:(eh + 1) * 512],
                                start=False, stop=True), deps=[])
                            tev = p.comp("scalar", lambda E, ct=ct, eh=eh, bnk=bnk: E.activation(
                                out=v16[:, ct, eh * 512:(eh + 1) * 512], in_=bnk[:, :],
                                func=AF.Copy), deps=[tm])
                            ev_hist.append(tev)
                            v_toks.append(tev)
                    ot_deps = []
                    tr_hist = []
                    sb_rd = {0: v_toks[-2:-1], 1: v_toks[-1:]}
                    exp_prev = []
                    for cqt in range(4):
                        sbnk = wkps[cqt % 2]
                        tm = None
                        for et in range(8):
                            tm = p.comp("tensor", lambda E, cqt=cqt, et=et, sbnk=sbnk: E.matmul(
                                out=sbnk[:, :], lhsT=qT[:, et, cqt * 128:(cqt + 1) * 128],
                                rhs=kT[:, et, :], start=(et == 0), stop=(et == 7)),
                                deps=qk_toks + sb_rd[cqt % 2])
                        t_max = p.comp("vector", lambda E, sbnk=sbnk: E.reduce_max(
                            out=nmax[:], in_=sbnk[:, :], axis=mybir.AxisListType.X,
                            negate=True), deps=[tm] + exp_prev)
                        t_exp = p.comp("scalar", lambda E, sbnk=sbnk: E.activation(
                            out=pexp[:], in_=sbnk[:, :], func=AF.Exp, bias=nmax[:, 0:1],
                            accum_out=ssum[:]), deps=[t_max])
                        sb_rd[cqt % 2] = [t_exp]
                        exp_prev = [t_exp]
                        t_rec = p.comp("vector", lambda E: E.reciprocal(
                            out=rsum[:], in_=ssum[:]), deps=[t_exp])
                        t_p16 = p.comp("vector", lambda E: E.tensor_scalar(
                            out=p16[:], in0=pexp[:], scalar1=rsum[:, 0:1], scalar2=None,
                            op0=ALU.mult), deps=[t_rec] + tr_hist[-1:])
                        for ckt in range(4):
                            ttr = p.comp("tensor", lambda E, ckt=ckt: E.transpose(
                                out=tps[:, :], in_=p16[:, ckt * 128:(ckt + 1) * 128],
                                identity=eye16[:]), deps=[t_p16] + tr_hist[-1:])
                            tev = p.comp("vector", lambda E, ckt=ckt, cqt=cqt: E.tensor_copy(
                                out=pT16[:, ckt, cqt * 128:(cqt + 1) * 128], in_=tps[:, :]),
                                deps=[ttr])
                            tr_hist.append(tev)
                            ot_deps.append(tev)
                    oev = []
                    for et in range(8):
                        bnk = wkps[et % 2]
                        tm = None
                        for ckt in range(4):
                            tm = p.comp("tensor", lambda E, et=et, ckt=ckt, bnk=bnk: E.matmul(
                                out=bnk[:, :], lhsT=v16[:, ckt, et * 128:(et + 1) * 128],
                                rhs=pT16[:, ckt, :], start=(ckt == 0), stop=(ckt == 3)),
                                deps=ot_deps + v_toks + oev[-2:-1])
                        tev = p.comp("vector", lambda E, et=et, bnk=bnk: E.tensor_add(
                            out=xtc[:, et, :], in0=bnk[:, :], in1=xtc[:, et, :]),
                            deps=[tm])
                        oev.append(tev)
                        p.dma("gpsimd", dst[et * 128:(et + 1) * 128, c0:c0 + T],
                              xtc[:, et, :], deps=[tev])

        def emit_l1attna(src, dst):
            with ctx.scope():
                bar0 = p.barrier()
                lng = sb(ctx, "lng", [128, 8], F32)
                lnb = sb(ctx, "lnb", [128, 8], F32)
                seg = sb(ctx, "seg", [128, 8, 16], F32)
                xtc = sb(ctx, "xtc", [128, 8, T], F32)
                xn16 = sb(ctx, "xn16", [128, 8, 2, T], BF16)
                qT = sb(ctx, "qT", [128, 8, 2, T], BF16)
                kT = sb(ctx, "kT", [128, 8, 2, T], BF16)
                vT = sb(ctx, "vT", [128, 8, 2, T], BF16)
                o16 = sb(ctx, "o16", [128, 8, 2, T], BF16)
                eq = sb(ctx, "eq", [128, 8, T], F32)
                pr2 = sb(ctx, "pr2", [128, 8, T], F32)
                yv = sb(ctx, "yv", [128, 8, T], F32)
                betaR = sb(ctx, "betaR", [16, 2, T], F32)
                rows = sb(ctx, "rows", [16, 8, T], F32)
                bc1 = sb(ctx, "bc1", [1, 2, T], F32)
                ln = LnTmp(ctx, T)
                wkps = [ps(ctx, f"wkps{i}", [128, T], F32) for i in range(2)]
                segb = ps(ctx, "segb", [16, T], F32)
                cbank = ps(ctx, "cbank", [128, T], F32)
                wl = [p.dma("sync", lng[:], lngA_in, deps=bar0),
                      p.dma("sync", lnb[:], lnbA_in, deps=bar0)]
                wc = [p.comp("vector", lambda E: E.memset(seg[:], 0.0))]
                for kt in range(8):
                    wc.append(p.comp("vector", lambda E, kt=kt: E.memset(
                        seg[0:64, kt, 2 * kt:2 * kt + 1], 1.0), deps=wc[0:1]))
                    wc.append(p.comp("vector", lambda E, kt=kt: E.memset(
                        seg[64:128, kt, 2 * kt + 1:2 * kt + 2], 1.0), deps=wc[0:1]))
                ones_c = kn.ap(1.0)
                prev_s_done = []
                p._bc_mm = []
                for pc in range(2):
                    bar = p.barrier()
                    lnd = []
                    for s in range(2):
                        ldx = p.dma("sync", xtc[:],
                                    src[:, s * 1024 + pc * T: s * 1024 + pc * T + T]
                                    .rearrange("(a p) t -> p a t", p=128),
                                    deps=bar + lnd + prev_s_done)
                        d, _ = emit_ln(p, ln, xtc, 8, ones_c, ones_row[:],
                                       xn16[:, :, s, :], [ldx] + prev_s_done,
                                       kn, T_=T, col0=0)
                        lnd += d
                    msA = MS2()
                    qkv_d = []
                    beta_d = []
                    if not hasattr(p, "_ms2cv"):
                        p._ms2cv = []
                        p._ms2rd = {0: [], 1: []}
                    if not hasattr(p, "_ms2st"):
                        p._ms2st = {}
                    for mt in range(25):
                        g = len(p._ms2cv)
                        q = g % 2
                        sl = g % 2
                        skey = (id(qkvb16s), mt)
                        if pc > 0:
                            tcv = p.dma(["sync", "scalar"][q], w16b[sl][:, 0:8, :],
                                        qkvb16s[mt],
                                        deps=p._ms2rd[sl] + p._ms2st[skey] +
                                        ([p._ms2cv[-2]] if len(p._ms2cv) >= 2 else bar))
                        else:
                            tld = p.dma(["sync", "scalar"][q], stg2[sl][:, 0:8, :],
                                        wqkvb_in[mt],
                                        deps=([p._ms2cv[-2]] if len(p._ms2cv) >= 2 else bar))
                            tcv = p.comp("vector", lambda E, sl=sl: E.tensor_copy(
                                out=w16b[sl][:, 0:8, :], in_=stg2[sl][:, 0:8, :]),
                                deps=[tld] + p._ms2rd[sl])
                            tst = p.dma(["sync", "scalar"][1 - q], qkvb16s[mt],
                                        w16b[sl][:, 0:8, :], deps=[tcv])
                            p._ms2st[skey] = [tst]
                            p._ms2rd[sl] = p._ms2rd[sl] + [tst]
                        p._ms2cv.append(tcv)
                        tmm = {0: None, 1: None}
                        for kt in range(8):
                            for s in range(2):
                                deps = [tcv]
                                if kt == 0:
                                    deps = deps + lnd + msA.bank_rd[s]
                                tmm[s] = p.comp("tensor", lambda E, sl=sl, kt=kt, s=s: E.matmul(
                                    out=wkps[s][:, :], lhsT=w16b[sl][:, kt, :],
                                    rhs=xn16[:, kt, s, :], start=(kt == 0), stop=(kt == 7)),
                                    deps=deps)
                        p._ms2rd[sl] = [tmm[0], tmm[1]]
                        for s in range(2):
                            if mt < 8:
                                tev = p.comp("vector", lambda E, mt=mt, s=s: E.tensor_copy(
                                    out=qT[:, mt, s, :], in_=wkps[s][:, :]), deps=[tmm[s]])
                            elif mt < 16:
                                tev = p.comp("vector", lambda E, mt=mt, s=s: E.tensor_copy(
                                    out=kT[:, mt - 8, s, :], in_=wkps[s][:, :]), deps=[tmm[s]])
                            elif mt < 24:
                                tev = p.comp("vector", lambda E, mt=mt, s=s: E.tensor_copy(
                                    out=vT[:, mt - 16, s, :], in_=wkps[s][:, :]), deps=[tmm[s]])
                            else:
                                tev = p.comp("vector", lambda E, s=s: E.tensor_copy(
                                    out=betaR[:, s, :], in_=wkps[s][0:16, :]), deps=[tmm[s]])
                                beta_d.append(tev)
                            msA.bank_rd[s] = [tev]
                            qkv_d.append(tev)
                    norm_d = []
                    seg_rd = []
                    for (srcq, ridx) in ((qT, 0), (kT, 1)):
                        for s in range(2):
                            t_m = p.comp("vector", lambda E, srcq=srcq, s=s: E.tensor_scalar(
                                out=eq[:], in0=srcq[:, :, s, :], scalar1=0.0, scalar2=None,
                                op0=ALU.min), deps=qkv_d + norm_d[-1:])
                            t_e = p.comp("scalar", lambda E: E.activation(
                                out=eq[:], in_=eq[:], func=AF.Exp), deps=[t_m])
                            t_r = p.comp("vector", lambda E, srcq=srcq, s=s: E.tensor_relu(
                                out=pr2[:], in_=srcq[:, :, s, :]), deps=qkv_d + norm_d[-1:])
                            t_a = p.comp("vector", lambda E: E.tensor_add(
                                out=eq[:], in0=eq[:], in1=pr2[:]), deps=[t_e, t_r])
                            t_sm = None
                            for kt in range(8):
                                t_sm = p.comp("tensor", lambda E, kt=kt: E.matmul(
                                    out=segb[:, :], lhsT=seg[:, kt, :], rhs=eq[:, kt, :],
                                    start=(kt == 0), stop=(kt == 7)),
                                    deps=[t_a] + wc + seg_rd)
                            ri = ridx * 2 + s
                            t_rc = p.comp("vector", lambda E, ri=ri: E.reciprocal(
                                out=rows[:, ri, :], in_=segb[:, :]), deps=[t_sm])
                            seg_rd = [t_rc]
                            bcm = p._bc_mm
                            for kt in range(8):
                                for hh in range(2):
                                    h = 2 * kt + hh
                                    tmv = p.dma("gpsimd", bc1[0:1, h % 2, 0:T],
                                                rows[h:h + 1, ri, :], deps=[t_rc] + bcm[-2:])
                                    bcm.append(p.comp("tensor", lambda E, hh=hh, h=h: E.matmul(
                                        out=cbank[64 * hh:64 * (hh + 1), :],
                                        lhsT=ones_row[0:1, 0:64], rhs=bc1[0:1, h % 2, 0:T],
                                        start=True, stop=True),
                                        deps=[tmv] + norm_d[-1:]))
                                norm_d.append(p.comp("vector", lambda E, srcq=srcq, s=s, kt=kt: E.tensor_mul(
                                    out=srcq[:, kt, s, :], in0=eq[:, kt, :], in1=cbank[:, :]),
                                    deps=bcm[-2:] + [t_a]))
                    dot_d = []
                    for di, (a_, sa, b_, sbi) in enumerate((
                            (qT, 0, kT, 0), (kT, 0, kT, 1), (qT, 1, kT, 0), (qT, 1, kT, 1))):
                        t_p = p.comp("vector", lambda E, a_=a_, sa=sa, b_=b_, sbi=sbi: E.tensor_mul(
                            out=pr2[:], in0=a_[:, :, sa, :], in1=b_[:, :, sbi, :]),
                            deps=norm_d + dot_d[-1:])
                        t_sm = None
                        for kt in range(8):
                            t_sm = p.comp("tensor", lambda E, kt=kt: E.matmul(
                                out=segb[:, :], lhsT=seg[:, kt, :], rhs=pr2[:, kt, :],
                                start=(kt == 0), stop=(kt == 7)),
                                deps=[t_p] + seg_rd)
                        t_ev = p.comp("vector", lambda E, di=di: E.tensor_copy(
                            out=rows[:, 4 + di, :], in_=segb[:, :]), deps=[t_sm])
                        seg_rd = [t_ev]
                        dot_d.append(t_ev)
                    t_b0 = p.comp("scalar", lambda E: E.activation(
                        out=rows[:, 3, :], in_=betaR[:, 0, :], func=AF.Sigmoid),
                        deps=beta_d + norm_d)
                    t_b1 = p.comp("scalar", lambda E: E.activation(
                        out=betaR[:, 1, :], in_=betaR[:, 1, :], func=AF.Sigmoid),
                        deps=beta_d + [t_b0])
                    t_c1 = p.comp("vector", lambda E: E.tensor_mul(
                        out=rows[:, 0, :], in0=rows[:, 4, :], in1=rows[:, 3, :]),
                        deps=[t_b0] + dot_d + norm_d)
                    t_c1 = p.comp("vector", lambda E: E.tensor_scalar(
                        out=rows[:, 0, :], in0=rows[:, 0, :], scalar1=0.125,
                        scalar2=None, op0=ALU.mult), deps=[t_c1])
                    t_cb = p.comp("vector", lambda E: E.tensor_mul(
                        out=rows[:, 2, :], in0=rows[:, 7, :], in1=betaR[:, 1, :]),
                        deps=[t_b1] + dot_d)
                    t_t = p.comp("vector", lambda E: E.tensor_mul(
                        out=rows[:, 1, :], in0=rows[:, 2, :], in1=rows[:, 5, :]),
                        deps=[t_cb])
                    t_t = p.comp("vector", lambda E: E.tensor_sub(
                        out=rows[:, 1, :], in0=rows[:, 6, :], in1=rows[:, 1, :]), deps=[t_t])
                    t_t = p.comp("vector", lambda E: E.tensor_mul(
                        out=rows[:, 1, :], in0=rows[:, 1, :], in1=rows[:, 3, :]),
                        deps=[t_t, t_b0])
                    t_ca = p.comp("vector", lambda E: E.tensor_scalar(
                        out=rows[:, 1, :], in0=rows[:, 1, :], scalar1=0.125,
                        scalar2=None, op0=ALU.mult), deps=[t_t])
                    t_cb = p.comp("vector", lambda E: E.tensor_scalar(
                        out=rows[:, 2, :], in0=rows[:, 2, :], scalar1=0.125,
                        scalar2=None, op0=ALU.mult), deps=[t_cb])
                    coef_d = [t_c1, t_ca, t_cb]
                    out_d = []
                    bcm = p._bc_mm
                    for (ri, s, vsrc, acc) in ((0, 0, 0, False), (1, 1, 0, False), (2, 1, 1, True)):
                        for kt in range(8):
                            for hh in range(2):
                                h = 2 * kt + hh
                                tmv = p.dma("gpsimd", bc1[0:1, h % 2, 0:T],
                                            rows[h:h + 1, ri, :], deps=coef_d + bcm[-2:])
                                bcm.append(p.comp("tensor", lambda E, hh=hh: E.matmul(
                                    out=cbank[64 * hh:64 * (hh + 1), :],
                                    lhsT=ones_row[0:1, 0:64], rhs=bc1[0:1, h % 2, 0:T],
                                    start=True, stop=True),
                                    deps=[tmv] + out_d[-2:]))
                            if not acc:
                                out_d.append(p.comp("vector", lambda E, s=s, kt=kt, vsrc=vsrc: E.tensor_mul(
                                    out=o16[:, kt, s, :] if s == 0 else yv[:, kt, :],
                                    in0=vT[:, kt, vsrc, :], in1=cbank[:, :]),
                                    deps=bcm[-2:] + norm_d + dot_d))
                            else:
                                t_m2 = p.comp("vector", lambda E, kt=kt, vsrc=vsrc: E.tensor_mul(
                                    out=pr2[:, kt, :], in0=vT[:, kt, vsrc, :], in1=cbank[:, :]),
                                    deps=bcm[-2:] + dot_d)
                                out_d.append(p.comp("vector", lambda E, kt=kt: E.tensor_add(
                                    out=o16[:, kt, 1, :], in0=yv[:, kt, :], in1=pr2[:, kt, :]),
                                    deps=[t_m2] + out_d))
                    for s in range(2):
                        def wo_evac(mt, bank, deps, s=s):
                            return p.comp("vector", lambda E, mt=mt, bank=bank, s=s: E.tensor_add(
                                out=yv[:, mt, :], in0=bank[:, :], in1=xn16[:, mt, s, :]),
                                deps=deps + out_d + prev_s_done)
                        yd = mm_stream2(p, msA, stg2, w16b, lambda mt: woA_in[mt],
                                        8, 8, lambda kt, s=s: o16[:, kt, s, :],
                                        wo_evac, wkps, out_d + prev_s_done, bar,
                                        scr=woA16s, first=(pc == 0 and s == 0))
                        sq_t, last_q, last_s = [], [], []
                        for i in range(8):
                            tq = p.comp("scalar", lambda E, i=i: E.activation(
                                out=ln.sq[:, 0:T], in_=yv[:, i, :], func=AF.Square),
                                deps=yd + last_q + sq_t[-1:])
                            sq_t.append(tq)
                            last_q = [p.comp("tensor", lambda E, i=i: E.matmul(
                                out=ln.ps_q[:, 0:T], lhsT=kn.ap(1.0), rhs=ln.sq[:, 0:T],
                                start=(i == 0), stop=(i == 7)), deps=[tq])]
                            last_s = [p.comp("tensor", lambda E, i=i: E.matmul(
                                out=ln.ps_s[:, 0:T], lhsT=kn.ap(1.0), rhs=yv[:, i, :],
                                start=(i == 0), stop=(i == 7)), deps=yd)]
                        a_inv = kn.row(1.0 / 1024.0)
                        t_mean = p.comp("scalar", lambda E: E.activation(
                            out=ln.r_mean[:, 0:T], in_=ln.ps_s[:, 0:T], func=AF.Copy,
                            scale=a_inv), deps=last_s + kn.dep())
                        t_msq = p.comp("scalar", lambda E: E.activation(
                            out=ln.r_msq[:, 0:T], in_=ln.ps_q[:, 0:T], func=AF.Copy,
                            scale=a_inv), deps=last_q + kn.dep())
                        t_m2_ = p.comp("scalar", lambda E: E.activation(
                            out=ln.r_tmp[:, 0:T], in_=ln.r_mean[:, 0:T], func=AF.Square),
                            deps=[t_mean])
                        t_var = p.comp("vector", lambda E: E.tensor_sub(
                            out=ln.r_msq[:, 0:T], in0=ln.r_msq[:, 0:T], in1=ln.r_tmp[:, 0:T]),
                            deps=[t_msq, t_m2_])
                        t_lnv = p.comp("scalar", lambda E: E.activation(
                            out=ln.r_tmp[:, 0:T], in_=ln.r_msq[:, 0:T], func=AF.Ln,
                            bias=kn.row(EPS)), deps=[t_var] + kn.dep())
                        t_rst = p.comp("scalar", lambda E: E.activation(
                            out=ln.r_rstd[:, 0:T], in_=ln.r_tmp[:, 0:T], func=AF.Exp,
                            scale=kn.row(-0.5)), deps=[t_lnv] + kn.dep())
                        t_mb = p.comp("tensor", lambda E: E.matmul(
                            out=ln.ps_mb[:, 0:T], lhsT=ones_row[:], rhs=ln.r_mean[:, 0:T],
                            start=True, stop=True), deps=[t_mean] + last_s)
                        t_rb = p.comp("tensor", lambda E: E.matmul(
                            out=ln.ps_rb[:, 0:T], lhsT=ones_row[:], rhs=ln.r_rstd[:, 0:T],
                            start=True, stop=True), deps=[t_rst])
                        ldr = [p.dma("sync", xtc[:],
                                     src[:, s * 1024 + pc * T: s * 1024 + pc * T + T]
                                     .rearrange("(a p) t -> p a t", p=128),
                                     deps=lnd + prev_s_done + [t_rb])]
                        st_d = []
                        for i in range(8):
                            ta = p.comp("vector", lambda E, i=i: E.tensor_sub(
                                out=ln.sq[:, 0:T], in0=yv[:, i, :], in1=ln.ps_mb[:, 0:T]),
                                deps=[t_mb] + sq_t + st_d[-1:])
                            tb2 = p.comp("vector", lambda E, i=i: E.tensor_mul(
                                out=ln.sq[:, 0:T], in0=ln.sq[:, 0:T], in1=ln.ps_rb[:, 0:T]),
                                deps=[t_rb, ta])
                            tc2 = p.comp("vector", lambda E, i=i: E.tensor_scalar(
                                out=ln.sq[:, 0:T], in0=ln.sq[:, 0:T], scalar1=lng[:, i:i + 1],
                                scalar2=lnb[:, i:i + 1], op0=ALU.mult, op1=ALU.add),
                                deps=[tb2] + wl)
                            tf = p.comp("vector", lambda E, i=i: E.tensor_add(
                                out=xtc[:, i, :], in0=xtc[:, i, :], in1=ln.sq[:, 0:T]),
                                deps=[tc2] + ldr)
                            st_d.append(tf)
                            st_d.append(p.dma("gpsimd", dst[i * 128:(i + 1) * 128,
                                                s * 1024 + pc * T: s * 1024 + pc * T + T],
                                  xtc[:, i, :], deps=[tf]))
                        prev_s_done = st_d + last_q
        def emit_l1ffn(src, dst, first=True):
            with ctx.scope():
                bar0 = p.barrier()
                rwst = sb(ctx, "rwst", [128, 8, 8], F32)
                rw16 = sb(ctx, "rw16", [128, 8, 8], BF16)
                wstg2 = [sb(ctx, f"wstg2_{i}", [128, 8, 128], F32) for i in range(2)]
                w16 = [sb(ctx, f"w16_{i}", [128, 8, 128], BF16) for i in range(2)]
                xtc = sb(ctx, "xtc", [128, 8, T], F32)
                xn16 = sb(ctx, "xn16", [128, 8, TOK], BF16)
                acc = sb(ctx, "acc", [128, 8, TOK], BF16)
                hid16 = sb(ctx, "hid16", [128, 8, TOK], BF16)
                s16 = sb(ctx, "s16", [128, TOK], BF16)
                logit = sb(ctx, "logit", [128, 16, 8], F32)
                prob = sb(ctx, "prob", [128, 16, 8], F32)
                gate = sb(ctx, "gate", [128, 16, 8], F32)
                gateT = sb(ctx, "gateT", [8, TOK], F32)
                gsb4 = sb(ctx, "gsb4", [128, NCH, T], BF16)
                t1sb = sb(ctx, "t1sb", [128, T], BF16)
                nmax = sb(ctx, "nmax", [128, 1], F32)
                ssum = sb(ctx, "ssum", [128, 1], F32)
                rsum = sb(ctx, "rsum", [128, 1], F32)
                m2p = sb(ctx, "m2p", [128, 1], F32)
                msk = sb(ctx, "msk", [128, 8], F32)
                bc1 = sb(ctx, "bc1", [1, 2, T], F32)
                ln = LnTmp(ctx, T)
                wkps = [ps(ctx, f"wkps{i}", [128, T], F32) for i in range(2)]
                gps = ps(ctx, "gps", [128, T], F32)
                tps = ps(ctx, "tps", [128, 128], F32)
                wl = [p.dma("sync", rwst[:], rw_in.rearrange("k p e -> p k e"), deps=bar0)]
                wc = [p.comp("scalar", lambda E: E.activation(out=rw16[:], in_=rwst[:], func=AF.Copy), deps=wl)] + wc0
                lnall = []
                ldxs = []
                for ch in range(NCH):
                    c0 = ch * T
                    ldx = [p.dma("sync", xtc[:], src[:, c0:c0 + T].rearrange(
                        "(a p) t -> p a t", p=128), deps=(lnall[-1:] or (wc + wl)) + bar0)]
                    ldxs.append(ldx)
                    lnd, _ = emit_ln(p, ln, xtc, 8, ones_col[:], ones_row[:], xn16, ldx,
                                     kn, T_=T, col0=0, ocol0=c0)
                    lnall += lnd
                gdone = []
                tT_rd = []
                for tt in range(16):
                    tm = None
                    for kt in range(8):
                        tm = p.comp("tensor", lambda E, tt=tt, kt=kt: E.matmul(
                            out=wkps[0][:, 0:8], lhsT=xn16[:, kt, tt * 128:(tt + 1) * 128],
                            rhs=rw16[:, kt, :], start=(kt == 0), stop=(kt == 7)),
                            deps=lnall + wc + gdone[-1:])
                    t_lg = p.comp("vector", lambda E, tt=tt: E.tensor_copy(
                        out=logit[:, tt, :], in_=wkps[0][:, 0:8]), deps=[tm])
                    t_mx = p.comp("vector", lambda E, tt=tt: E.reduce_max(
                        out=nmax[:], in_=logit[:, tt, :], axis=mybir.AxisListType.X,
                        negate=True), deps=[t_lg] + gdone[-1:])
                    t_ex = p.comp("scalar", lambda E, tt=tt: E.activation(
                        out=prob[:, tt, :], in_=logit[:, tt, :], func=AF.Exp,
                        bias=nmax[:, 0:1], accum_out=ssum[:]), deps=[t_mx])
                    t_rc = p.comp("vector", lambda E: E.reciprocal(
                        out=rsum[:], in_=ssum[:]), deps=[t_ex])
                    t_pr = p.comp("vector", lambda E, tt=tt: E.tensor_scalar(
                        out=prob[:, tt, :], in0=prob[:, tt, :], scalar1=rsum[:, 0:1],
                        scalar2=None, op0=ALU.mult), deps=[t_rc])
                    t_m1 = p.comp("vector", lambda E, tt=tt: E.reduce_max(
                        out=nmax[:], in_=prob[:, tt, :], axis=mybir.AxisListType.X),
                        deps=[t_pr])
                    t_eq = p.comp("vector", lambda E, tt=tt: E.tensor_scalar(
                        out=msk[:], in0=prob[:, tt, :], scalar1=nmax[:, 0:1],
                        scalar2=None, op0=ALU.is_lt), deps=[t_m1])
                    t_mk = p.comp("vector", lambda E, tt=tt: E.tensor_mul(
                        out=msk[:], in0=msk[:], in1=prob[:, tt, :]), deps=[t_eq])
                    t_m2 = p.comp("vector", lambda E: E.reduce_max(
                        out=m2p[:], in_=msk[:], axis=mybir.AxisListType.X), deps=[t_mk])
                    t_dn = p.comp("vector", lambda E: E.tensor_add(
                        out=ssum[:], in0=nmax[:], in1=m2p[:]), deps=[t_m2])
                    t_rc2 = p.comp("vector", lambda E: E.reciprocal(
                        out=rsum[:], in_=ssum[:]), deps=[t_dn])
                    t_ge = p.comp("vector", lambda E, tt=tt: E.tensor_scalar(
                        out=msk[:], in0=prob[:, tt, :], scalar1=m2p[:, 0:1],
                        scalar2=None, op0=ALU.is_ge), deps=[t_m2, t_mk])
                    t_g1 = p.comp("vector", lambda E, tt=tt: E.tensor_mul(
                        out=gate[:, tt, :], in0=msk[:], in1=prob[:, tt, :]), deps=[t_ge])
                    t_g2 = p.comp("vector", lambda E, tt=tt: E.tensor_scalar(
                        out=gate[:, tt, :], in0=gate[:, tt, :], scalar1=rsum[:, 0:1],
                        scalar2=None, op0=ALU.mult), deps=[t_g1, t_rc2])
                    t_tr = p.comp("tensor", lambda E, tt=tt: E.transpose(
                        out=tps[0:8, :], in_=gate[:, tt, 0:8], identity=eyef[:]),
                        deps=[t_g2] + tT_rd)
                    t_gt = p.comp("vector", lambda E, tt=tt: E.tensor_copy(
                        out=gateT[0:8, tt * 128:(tt + 1) * 128], in_=tps[0:8, :]),
                        deps=[t_tr])
                    tT_rd = [t_gt]
                    gdone.append(t_g2)
                    gdone.append(t_gt)
                accd = []
                ev_all = list(gdone)
                p._moe_cv = []
                p._moe_mm = []
                p._gsb_rd = []
                p._tgb_hist = []
                qg = [0]

                if not hasattr(p, "_ms2st"):
                    p._ms2st = {}
                p._moe_wg = {0: [], 1: []}

                def load_w(src_ap, scr_ap, skey):
                    q = qg[0] % 2
                    qg[0] += 1
                    if not first:
                        tcv = p.dma(["sync", "gpsimd"][q], w16[q][:], scr_ap,
                                    deps=(p._moe_cv[-2:-1] or (wl + lnall)) +
                                    p._moe_mm[-5:-4] + p._ms2st[skey])
                    else:
                        tld = p.dma(["sync", "gpsimd"][q], wstg2[q][:], src_ap,
                                    deps=(p._moe_cv[-2:-1] or (wl + lnall)))
                        tcv = p.comp("scalar", lambda E, q=q: E.activation(
                            out=w16[q][:], in_=wstg2[q][:], func=AF.Copy),
                            deps=[tld] + p._moe_mm[-5:-4] + p._moe_wg[q])
                        tst = p.dma(["gpsimd", "sync"][q], scr_ap, w16[q][:],
                                    deps=[tcv])
                        p._ms2st[skey] = [tst]
                        p._moe_wg[q] = [tst]
                    p._moe_cv.append(tcv)
                    return q, tcv

                for e in range(8):
                    hidd = []
                    sils = {}
                    for jj in range(8):
                        for half in range(2):
                            mt = jj + 8 * half
                            q, tcv = load_w(up_in[e, mt], up16s[e, mt],
                                            ("up", e, mt))
                            for ch in range(NCH):
                                c0 = ch * T
                                tm = None
                                for kt in range(8):
                                    tm = p.comp("tensor", lambda E, q=q, kt=kt, ch=ch, c0=c0: E.matmul(
                                        out=wkps[ch % 2][:, :], lhsT=w16[q][:, kt, :],
                                        rhs=xn16[:, kt, c0:c0 + T], start=(kt == 0),
                                        stop=(kt == 7)),
                                        deps=[tcv] + lnall + wc + ev_all[-2:-1] + accd[-1:])
                                p._moe_mm.append(tm)
                                if half == 0:
                                    tev = p.comp("scalar", lambda E, ch=ch, c0=c0: E.activation(
                                        out=s16[:, c0:c0 + T], in_=wkps[ch % 2][:, :],
                                        func=AF.Silu), deps=[tm] + hidd[-1:])
                                    sils[ch] = tev
                                else:
                                    tev = p.comp("vector", lambda E, jj=jj, ch=ch, c0=c0: E.tensor_mul(
                                        out=hid16[:, jj, c0:c0 + T], in0=s16[:, c0:c0 + T],
                                        in1=wkps[ch % 2][:, :]),
                                        deps=[tm, sils[ch]] + hidd[-4:-3])
                                    hidd.append(tev)
                                ev_all.append(tev)
                    gds = []
                    for ch in range(NCH):
                        c0 = ch * T
                        tmv = p.dma("gpsimd", bc1[0:1, ch % 2, 0:T],
                                    gateT[e:e + 1, c0:c0 + T],
                                    deps=gdone + (gds[-2:-1] or p._tgb_hist[-2:]))
                        tgb = p.comp("tensor", lambda E, ch=ch: E.matmul(
                            out=gps[:, :], lhsT=ones_row[:], rhs=bc1[0:1, ch % 2, 0:T],
                            start=True, stop=True), deps=[tmv] + gds[-1:] + p._gsb_rd)
                        tgs = p.comp("scalar", lambda E, ch=ch: E.activation(
                            out=gsb4[:, ch, :], in_=gps[:, :], func=AF.Copy),
                            deps=[tgb] + p._gsb_rd)
                        gds.append(tgs)
                        p._tgb_hist.append(tgb)
                    p._gsb_rd = []
                    for mt in range(8):
                        q, tcv = load_w(dn_in[e, mt], dn16s[e, mt],
                                        ("dn", e, mt))
                        for ch in range(NCH):
                            c0 = ch * T
                            tm = None
                            for kt in range(8):
                                tm = p.comp("tensor", lambda E, q=q, kt=kt, ch=ch, c0=c0: E.matmul(
                                    out=wkps[ch % 2][:, :], lhsT=w16[q][:, kt, :],
                                    rhs=hid16[:, kt, c0:c0 + T], start=(kt == 0),
                                    stop=(kt == 7)), deps=[tcv] + hidd + ev_all[-2:-1])
                            p._moe_mm.append(tm)
                            t_t1 = p.comp("vector", lambda E, ch=ch: E.tensor_mul(
                                out=t1sb[:, :], in0=wkps[ch % 2][:, :], in1=gsb4[:, ch, :]),
                                deps=[tm] + gds + accd[-1:])
                            p._gsb_rd.append(t_t1)
                            ev_all.append(t_t1)
                            if e == 0:
                                ta = p.comp("vector", lambda E, mt=mt, c0=c0: E.tensor_copy(
                                    out=acc[:, mt, c0:c0 + T], in_=t1sb[:, :]), deps=[t_t1])
                            else:
                                ta = p.comp("vector", lambda E, mt=mt, c0=c0: E.tensor_add(
                                    out=acc[:, mt, c0:c0 + T], in0=acc[:, mt, c0:c0 + T],
                                    in1=t1sb[:, :]), deps=[t_t1] + accd[-32:-31])
                            accd.append(ta)
                st_prev = []
                for ch in range(NCH):
                    c0 = ch * T
                    ldx2 = p.dma("sync", xtc[:], src[:, c0:c0 + T].rearrange(
                        "(a p) t -> p a t", p=128), deps=accd + ldxs[-1] + st_prev)
                    st_prev = []
                    for mt in range(8):
                        tf = p.comp("vector", lambda E, mt=mt, c0=c0: E.tensor_add(
                            out=xtc[:, mt, :], in0=xtc[:, mt, :], in1=acc[:, mt, c0:c0 + T]),
                            deps=[ldx2] + accd)
                        st_prev.append(p.dma("gpsimd", dst[mt * 128:(mt + 1) * 128, c0:c0 + T],
                                             xtc[:, mt, :], deps=[tf]))

        def emit_l1_mamba(src, dst):
            mb = MB()
            mb.is_l0 = False
            mb.kzero = kzero
            mb.eye16 = eye16
            mb.ones_row = ones_row
            mb.scr_y, mb.scr_ds = scr_y, scr_ds
            mb.scr_xi, mb.scr_z, mb.scr_cr = scr_xi, scr_z, scr_cr
            mb.spS = lambda b, dt, n: spcol(b, dt, n)
            mb.spP = lambda b, dt, n: spcol(2 + b, dt, n)
            mb.Hcol = lambda b, dt, n: Htile[:, b * 256 + dt * 16 + n:
                                             b * 256 + dt * 16 + n + 1]
            with ctx.scope():
                bar0 = p.barrier()
                wbst = sb(ctx, "wbst", [128, 16, 32], F32)
                wb16 = sb(ctx, "wb16", [128, 16, 32], BF16)
                convw = sb(ctx, "convw", [128, 16, 4], F32)
                convb = sb(ctx, "convb", [128, 16], F32)
                bdel = sb(ctx, "bdel", [128, 16], F32)
                lng = sb(ctx, "lng", [128, 16], F32)
                lnb = sb(ctx, "lnb", [128, 16], F32)
                arow = sb(ctx, "arow", [1, 16], F32)
                ar1 = sb(ctx, "ar1", [1, 16], F32)
                ar2 = sb(ctx, "ar2", [1, 16], F32)
                acol = sb(ctx, "acol", [128, 16], F32)
                mb.acol_fn = lambda dt, n: acol[:, n:n + 1]
                wl = [p.dma("sync", dst_, src_, deps=bar0) for dst_, src_ in [
                    (convw[:], convw2_in), (convb[:], convb2_in), (bdel[:], bdel_in),
                    (arow[:], alog2_in), (lng[:], lng2_in), (lnb[:], lnb2_in),
                    (wbst[:, :, 0:16], wb_in.rearrange("k p n -> p k n")),
                    (wbst[:, :, 16:32], wcc_in.rearrange("k p n -> p k n"))]]
                wc = [p.comp("scalar", lambda E: E.activation(out=wb16[:], in_=wbst[:],
                             func=AF.Copy), deps=wl)] + wc0
                t_a = p.comp("scalar", lambda E: E.activation(
                    out=ar1[:], in_=arow[:], func=AF.Abs), deps=wl)
                t_a = p.comp("scalar", lambda E: E.activation(
                    out=ar1[:], in_=ar1[:], func=AF.Exp, scale=-1.0), deps=[t_a])
                t_a = p.comp("scalar", lambda E: E.activation(
                    out=ar1[:], in_=ar1[:], func=AF.Ln, bias=1.0), deps=[t_a])
                t_a2 = p.comp("scalar", lambda E: E.activation(
                    out=ar2[:], in_=arow[:], func=AF.Relu), deps=wl)
                t_arow = p.comp("vector", lambda E: E.tensor_add(
                    out=ar1[:], in0=ar1[:], in1=ar2[:]), deps=[t_a, t_a2])
                bnds = sb(ctx, "bnds", [128, 8, 6], F32)
                tb1 = p.dma("sync", bnds[:, :, 0:3],
                            src[:, SEG - 3:SEG].rearrange("(a p) t -> p a t", p=128),
                            deps=bar0)
                tb2 = p.dma("sync", bnds[:, :, 3:6],
                            src[:, TOK - 3:TOK].rearrange("(a p) t -> p a t", p=128),
                            deps=bar0)
                tc3 = emit_gather(p, ctx, G, bnds.rearrange("p a t -> p (a t)"),
                                  cc3_in, cc3_out, 48, [tb1, tb2])
                bsf = [sb(ctx, f"bsf{i}", [128, 8, 6], F32) for i in range(2)]
                tsel = p.comp("vector", lambda E: E.memset(bsel[:], 0.0), deps=[tc3])
                bsf_rd = {0: [], 1: []}
                for r in range(8):
                    sl = r % 2
                    tl = p.dma("sync", bsf[sl][:],
                               cc3_out[r * 128:(r + 1) * 128, :].rearrange(
                                   "p (a t) -> p a t", a=8), deps=[tc3] + bsf_rd[sl])
                    tsel = p.comp("vector", lambda E, r=r, sl=sl: E.scalar_tensor_tensor(
                        out=bsel[:], in0=bsf[sl][:], scalar=G.hsel[:, r:r + 1],
                        in1=bsel[:], op0=ALU.mult, op1=ALU.add), deps=[tl, tsel])
                    bsf_rd[sl] = [tsel]
                bseld = [tsel]

                def _phaseA():
                    xtc = sb(ctx, "xtc", [128, 8, TE], F32)
                    xn16 = sb(ctx, "xn16", [128, 8, T], BF16)
                    xn16e = sb(ctx, "xn16e", [128, 8, 3], BF16)
                    lpre16 = sb(ctx, "lpre16", [128, 16, TE], BF16)
                    left16 = sb(ctx, "left16", [128, 16, T], BF16)
                    g16 = sb(ctx, "g16", [128, 2, T], BF16)
                    delta16 = sb(ctx, "delta16", [128, 16, T], BF16)
                    bbc16 = sb(ctx, "bbc16", [128, 16, T], BF16)
                    cbc16 = sb(ctx, "cbc16", [128, 16, T], BF16)
                    bc1 = sb(ctx, "bc1", [1, 2, T], F32)
                    mb.bc1 = bc1
                    dsc = sb(ctx, "dsc", [128, 2, T], F32)
                    mb.du16 = sb(ctx, "du16", [128, 2, T], BF16)
                    mb.u16 = sb(ctx, "u16", [128, 2, T], BF16)
                    mb.tmp16 = sb(ctx, "tmp16", [128, NR, T], BF16)
                    mb.dA = sb(ctx, "dA", [128, NR, T], F32)
                    mb.hS = sb(ctx, "hS", [128, 2, T], F32)
                    mb.Ds = sb(ctx, "Ds", [128, 2, T], F32)
                    mb.z16zero = sb(ctx, "zz", [128, T], BF16)
                    mb.dstate = sb(ctx, "dstate", [128, 16], F32)
                    mb.ysb = sb(ctx, "ysb", [128, 2, T], F32)
                    ln = LnTmp(ctx, T)
                    cacc = ln.t1
                    bcst = ln.t1
                    wkps = [ps(ctx, f"wkps{i}", [128, T], F32) for i in range(2)]
                    mb.ybank = ps(ctx, "ybank", [128, T], F32)
                    mb.bcbank = ps(ctx, "bcbank", [128, T], F32)
                    mb.g = 0
                    mb.last_ya = []
                    mb.ysb_rd = {0: [], 1: []}
                    mb.ds_rd = {0: [], 1: []}
                    mb.du_rd = {0: [], 1: []}
                    mb.da_rd = {s: [] for s in range(NR)}
                    mb.u_rd = {0: [], 1: []}
                    mb.h_rd = {0: [], 1: []}
                    mb.tmp_rd = {s: [] for s in range(NR)}
                    ms = MS2()
                    stgL = [sb(ctx, f"stgL_{i}", [128, 16, 128], F32) for i in range(2)]
                    w16L = [sb(ctx, f"w16L_{i}", [128, 16, 128], BF16) for i in range(2)]
                    t_ab = p.comp("tensor", lambda E: E.matmul(
                        out=mb.bcbank[:, 0:16], lhsT=ones_row[:], rhs=ar1[:],
                        start=True, stop=True), deps=[t_arow])
                    t_ac = p.comp("vector", lambda E: E.tensor_scalar(
                        out=acol[:], in0=mb.bcbank[:, 0:16], scalar1=-1.0,
                        scalar2=None, op0=ALU.mult), deps=[t_ab])
                    wc2 = wc + [
                        t_ac,
                        p.comp("vector", lambda E: E.memset(mb.z16zero[:], 0.0)),
                        p.comp("vector", lambda E: E.memset(lpre16[:, :, 0:3], 0.0))]
                    mb.t1_rd = []
                    mb.cbc_rd = []
                    mb.xtc_free = []
                    grd = {0: [], 1: []}

                    def frontA(ch):
                        b, j = ch // 2, ch % 2
                        cb = b * SEG + j * T
                        bar = p.barrier() if ch == 0 else []
                        if j == 0:
                            ldx = [p.dma("sync", xtc[:, :, 3:3 + T],
                                         src[:, cb:cb + T].rearrange("(a p) t -> p a t", p=128),
                                         deps=bar + mb.xtc_free)]
                            tcp = p.comp("vector", lambda E, b=b: E.tensor_copy(
                                out=xtc[:, :, 0:3], in_=bsel[:, :, b * 3:(b + 1) * 3]),
                                deps=bseld + bar + mb.xtc_free)
                            ldx = ldx + [tcp]
                        else:
                            ldx = [p.dma("sync", xtc[:],
                                         src[:, cb - 3:cb + T].rearrange("(a p) t -> p a t", p=128),
                                         deps=bar + mb.xtc_free)]
                        lnd, _ = emit_ln(p, ln, xtc, 8, ones_col[:], ones_row[:], xn16,
                                         ldx + mb.t1_rd, kn, T_=T, col0=3)
                        if j == 0:
                            lnde, _ = emit_ln(p, ln, xtc, 8, ones_col[:], ones_row[:],
                                              xn16e, ldx + lnd + mb.t1_rd, kn, T_=3, col0=0)
                            lnd = lnd + lnde
                        mb.xtc_free = lnd[-1:]

                        def lp_evac(mt, bank, deps):
                            return p.comp("vector", lambda E, mt=mt, bank=bank: E.tensor_copy(
                                out=lpre16[:, mt, 3:3 + T], in_=bank[:, :]), deps=deps)

                        def lpe_evac(mt, bank, deps):
                            return p.comp("scalar", lambda E, mt=mt, bank=bank: E.activation(
                                out=lpre16[:, mt, 0:3], in_=bank[:, 0:3], func=AF.Copy), deps=deps)

                        lpd = mm_stream2(p, ms, stgL, w16L, lambda mt: wl_in[mt],
                                         8, 16, lambda kt: xn16[:, kt, :], lp_evac,
                                         wkps, lnd, bar,
                                         ext=((lambda kt: xn16e[:, kt, :]), lpe_evac,
                                              3, lnd, 16) if j == 0 else None,
                                         scr=wl16s, first=(ch == 0))

                        def g_evac(mt, bank, deps, ch=ch):
                            t = p.comp("scalar", lambda E, mt=mt, bank=bank: E.activation(
                                out=g16[:, mt % 2, :], in_=bank[:, :], func=AF.Silu),
                                deps=deps + grd[mt % 2])
                            td = p.dma("scalar", scr_z[ch, mt], g16[:, mt % 2, :],
                                       deps=[t])
                            grd[mt % 2] = [td]
                            return t
                        gd = mm_stream2(p, ms, stgL, w16L, lambda mt: wr_in[mt],
                                        8, 16, lambda kt: xn16[:, kt, :], g_evac,
                                        wkps, lnd, bar + lpd,
                                        scr=wr16s, first=(ch == 0))
                        return dict(ch=ch, b=b, j=j, cb=cb, lpd=lpd, gd=gd)

                    def midA(fa):
                        ch = fa["ch"]
                        lpd = fa["lpd"]
                        gd = fa["gd"]
                        conv_d = []
                        for dt in range(16):
                            cap = cacc[:, 0:T] if dt % 2 == 0 else dsc[:, 0, 0:T]
                            tc_ = p.comp("vector", lambda E, dt=dt, cap=cap: E.tensor_scalar(
                                out=cap, in0=lpre16[:, dt, 0:T],
                                scalar1=convw[:, dt, 0:1], scalar2=convb[:, dt:dt + 1],
                                op0=ALU.mult, op1=ALU.add), deps=lpd + conv_d[-2:])
                            for k in range(1, 4):
                                tc_ = p.comp("vector", lambda E, dt=dt, k=k, cap=cap: E.scalar_tensor_tensor(
                                    out=cap, in0=lpre16[:, dt, k:k + T],
                                    scalar=convw[:, dt, k:k + 1], in1=cap,
                                    op0=ALU.mult, op1=ALU.add), deps=[tc_])
                            conv_d.append(p.comp("scalar", lambda E, dt=dt, cap=cap: E.activation(
                                out=left16[:, dt, :], in_=cap, func=AF.Silu),
                                deps=[tc_]))
                        t_carry = p.comp("vector", lambda E: E.tensor_copy(
                            out=lpre16[:, :, 0:3], in_=lpre16[:, :, T:T + 3]), deps=conv_d)
                        t_b = None
                        t_c = None
                        for kt in range(16):
                            t_b = p.comp("tensor", lambda E, kt=kt: E.matmul(
                                out=wkps[0][0:16, :], lhsT=wb16[:, kt, 0:16],
                                rhs=left16[:, kt, :], start=(kt == 0), stop=(kt == 15)),
                                deps=conv_d + wc2 + lpd + [t_carry] +
                                (ms.bank_rd[0] if kt == 0 else []))
                        for kt in range(16):
                            t_c = p.comp("tensor", lambda E, kt=kt: E.matmul(
                                out=wkps[1][0:16, :], lhsT=wb16[:, kt, 16:32],
                                rhs=left16[:, kt, :], start=(kt == 0), stop=(kt == 15)),
                                deps=conv_d + wc2 + (ms.bank_rd[1] if kt == 0 else []))
                        t_bs = p.comp("vector", lambda E: E.tensor_copy(
                            out=bcst[0:16, 0:T], in_=wkps[0][0:16, :]), deps=[t_b])
                        t_cs = p.comp("vector", lambda E: E.tensor_copy(
                            out=bcst[32:48, 0:T], in_=wkps[1][0:16, :]), deps=[t_c])
                        ms.bank_rd[0] = ms.bank_rd[0] + [t_bs]
                        ms.bank_rd[1] = ms.bank_rd[1] + [t_cs]
                        t_crs = p.dma("sync", scr_cr[ch].rearrange("n o t -> (n o) t"),
                                      bcst[32:48, 0:T], deps=[t_cs])
                        deld = []
                        dsc_rd = {0: [], 1: []}

                        def d_evac(mt, bank, deps):
                            ds2 = mt % 2
                            ta1 = p.comp("scalar", lambda E, mt=mt, bank=bank: E.activation(
                                out=ln.sq[:, 0:T], in_=bank[:, :], func=AF.Abs,
                                bias=bdel[:, mt:mt + 1]), deps=deps + deld[-1:])
                            ta2 = p.comp("scalar", lambda E: E.activation(
                                out=ln.sq[:, 0:T], in_=ln.sq[:, 0:T], func=AF.Exp,
                                scale=-1.0), deps=[ta1])
                            ta3 = p.comp("scalar", lambda E: E.activation(
                                out=ln.sq[:, 0:T], in_=ln.sq[:, 0:T], func=AF.Ln,
                                bias=1.0), deps=[ta2])
                            ta4 = p.comp("scalar", lambda E, mt=mt, bank=bank, ds2=ds2: E.activation(
                                out=dsc[:, ds2, :], in_=bank[:, :], func=AF.Relu,
                                bias=bdel[:, mt:mt + 1]), deps=[ta3] + dsc_rd[ds2])
                            t5 = p.comp("vector", lambda E, ds2=ds2: E.tensor_add(
                                out=dsc[:, ds2, :], in0=ln.sq[:, 0:T],
                                in1=dsc[:, ds2, :]), deps=[ta4])
                            t6 = p.comp("vector", lambda E, mt=mt, ds2=ds2: E.tensor_scalar(
                                out=delta16[:, mt, :], in0=dsc[:, ds2, :], scalar1=1e-4,
                                scalar2=10.0, op0=ALU.max, op1=ALU.min), deps=[t5])
                            dsc_rd[ds2] = [t6]
                            deld.append(t6)
                            return ta4

                        mm_stream2(p, ms, stgL, w16L, lambda mt: wd_in[mt],
                                   16, 16, lambda kt: left16[:, kt, :], d_evac,
                                   wkps, conv_d + [t_bs, t_cs, t_crs], gd,
                                   scr=wd16s, first=(ch == 0))
                        bc_d = []
                        bmm = []
                        tmv = None
                        for n in range(32):
                            row = n if n < 16 else n + 16
                            tmv = p.dma("sync", bc1[0:1, n % 2, 0:T], bcst[row:row + 1, 0:T],
                                        deps=[t_bs, t_cs] + bmm[-2:-1] + deld[-1:])
                            tb = p.comp("tensor", lambda E, n=n: E.matmul(
                                out=mb.bcbank[:, :], lhsT=ones_row[:],
                                rhs=bc1[0:1, n % 2, 0:T], start=True, stop=True),
                                deps=[tmv] + bc_d[-1:])
                            bmm.append(tb)
                            dstt = bbc16 if n < 16 else cbc16
                            bc_d.append(p.comp("scalar", lambda E, n=n, dstt=dstt: E.activation(
                                out=dstt[:, n % 16, :], in_=mb.bcbank[:, :], func=AF.Copy),
                                deps=[tb] + mb.cbc_rd))
                        mb.t1_rd = [t_crs, tmv]
                        return dict(ch=ch, deld=deld, t_carry=t_carry,
                                    bc_d=bc_d, conv_d=conv_d)

                    fas = [None] * NCH
                    mds = [None] * NCH
                    fas[0] = frontA(0)
                    mds[0] = midA(fas[0])
                    for ch in range(NCH):
                        if ch + 1 < NCH:
                            fas[ch + 1] = frontA(ch + 1)
                        md = mds[ch]
                        scan_A(p, ctx, mb, ch, delta16, left16, bbc16, cbc16,
                               md["deld"] + [md["t_carry"]], md["bc_d"],
                               md["conv_d"], wc2)
                        if ch + 1 < NCH:
                            mds[ch + 1] = midA(fas[ch + 1])
                with ctx.scope():
                    _phaseA()
                barx = p.barrier()
                tc2 = emit_gather(p, ctx, G, sp_all[:, :], cc2_in, cc2_out, 1024, barx)
                wcH = emit_fold(p, ctx, G, cc2_out, Htile, tc2)

                def _phaseB():
                    xtc = sb(ctx, "xtcB", [128, 8, T], F32)
                    cbc16 = sb(ctx, "cbcB", [128, 16, T], BF16)
                    y2 = sb(ctx, "y2B", [128, 16, T], BF16)
                    y3 = sb(ctx, "y3B", [128, 16, T], BF16)
                    mb.dsb = sb(ctx, "dsb", [128, 2, T], F32)
                    mb.yacc = sb(ctx, "yacc", [128, 2, T], F32)
                    mb.xib = sb(ctx, "xib", [128, 2, T], BF16)
                    mb.dec = sb(ctx, "dec", [128, NR, T], F32)
                    mb.tmpc2 = sb(ctx, "tmpc2", [128, 4, T], F32)
                    ones_c16 = sb(ctx, "ones_c16", [128, 1], BF16)
                    bc1 = sb(ctx, "bc1B", [1, 2, T], F32)
                    mb.bc1 = bc1
                    ln = LnTmp(ctx, T)
                    wkps = [ps(ctx, f"wkpsB{i}", [128, T], F32) for i in range(2)]
                    mb.bcbank = ps(ctx, "bcbankB", [128, T], F32)
                    mb.g = 0
                    mb.dsb_rd = {0: [], 1: []}
                    mb.yacc_rd = {0: [], 1: []}
                    mb.xib_rd = {0: [], 1: []}
                    mb.dec_rd = {s: [] for s in range(NR)}
                    mb.tmpc_rd = {i: [] for i in range(4)}
                    ms = MS2()
                    stgB2 = [sb(ctx, f"stgB2_{i}", [128, 16, 128], F32) for i in range(2)]
                    w16B2 = [sb(ctx, f"w16B2_{i}", [128, 16, 128], BF16) for i in range(2)]
                    wc3 = [p.comp("vector", lambda E: E.memset(ones_c16[:], 1.0))]
                    a_eps = kn.row(EPS)
                    a_mh = kn.row(-0.5)
                    for ch in range(NCH):
                        b, j = ch // 2, ch % 2
                        cb = b * SEG + j * T
                        bar = p.barrier()
                        bc_d = emit_cbc(p, ctx, mb, ch, cbc16, bar)
                        y2d = corr_B(p, ctx, mb, ch, cbc16, y2, bar, wcH, bc_d)
                        sq_t, last_q, last_s = [], [], []
                        for i in range(16):
                            tq = p.comp("scalar", lambda E, i=i: E.activation(
                                out=ln.sq[:, 0:T], in_=y2[:, i, :], func=AF.Square),
                                deps=y2d + last_q + sq_t[-1:])
                            sq_t.append(tq)
                            last_q = [p.comp("tensor", lambda E, i=i: E.matmul(
                                out=ln.ps_q[:, 0:T], lhsT=ones_col[:], rhs=ln.sq[:, 0:T],
                                start=(i == 0), stop=(i == 15)), deps=[tq])]
                            last_s = [p.comp("tensor", lambda E, i=i: E.matmul(
                                out=ln.ps_s[:, 0:T], lhsT=ones_c16[:], rhs=y2[:, i, :],
                                start=(i == 0), stop=(i == 15)), deps=y2d + wc3)]
                        a_inv2 = kn.row(1.0 / 2048.0)
                        t_mean = p.comp("scalar", lambda E: E.activation(
                            out=ln.r_mean[:, 0:T], in_=ln.ps_s[:, 0:T], func=AF.Copy,
                            scale=a_inv2), deps=last_s + kn.dep())
                        t_msq = p.comp("scalar", lambda E: E.activation(
                            out=ln.r_msq[:, 0:T], in_=ln.ps_q[:, 0:T], func=AF.Copy,
                            scale=a_inv2), deps=last_q + kn.dep())
                        t_m2 = p.comp("scalar", lambda E: E.activation(
                            out=ln.r_tmp[:, 0:T], in_=ln.r_mean[:, 0:T], func=AF.Square),
                            deps=[t_mean])
                        t_var = p.comp("vector", lambda E: E.tensor_sub(
                            out=ln.r_msq[:, 0:T], in0=ln.r_msq[:, 0:T],
                            in1=ln.r_tmp[:, 0:T]), deps=[t_msq, t_m2])
                        t_lnv = p.comp("scalar", lambda E: E.activation(
                            out=ln.r_tmp[:, 0:T], in_=ln.r_msq[:, 0:T], func=AF.Ln,
                            bias=a_eps), deps=[t_var])
                        t_rst = p.comp("scalar", lambda E: E.activation(
                            out=ln.r_rstd[:, 0:T], in_=ln.r_tmp[:, 0:T], func=AF.Exp,
                            scale=a_mh), deps=[t_lnv])
                        t_mb = p.comp("tensor", lambda E: E.matmul(
                            out=ln.ps_mb[:, 0:T], lhsT=ones_row[:], rhs=ln.r_mean[:, 0:T],
                            start=True, stop=True), deps=[t_mean] + last_s)
                        t_rb = p.comp("tensor", lambda E: E.matmul(
                            out=ln.ps_rb[:, 0:T], lhsT=ones_row[:], rhs=ln.r_rstd[:, 0:T],
                            start=True, stop=True), deps=[t_rst])
                        y3d = []
                        for i in range(16):
                            ta = p.comp("vector", lambda E, i=i: E.tensor_sub(
                                out=ln.sq[:, 0:T], in0=y2[:, i, :], in1=ln.ps_mb[:, 0:T]),
                                deps=[t_mb] + y2d + sq_t + y3d[-1:])
                            tb2 = p.comp("vector", lambda E, i=i: E.tensor_mul(
                                out=ln.sq[:, 0:T], in0=ln.sq[:, 0:T], in1=ln.ps_rb[:, 0:T]),
                                deps=[t_rb, ta])
                            y3d.append(p.comp("vector", lambda E, i=i: E.tensor_scalar(
                                out=y3[:, i, :], in0=ln.sq[:, 0:T], scalar1=lng[:, i:i + 1],
                                scalar2=lnb[:, i:i + 1], op0=ALU.mult, op1=ALU.add),
                                deps=[tb2] + wl))
                        ldx2 = [p.dma("sync", xtc[:], src[:, cb:cb + T].rearrange(
                            "(a p) t -> p a t", p=128), deps=bar)]

                        def o_evac(mt, bank, deps):
                            t = p.comp("vector", lambda E, mt=mt, bank=bank: E.tensor_add(
                                out=xtc[:, mt, :], in0=bank[:, :], in1=xtc[:, mt, :]),
                                deps=deps + ldx2)
                            p.dma("gpsimd", dst[mt * 128:(mt + 1) * 128, cb:cb + T],
                                  xtc[:, mt, :], deps=[t])
                            return t

                        mm_stream2(p, ms, stgB2, w16B2, lambda mt: wo2_in[mt],
                                   16, 8, lambda kt: y3[:, kt, :], o_evac, wkps,
                                   y3d, bar, scr=wo216s, first=(ch == 0))
                with ctx.scope():
                    _phaseB()

        stages = [
            (1, lambda: emit_l0_mamba(), sA),
            (3, lambda: emit_l0ffn(sA, sB), sB),
            (4, lambda: emit_l0attnb(sB, sA), sA),
            (5, lambda: emit_l0ffn(sA, sB), sB),
            (6, lambda: emit_l1attna(sB, sA), sA),
            (7, lambda: emit_l1ffn(sA, sB, first=True), sB),
            (8, lambda: emit_l1_mamba(sB, sA), sA),
            (10, lambda: emit_l1ffn(sA, xt_out, first=False), xt_out),
        ]
        cur = None
        for (sid, fn, buf) in stages:
            if sid > stop_after:
                break
            fn()
            cur = buf
        if dbg is not None and cur is not None and cur is not xt_out:
            fin = p.barrier()
            p.dma("gpsimd", dbg, cur, deps=fin)
        p.emit()
    return nc



_EYE = np.eye(128, dtype=np.float32)


def prep_weights(inputs):
    f32 = np.float32
    A = np.asarray

    def pc(v, n):
        return np.ascontiguousarray(A(v, f32).reshape(n, 128).T)

    w = {}
    w["ip2"] = np.ascontiguousarray(
        A(inputs["L0_attn_a_in_proj"], f32).reshape(8, 128, 32, 128).transpose(2, 1, 0, 3))
    w["op2"] = np.ascontiguousarray(
        A(inputs["L0_attn_a_out_proj"], f32).reshape(16, 128, 8, 128).transpose(2, 1, 0, 3))
    w["xp_l"] = np.ascontiguousarray(A(inputs["L0_attn_a_x_proj"], f32).reshape(16, 128, 96).transpose(1, 0, 2))
    w["dtw_l"] = np.ascontiguousarray(A(inputs["L0_attn_a_dt_w"], f32).reshape(64, 16, 128))
    w["rms_c"] = pc(inputs["L0_attn_a_rms_w"], 8)
    w["convw_c"] = np.ascontiguousarray(A(inputs["L0_attn_a_conv_w"], f32).reshape(16, 128, 4).transpose(1, 0, 2))
    w["convb_c"] = pc(inputs["L0_attn_a_conv_b"], 16)
    w["dtb_c"] = pc(inputs["L0_attn_a_dt_b"], 16)
    w["alog_c"] = np.ascontiguousarray(A(inputs["L0_attn_a_A_log"], f32).reshape(16, 128, 16).transpose(1, 0, 2))
    w["d_c"] = pc(inputs["L0_attn_a_D"], 16)
    w["win_l"] = np.ascontiguousarray(A(inputs["L0_ffn_Win"], f32).reshape(8, 128, 64).transpose(1, 0, 2))
    w["bin_c"] = A(inputs["L0_ffn_bin_"], f32).reshape(64, 1)
    w["w1"] = A(inputs["L0_ffn_W1"], f32)
    w["b1"] = A(inputs["L0_ffn_b1"], f32)
    w["w2"] = A(inputs["L0_ffn_W2"], f32)
    w["b2"] = A(inputs["L0_ffn_b2"], f32).reshape(64, 1)
    w["wout_l"] = np.ascontiguousarray(A(inputs["L0_ffn_Wout"], f32).reshape(64, 8, 128))
    w["bout_c"] = np.ascontiguousarray(A(inputs["L0_ffn_bout"], f32).reshape(8, 128).T)
    def tiler(wt):
        return np.ascontiguousarray(A(wt, f32).T.reshape(8, 128, D).transpose(1, 0, 2))
    w["wq_t"] = tiler(inputs["L0_attn_b_Wq"])
    w["wk_t"] = tiler(inputs["L0_attn_b_Wk"])
    w["wv_t"] = tiler(inputs["L0_attn_b_Wv"])
    w["bq_c"] = np.ascontiguousarray(A(inputs["L0_attn_b_bq"], f32).reshape(8, 128).T)
    w["bk_c"] = np.ascontiguousarray(A(inputs["L0_attn_b_bk"], f32).reshape(8, 128).T)
    w["bv_r"] = A(inputs["L0_attn_b_bv"], f32).reshape(1, D)
    W = A(inputs["L1_attn_a_W_qkvb"], f32)
    perm = []
    for comp in range(3):
        for h in range(16):
            perm.extend(range(h * 193 + comp * 64, h * 193 + comp * 64 + 64))
    perm.extend(h * 193 + 192 for h in range(16))
    Wp = W[np.array(perm)]
    WpT = np.zeros((1024, 3200), f32)
    WpT[:, 0:3088] = Wp.T
    w["wqkvb2"] = np.ascontiguousarray(
        WpT.reshape(8, 128, 25, 128).transpose(2, 1, 0, 3))
    w["woA2"] = np.ascontiguousarray(
        A(inputs["L1_attn_a_W_o"], f32).T.reshape(8, 128, 8, 128).transpose(2, 1, 0, 3))
    w["lng_c"] = np.ascontiguousarray(A(inputs["L1_attn_a_ln_g"], f32).reshape(8, 128).T)
    w["lnb_c"] = np.ascontiguousarray(A(inputs["L1_attn_a_ln_b"], f32).reshape(8, 128).T)
    up = A(inputs["L1_ffn_up_proj"], f32)
    dn = A(inputs["L1_ffn_down_proj"], f32)
    w["up_t"] = np.ascontiguousarray(up.reshape(8, 8, 128, 16, 128).transpose(0, 3, 2, 1, 4))
    w["dn_t"] = np.ascontiguousarray(dn.reshape(8, 8, 128, 8, 128).transpose(0, 3, 2, 1, 4))
    w["rw_l"] = np.ascontiguousarray(A(inputs["L1_ffn_router_w"], f32).T.reshape(8, 128, 8))
    w["wl2"] = np.ascontiguousarray(
        A(inputs["L1_attn_b_W_left"], f32).T.reshape(8, 128, 16, 128).transpose(2, 1, 0, 3))
    w["wd2"] = np.ascontiguousarray(
        A(inputs["L1_attn_b_W_delta"], f32).T.reshape(16, 128, 16, 128).transpose(2, 1, 0, 3))
    w["wr2"] = np.ascontiguousarray(
        A(inputs["L1_attn_b_W_right"], f32).T.reshape(8, 128, 16, 128).transpose(2, 1, 0, 3))
    w["wo22"] = np.ascontiguousarray(
        A(inputs["L1_attn_b_W_out"], f32).T.reshape(16, 128, 8, 128).transpose(2, 1, 0, 3))
    w["wb_l"] = np.ascontiguousarray(A(inputs["L1_attn_b_W_B"], f32).T.reshape(16, 128, 16))
    w["wc_l"] = np.ascontiguousarray(A(inputs["L1_attn_b_W_C"], f32).T.reshape(16, 128, 16))
    w["convw2_c"] = np.ascontiguousarray(A(inputs["L1_attn_b_conv_w"], f32).reshape(16, 128, 4).transpose(1, 0, 2))
    w["convb2_c"] = pc(inputs["L1_attn_b_conv_b"], 16)
    w["bdel_c"] = pc(inputs["L1_attn_b_b_delta"], 16)
    w["alog_r"] = A(inputs["L1_attn_b_A_log"], f32).reshape(1, 16)
    w["lng2_c"] = pc(inputs["L1_attn_b_ln_g"], 16)
    w["lnb2_c"] = pc(inputs["L1_attn_b_ln_b"], 16)
    w["eye"] = _EYE
    return w


def make_inmaps(inputs):
    w = prep_weights(inputs)
    x = np.asarray(inputs["L0_attn_a_x"], np.float32)
    xT = np.ascontiguousarray(x.reshape(B * L, D).T)
    maps = []
    for c in range(NCORES):
        m = dict(w)
        x0e = np.zeros((D, 2 * (SEG + 3)), np.float32)
        for b in range(B):
            lo = b * L + c * SEG
            x0e[:, b * (SEG + 3) + 3: (b + 1) * (SEG + 3)] = xT[:, lo:lo + SEG]
            if c > 0:
                x0e[:, b * (SEG + 3): b * (SEG + 3) + 3] = xT[:, lo - 3:lo]
        m["x0e"] = x0e
        om = np.zeros((128, 8), np.float32)
        om[:, c] = 1.0
        m["ownm"] = om
        pm1 = np.zeros((128, 8), np.float32)
        pm1[:, :c] = 1.0
        m["pm1"] = pm1
        hs = np.zeros((128, 8), np.float32)
        if c > 0:
            hs[:, c - 1] = 1.0
        m["hsel"] = hs
        maps.append(m)
    return maps


def assemble(results, key="xt_new"):
    xT = np.empty((D, B * L), np.float32)
    for c in range(NCORES):
        r = results[c][key]
        for b in range(B):
            xT[:, b * L + c * SEG:(b * L) + (c + 1) * SEG] = \
                r[:, b * SEG:(b + 1) * SEG]
    return np.ascontiguousarray(xT.T).reshape(B, L, D)


def kernel(**inputs):
    if "nc" not in _CACHE:
        _CACHE["nc"] = build_fused()
    nc = _CACHE["nc"]
    maps = make_inmaps(inputs)
    res = run_bass_kernel_spmd(nc, maps, core_ids=list(range(NCORES)))
    return assemble(res.results)


# version 13
# speedup vs baseline: 1.2148x; 1.0314x over previous
import contextlib
import numpy as np
import concourse.bass as bass
import concourse.mybir as mybir
from concourse.bass_utils import run_bass_kernel_spmd

F32 = mybir.dt.float32
BF16 = mybir.dt.bfloat16
AF = mybir.ActivationFunctionType
ALU = mybir.AluOpType

NCORES = 8
B, L, D = 2, 8192, 1024
SEG = 1024
TOK = 2 * SEG
T = 512
NCH = 4
TE = T + 3
EPS = 1e-5
RG = [list(range(8))]



class Ctx:
    def __init__(self, nc, stack):
        self.nc = nc
        self.stacks = [stack]
        self.sems = {}
        self.scope_id = 0

    @property
    def stack(self):
        return self.stacks[-1]

    @contextlib.contextmanager
    def scope(self):
        self.scope_id += 1
        es = contextlib.ExitStack()
        self.stacks.append(es)
        try:
            yield
        finally:
            self.stacks.pop()
            es.close()

    def sem(self, name):
        if name not in self.sems:
            self.sems[name] = [self.stacks[0].enter_context(self.nc.semaphore(name)), 0]
        return self.sems[name]


def sb(ctx, name, shape, dtype):
    return ctx.stack.enter_context(
        ctx.nc.sbuf_tensor(f"s{ctx.scope_id}_{name}", shape, dtype))


def ps(ctx, name, shape, dtype=F32):
    return ctx.stack.enter_context(
        ctx.nc.psum_tensor(f"s{ctx.scope_id}_{name}", shape, dtype))


ENGINES = ("sync", "tensor", "vector", "scalar", "gpsimd")
NDMASLOT = 8


class P:

    def __init__(self, ctx):
        self.ctx = ctx
        self.ops = []
        self.cnt = {e: 0 for e in ENGINES}
        self.dcnt = {}
        self.didx = {e: 0 for e in ENGINES}
        self.dlast = {}
        self.ccnt = 0
        self.cclast = None

    def comp(self, eng, fn, deps=()):
        self.cnt[eng] += 1
        key = "E_" + eng
        tok = (key, self.cnt[eng])
        self.ops.append((eng, fn, list(deps), 1, key))
        return tok

    def dma(self, eng, out, in_, deps=()):
        slot = self.didx[eng] % NDMASLOT
        self.didx[eng] += 1
        key = f"D_{eng}{slot}"
        deps = list(deps) + ([self.dlast[key]] if key in self.dlast else [])
        self.dcnt[key] = self.dcnt.get(key, 0) + 16
        tok = (key, self.dcnt[key])
        self.dlast[key] = tok
        self.ops.append((eng, lambda E, o=out, i=in_: E.dma_start(out=o, in_=i),
                         deps, 16, key))
        return tok

    def coll(self, fn, deps=()):
        self.ccnt += 1
        key = "CC"
        deps = list(deps) + ([self.cclast] if self.cclast else [])
        tok = (key, self.ccnt)
        self.cclast = tok
        self.ops.append(("gpsimd", fn, deps, 1, key))
        return tok

    def barrier(self):
        toks = [("E_" + e, self.cnt[e]) for e in ENGINES if self.cnt[e] > 0]
        toks += [(k, v) for k, v in self.dcnt.items()]
        if self.ccnt:
            toks.append(("CC", self.ccnt))
        return toks

    def emit(self):
        nc = self.ctx.nc
        semof = {}
        for e in ENGINES:
            semof["E_" + e] = self.ctx.sem("E_" + e)[0]
        for k in self.dcnt:
            semof[k] = self.ctx.sem(k)[0]
        if self.ccnt:
            semof["CC"] = self.ctx.sem("CC")[0]
        ops = self.ops
        final_dma = dict(self.dcnt)
        final_cnt = dict(self.cnt)
        final_cc = self.ccnt

        def make(eng_name):
            def run(E):
                mykey = "E_" + eng_name
                waited = {}
                for (eng, fn, deps, inc, key) in ops:
                    if eng != eng_name:
                        continue
                    for (dk, dv) in deps:
                        if dv <= waited.get(dk, 0):
                            continue
                        E.wait_ge(semof[dk], dv)
                        waited[dk] = dv
                    instr = fn(E)
                    if isinstance(instr, list):
                        instr = instr[-1]
                    instr.then_inc(semof[key], inc)
                if final_cnt[eng_name] > 0:
                    E.wait_ge(semof[mykey], final_cnt[eng_name])
                for s in range(NDMASLOT):
                    k = f"D_{eng_name}{s}"
                    if k in final_dma:
                        E.wait_ge(semof[k], final_dma[k])
                if eng_name == "gpsimd" and final_cc:
                    E.wait_ge(semof["CC"], final_cc)
            return run

        with nc.Block("prog") as blk:
            blk.sync(make("sync"))
            blk.tensor(make("tensor"))
            blk.vector(make("vector"))
            blk.scalar(make("scalar"))
            blk.gpsimd(make("gpsimd"))


class Konst:
    def __init__(self, ctx, p, n=24):
        self.tile = sb(ctx, "konst", [128, n], F32)
        self.p = p
        self.vals = {}
        self.toks = []
        self.n = n

    def ap(self, v):
        v = float(v)
        if v not in self.vals:
            i = len(self.vals)
            assert i < self.n
            self.vals[v] = i
            self.toks.append(self.p.comp(
                "vector", lambda E, i=i, v=v: E.memset(self.tile[:, i:i + 1], v)))
        i = self.vals[v]
        return self.tile[:, i:i + 1]

    def row(self, v):
        return self.ap(v)[0:1, :]

    def dep(self):
        return list(self.toks)


class LnTmp:
    def __init__(self, ctx, T_, tag=""):
        self.T = T_
        self.sq = sb(ctx, f"ln_sq{tag}", [128, T_], F32)
        self.r_mean = sb(ctx, f"ln_rm{tag}", [1, T_], F32)
        self.r_msq = sb(ctx, f"ln_rq{tag}", [1, T_], F32)
        self.r_tmp = sb(ctx, f"ln_rt{tag}", [1, T_], F32)
        self.r_rstd = sb(ctx, f"ln_rr{tag}", [1, T_], F32)
        self.t1 = sb(ctx, f"ln_t1{tag}", [128, T_], F32)
        self.ps_s = ps(ctx, f"ln_ps_s{tag}", [1, T_], F32)
        self.ps_q = ps(ctx, f"ln_ps_q{tag}", [1, T_], F32)
        self.ps_mb = ps(ctx, f"ln_ps_mb{tag}", [128, T_], F32)
        self.ps_rb = ps(ctx, f"ln_ps_rb{tag}", [128, T_], F32)


def emit_ln(p, ln, xt, nt, ones_col, ones_row, out16, dep_x, kn, T_=None,
            out_f32=None, col0=0, ocol0=0):
    T_ = T_ or ln.T
    inv = 1.0 / (nt * 128)
    mm_toks = []
    sq_toks = []
    last_qmm = []
    for i in range(nt):
        tsq = p.comp("scalar", lambda E, i=i: E.activation(
            out=ln.sq[:, 0:T_], in_=xt[:, i, col0:col0 + T_], func=AF.Square),
            deps=dep_x + last_qmm + sq_toks[-1:])
        sq_toks.append(tsq)
        mm_toks.append(p.comp("tensor", lambda E, i=i: E.matmul(
            out=ln.ps_s[:, 0:T_], lhsT=ones_col, rhs=xt[:, i, col0:col0 + T_],
            start=(i == 0), stop=(i == nt - 1)), deps=dep_x))
        last_qmm = [p.comp("tensor", lambda E, i=i: E.matmul(
            out=ln.ps_q[:, 0:T_], lhsT=ones_col, rhs=ln.sq[:, 0:T_],
            start=(i == 0), stop=(i == nt - 1)), deps=[tsq])]
        mm_toks.append(last_qmm[0])
    mm = [mm_toks[-1]]
    a_inv = kn.row(inv)
    a_eps = kn.row(EPS)
    a_mh = kn.row(-0.5)
    kd = kn.dep()
    t_mean = p.comp("scalar", lambda E: E.activation(
        out=ln.r_mean[:, 0:T_], in_=ln.ps_s[:, 0:T_], func=AF.Copy,
        scale=a_inv), deps=mm + kd)
    t_msq = p.comp("scalar", lambda E: E.activation(
        out=ln.r_msq[:, 0:T_], in_=ln.ps_q[:, 0:T_], func=AF.Copy,
        scale=a_inv), deps=mm + kd)
    t_m2 = p.comp("scalar", lambda E: E.activation(
        out=ln.r_tmp[:, 0:T_], in_=ln.r_mean[:, 0:T_], func=AF.Square), deps=[t_mean])
    t_var = p.comp("vector", lambda E: E.tensor_sub(
        out=ln.r_msq[:, 0:T_], in0=ln.r_msq[:, 0:T_], in1=ln.r_tmp[:, 0:T_]),
        deps=[t_msq, t_m2])
    t_ln = p.comp("scalar", lambda E: E.activation(
        out=ln.r_tmp[:, 0:T_], in_=ln.r_msq[:, 0:T_], func=AF.Ln,
        bias=a_eps), deps=[t_var] + kd)
    t_rstd = p.comp("scalar", lambda E: E.activation(
        out=ln.r_rstd[:, 0:T_], in_=ln.r_tmp[:, 0:T_], func=AF.Exp,
        scale=a_mh), deps=[t_ln] + kd)
    t_mb = p.comp("tensor", lambda E: E.matmul(
        out=ln.ps_mb[:, 0:T_], lhsT=ones_row, rhs=ln.r_mean[:, 0:T_],
        start=True, stop=True), deps=[t_mean, mm[0]])
    t_rb = p.comp("tensor", lambda E: E.matmul(
        out=ln.ps_rb[:, 0:T_], lhsT=ones_row, rhs=ln.r_rstd[:, 0:T_],
        start=True, stop=True), deps=[t_rstd])
    outdeps = []
    for i in range(nt):
        ta = p.comp("vector", lambda E, i=i: E.tensor_sub(
            out=ln.t1[:, 0:T_], in0=xt[:, i, col0:col0 + T_], in1=ln.ps_mb[:, 0:T_]),
            deps=dep_x + [t_mb] + outdeps[-1:])
        tb = p.comp("vector", lambda E, i=i: E.tensor_mul(
            out=out16[:, i, ocol0:ocol0 + T_], in0=ln.t1[:, 0:T_], in1=ln.ps_rb[:, 0:T_]),
            deps=[ta, t_rb])
        outdeps.append(tb)
        if out_f32 is not None:
            outdeps.append(p.comp("vector", lambda E, i=i: E.tensor_mul(
                out=out_f32[:, i, 0:T_], in0=ln.t1[:, 0:T_], in1=ln.ps_rb[:, 0:T_]),
                deps=[ta, t_rb]))
    return outdeps, (t_mb, t_rb)


class MS2:

    def __init__(self):
        self.bank_rd = {0: [], 1: []}


def mm_stream2(p, st, stg, w16, w_ap, nkt, nmt, rhs_ap, evac, banks, dep_rhs,
               bar, ext=None, scr=None, first=True):
    queues = ("sync", "scalar")
    if not hasattr(p, "_ms2cv"):
        p._ms2cv = []
        p._ms2rd = {0: [], 1: []}
    if not hasattr(p, "_ms2st"):
        p._ms2st = {}
    cv = p._ms2cv
    out_toks = []
    for mt in range(nmt):
        g = len(cv)
        q = g % 2
        sl = g % 2
        if scr is not None and not first:
            skey = (id(scr), mt)
            tld = p.dma(queues[q], w16[sl][:, 0:nkt, :], scr[mt],
                        deps=p._ms2rd[sl] + p._ms2st.get(skey, []) +
                        ([cv[-2]] if len(cv) >= 2 else bar))
            tcv = tld
        else:
            tld = p.dma(queues[q], stg[sl][:, 0:nkt, :], w_ap(mt),
                        deps=([cv[-2]] if len(cv) >= 2 else bar))
            tcv = p.comp("vector", lambda E, sl=sl, nkt=nkt: E.tensor_copy(
                out=w16[sl][:, 0:nkt, :], in_=stg[sl][:, 0:nkt, :]),
                deps=[tld] + p._ms2rd[sl])
            if scr is not None:
                skey = (id(scr), mt)
                tst = p.dma(queues[1 - q], scr[mt], w16[sl][:, 0:nkt, :],
                            deps=[tcv])
                p._ms2st[skey] = [tst]
                p._ms2rd[sl] = p._ms2rd[sl] + [tst]
        cv.append(tcv)
        bank = banks[mt % 2]
        tmm = None
        for kt in range(nkt):
            deps = [tcv]
            if kt == 0:
                deps = deps + dep_rhs + st.bank_rd[mt % 2]
            tmm = p.comp("tensor", lambda E, sl=sl, kt=kt, bank=bank: E.matmul(
                out=bank[:, :], lhsT=w16[sl][:, kt, :], rhs=rhs_ap(kt),
                start=(kt == 0), stop=(kt == nkt - 1)), deps=deps)
        tev = evac(mt, bank, [tmm])
        out_toks.append(tev)
        st.bank_rd[mt % 2] = [tev]
        p._ms2rd[sl] = [tmm]
        if ext is not None and mt < ext[4]:
            rhs2_ap, evac2, n2, dep_rhs2 = ext[0], ext[1], ext[2], ext[3]
            tm2 = None
            for kt in range(nkt):
                deps = [tcv]
                if kt == 0:
                    deps = deps + list(dep_rhs2) + [tev]
                tm2 = p.comp("tensor", lambda E, sl=sl, kt=kt, bank=bank, n2=n2: E.matmul(
                    out=bank[:, 0:n2], lhsT=w16[sl][:, kt, :], rhs=rhs2_ap(kt),
                    start=(kt == 0), stop=(kt == nkt - 1)), deps=deps)
            tev2 = evac2(mt, bank, [tm2])
            out_toks.append(tev2)
            st.bank_rd[mt % 2] = [tev, tev2]
            p._ms2rd[sl] = [tm2]
    return out_toks


def mm_stream(p, stages, w16s, w_ap, nkt, nmt, rhs_ap, evac, banks, dep_rhs,
              bar, ktp=128, nout=None):
    queues = ("sync", "gpsimd")
    ev_hist = []
    if not hasattr(p, "_mm_cv"):
        p._mm_cv = []
        p._mm_mm = []
    cv_hist = p._mm_cv
    mm_hist = p._mm_mm
    out_toks = []
    for mt in range(nmt):
        bank = banks[mt % 2]
        tmm = None
        for kt in range(nkt):
            g = len(cv_hist)
            q = g % 2
            s = g % 4
            tld = p.dma(queues[q], stages[q][:], w_ap(kt, mt),
                        deps=(cv_hist[-2:-1] if len(cv_hist) >= 2 else bar))
            tcv = p.comp("vector", lambda E, s=s, q=q: E.tensor_copy(
                out=w16s[:, s, :], in_=stages[q][:]),
                deps=[tld] + (mm_hist[-4:-3] if len(mm_hist) >= 4 else []))
            cv_hist.append(tcv)
            deps = [tcv]
            if kt == 0:
                deps += dep_rhs + ev_hist[-2:-1]
            tmm = p.comp("tensor", lambda E, s=s, kt=kt, bank=bank: E.matmul(
                out=bank[:, :] if nout is None else bank[:, 0:nout],
                lhsT=w16s[:, s, :], rhs=rhs_ap(kt),
                start=(kt == 0), stop=(kt == nkt - 1)), deps=deps)
            mm_hist.append(tmm)
        tev = evac(mt, bank, [tmm])
        ev_hist.append(tev)
        out_toks.append(tev)
    return out_toks




def emit_gather(p, ctx, G, src_ap, cc_in, cc_out, ncols, deps):
    slot = [sb(ctx, f"agslot{i}", [128, ncols], F32) for i in range(2)]
    slot_rd = {0: [], 1: []}
    dts = []
    for r in range(8):
        sl = r % 2
        tv = p.comp("vector", lambda E, r=r, sl=sl: E.tensor_scalar(
            out=slot[sl][:, :], in0=src_ap, scalar1=G.ownm[:, r:r + 1],
            scalar2=None, op0=ALU.mult), deps=deps + slot_rd[sl])
        td = p.dma("gpsimd", cc_in[r * 128:(r + 1) * 128, :], slot[sl][:, :],
                   deps=[tv])
        slot_rd[sl] = [td]
        dts.append(td)
    tc = p.coll(lambda E: E.collective_compute(
        "AllReduce", ALU.add, replica_groups=RG,
        ins=[cc_in.opt()], outs=[cc_out.opt()]), deps=dts)
    return tc


def emit_fold(p, ctx, G, cc_out, Htile, tc):
    fi = [sb(ctx, f"fold{i}", [128, 1024], F32) for i in range(2)]
    pmt = sb(ctx, "pmt", [128, 512], F32)
    t0 = p.comp("vector", lambda E: E.memset(Htile[:, :], 0.0), deps=[tc])
    fi_rd = {0: [], 1: []}
    th2 = t0
    for r in range(8):
        sl = r % 2
        tl = p.dma("sync", fi[sl][:, :], cc_out[r * 128:(r + 1) * 128, :],
                   deps=[tc] + fi_rd[sl])
        tpm = p.comp("vector", lambda E, r=r, sl=sl: E.tensor_scalar(
            out=pmt[:, :], in0=fi[sl][:, 512:1024], scalar1=G.pm1[:, r:r + 1],
            scalar2=G.pm0[:, r:r + 1], op0=ALU.mult, op1=ALU.add),
            deps=[tl, th2])
        th1 = p.comp("vector", lambda E: E.tensor_mul(
            out=Htile[:, :], in0=Htile[:, :], in1=pmt[:, :]), deps=[tpm])
        th2 = p.comp("vector", lambda E, r=r, sl=sl: E.scalar_tensor_tensor(
            out=Htile[:, :], in0=fi[sl][:, 0:512], scalar=G.pm1[:, r:r + 1],
            in1=Htile[:, :], op0=ALU.mult, op1=ALU.add), deps=[th1])
        fi_rd[sl] = [th2]
    return [th2]




NR = 3


def scan_A(p, ctx, mb, ch, delta16, xi_t, bbc16, cbc16, deltad, bc_d, conv_d, wc):
    b, j = ch // 2, ch % 2
    st_toks = []
    for dt in range(16):
        dsl = dt % 2
        init_ds = mb.kzero if j == 0 else mb.dstate[:, dt:dt + 1]
        t_ds = p.comp("vector", lambda E, dt=dt, dsl=dsl, init_ds=init_ds: E.tensor_tensor_scan(
            out=mb.Ds[:, dsl, :], data0=delta16[:, dt, :], data1=mb.z16zero[:],
            initial=init_ds, op0=ALU.add, op1=ALU.add),
            deps=deltad + wc + mb.ds_rd[dsl])
        t_dss = p.comp("vector", lambda E, dt=dt, dsl=dsl: E.tensor_copy(
            out=mb.dstate[:, dt:dt + 1], in_=mb.Ds[:, dsl, T - 1:T]), deps=[t_ds])
        t_sds = p.dma("sync", mb.scr_ds[ch, dt], mb.Ds[:, dsl, :], deps=[t_ds])
        st_toks.append(t_sds)
        mb.ds_rd[dsl] = [t_dss, t_sds]
        t_du = p.comp("vector", lambda E, dt=dt, dsl=dsl: E.tensor_mul(
            out=mb.du16[:, dsl, :], in0=delta16[:, dt, :], in1=xi_t[:, dt, :]),
            deps=deltad + conv_d + mb.du_rd[dsl])
        t_h = None
        t_hv = {0: None, 1: None}
        for n in range(16):
            sl = mb.g % NR
            s2 = mb.g % 2
            mb.g += 1
            acol_ap = mb.acol_fn(dt, n)
            t_dA = p.comp("scalar", lambda E, dt=dt, sl=sl, acol_ap=acol_ap: E.activation(
                out=mb.dA[:, sl, :], in_=delta16[:, dt, :], func=AF.Exp,
                scale=acol_ap), deps=deltad + wc + mb.da_rd[sl])
            t_u = p.comp("vector", lambda E, n=n, s2=s2, dsl=dsl: E.tensor_mul(
                out=mb.u16[:, s2, :], in0=mb.du16[:, dsl, :], in1=bbc16[:, n, :]),
                deps=[t_du] + bc_d + mb.u_rd[s2])
            scol = mb.spS(b, dt, n)
            init_h = mb.kzero if j == 0 else scol
            eng_h = "vector"
            t_h = p.comp(eng_h, lambda E, sl=sl, s2=s2, init_h=init_h: E.tensor_tensor_scan(
                out=mb.hS[:, s2, :], data0=mb.dA[:, sl, :], data1=mb.u16[:, s2, :],
                initial=init_h, op0=ALU.mult, op1=ALU.add),
                deps=[t_dA, t_u] + mb.h_rd[s2])
            t_hv[s2] = t_h
            mb.da_rd[sl] = [t_h]
            mb.u_rd[s2] = [t_h]
            t_cp = p.comp("gpsimd", lambda E, s2=s2, scol=scol: E.tensor_copy(
                out=scol, in_=mb.hS[:, s2, T - 1:T]), deps=[t_h])
            eng_t = "gpsimd"
            t_tmp = p.comp(eng_t, lambda E, n=n, sl=sl, s2=s2: E.tensor_mul(
                out=mb.tmp16[:, sl, :], in0=mb.hS[:, s2, :], in1=cbc16[:, n, :]),
                deps=[t_h] + bc_d + mb.tmp_rd[sl])
            mb.h_rd[s2] = [t_cp, t_tmp]
            ymm = p.comp("tensor", lambda E, n=n, sl=sl: E.matmul(
                out=mb.ybank[:, :], lhsT=mb.eye16[:], rhs=mb.tmp16[:, sl, :],
                start=(n == 0), stop=(n == 15)),
                deps=[t_tmp] + wc + (mb.last_ya if n == 0 else []))
            mb.tmp_rd[sl] = [ymm]
            mb.cbc_rd = [t_tmp]
        mb.du_rd[dsl] = [t for t in t_hv.values() if t is not None]
        ysl = dt % 2
        t_ev = p.comp("scalar", lambda E, ysl=ysl: E.activation(
            out=mb.ysb[:, ysl, :], in_=mb.ybank[:, :], func=AF.Copy),
            deps=mb.tmp_rd[(mb.g - 1) % NR] + mb.ysb_rd[ysl])
        mb.last_ya = [t_ev]
        t_sy = p.dma("scalar", mb.scr_y[ch, dt], mb.ysb[:, ysl, :], deps=[t_ev])
        mb.ysb_rd[ysl] = [t_sy]
        st_toks.append(t_sy)
        if j == 1:
            pts = []
            for n in range(16):
                pcol = mb.spP(b, dt, n)
                pts.append(p.comp("scalar", lambda E, pcol=pcol, dsl=dsl,
                                  acol_ap=mb.acol_fn(dt, n): E.activation(
                    out=pcol, in_=mb.Ds[:, dsl, T - 1:T], func=AF.Exp,
                    scale=acol_ap), deps=[t_ds] + wc))
            mb.ds_rd[dsl] = mb.ds_rd[dsl] + pts
    return st_toks


def corr_B(p, ctx, mb, ch, cbc16, y2, bar, wcH, bc_d, ldbar=None):
    b = ch // 2
    lb = bar if ldbar is None else ldbar
    y2d = []
    for dt in range(16):
        s2 = dt % 2
        tld_ds = p.dma("sync", mb.dsb[:, s2, :], mb.scr_ds[ch, dt],
                       deps=lb + mb.dsb_rd[s2])
        tld_y = p.dma("scalar", mb.yacc[:, s2, :], mb.scr_y[ch, dt],
                      deps=lb + mb.yacc_rd[s2])
        fma_last = [tld_y]
        t_dec = None
        eng = "vector" if s2 == 0 else "gpsimd"
        for n in range(16):
            sl = mb.g % NR
            mb.g += 1
            t_dec = p.comp("scalar", lambda E, s2=s2, sl=sl, acol_ap=mb.acol_fn(dt, n): E.activation(
                out=mb.dec[:, sl, :], in_=mb.dsb[:, s2, :], func=AF.Exp,
                scale=acol_ap), deps=[tld_ds] + mb.dec_rd[sl])
            st = s2 * 2 + (n % 2)
            eng_t2 = "vector" if (s2 == 0 and n % 3 == 0) else "gpsimd"
            t_t2 = p.comp(eng_t2, lambda E, n=n, sl=sl, st=st: E.tensor_mul(
                out=mb.tmpc2[:, st, :], in0=mb.dec[:, sl, :], in1=cbc16[:, n, :]),
                deps=[t_dec] + bc_d + mb.tmpc_rd[st])
            mb.dec_rd[sl] = [t_t2]
            hcol = mb.Hcol(b, dt, n)
            t_fma = p.comp("vector", lambda E, s2=s2, st=st, hcol=hcol: E.scalar_tensor_tensor(
                out=mb.yacc[:, s2, :], in0=mb.tmpc2[:, st, :], scalar=hcol,
                in1=mb.yacc[:, s2, :], op0=ALU.mult, op1=ALU.add),
                deps=[t_t2] + fma_last + wcH)
            fma_last = [t_fma]
            mb.tmpc_rd[st] = [t_fma]
        mb.dsb_rd[s2] = [t_dec]
        if mb.is_l0:
            tld_xi = p.dma("sync", mb.xib[:, s2, :], mb.scr_xi[ch, dt],
                           deps=lb + mb.xib_rd[s2])
            tld_z = p.dma("scalar", mb.zb[:, s2, :], mb.scr_z[ch, dt],
                          deps=lb + mb.zb_rd[s2])
            t_sz = p.comp("scalar", lambda E, s2=s2: E.activation(
                out=mb.szb[:, s2, :], in_=mb.zb[:, s2, :], func=AF.Silu),
                deps=[tld_z] + mb.szb_rd[s2])
            t_ya = p.comp("vector", lambda E, s2=s2, dt=dt: E.scalar_tensor_tensor(
                out=mb.yacc[:, s2, :], in0=mb.xib[:, s2, :],
                scalar=mb.d_c[:, dt:dt + 1], in1=mb.yacc[:, s2, :],
                op0=ALU.mult, op1=ALU.add), deps=[tld_xi] + fma_last)
            t_y2 = p.comp(eng, lambda E, s2=s2, dt=dt: E.tensor_mul(
                out=y2[:, dt, :], in0=mb.yacc[:, s2, :], in1=mb.szb[:, s2, :]),
                deps=[t_ya, t_sz])
            mb.xib_rd[s2] = [t_ya]
            mb.zb_rd[s2] = [t_sz]
            mb.szb_rd[s2] = [t_y2]
            mb.yacc_rd[s2] = [t_y2]
        else:
            tld_g = p.dma("sync", mb.xib[:, s2, :], mb.scr_z[ch, dt],
                          deps=lb + mb.xib_rd[s2])
            t_y2 = p.comp(eng, lambda E, s2=s2, dt=dt: E.tensor_mul(
                out=y2[:, dt, :], in0=mb.yacc[:, s2, :], in1=mb.xib[:, s2, :]),
                deps=[tld_g] + fma_last)
            mb.xib_rd[s2] = [t_y2]
            mb.yacc_rd[s2] = [t_y2]
        y2d.append(t_y2)
    return y2d


def emit_cbc(p, ctx, mb, ch, cbc16, bar, extra=()):
    bc_d = []
    bmm = []
    for n in range(16):
        tmv = p.dma("sync", mb.bc1[0:1, n % 2, 0:T], mb.scr_cr[ch, n],
                    deps=bar + bmm[-2:-1] + list(extra))
        tb = p.comp("tensor", lambda E, n=n: E.matmul(
            out=mb.bcbank[:, :], lhsT=mb.ones_row[:], rhs=mb.bc1[0:1, n % 2, 0:T],
            start=True, stop=True), deps=[tmv] + bc_d[-1:])
        bmm.append(tb)
        bc_d.append(p.comp("scalar", lambda E, n=n: E.activation(
            out=cbc16[:, n, :], in_=mb.bcbank[:, :], func=AF.Copy),
            deps=[tb]))
    return bc_d


class MB:
    pass



_CACHE = {}


def build_fused(stop_after=99):
    nc = bass.Bass("TRN2", target_bir_lowering=False)

    def din(name, shape, dtype=F32):
        return nc.dram_tensor(name, shape, dtype, kind="ExternalInput").ap()

    x0e = din("x0e", [D, 2 * (SEG + 3)])
    eye_in = din("eye", [128, 128])
    ownm_in = din("ownm", [128, 8])
    pm1_in = din("pm1", [128, 8])
    hsel_in = din("hsel", [128, 8])
    xt_out = nc.dram_tensor("xt_new", [D, TOK], F32, kind="ExternalOutput").ap()
    dbg = (nc.dram_tensor("dbg", [D, TOK], F32, kind="ExternalOutput").ap()
           if stop_after < 99 else None)
    ip_in = din("ip2", [32, 128, 8, 128])
    op_in = din("op2", [8, 128, 16, 128])
    xp_in = din("xp_l", [128, 16, 96])
    dtw_in = din("dtw_l", [64, 16, 128])
    rms_in = din("rms_c", [128, 8])
    convw_in = din("convw_c", [128, 16, 4])
    convb_in = din("convb_c", [128, 16])
    dtb_in = din("dtb_c", [128, 16])
    alog_in = din("alog_c", [128, 16, 16])
    d_in = din("d_c", [128, 16])
    win_in = din("win_l", [128, 8, 64])
    bin_in = din("bin_c", [64, 1])
    w1_in = din("w1", [64, 16])
    b1_in = din("b1", [64, 16])
    w2_in = din("w2", [64, 16])
    b2_in = din("b2", [64, 1])
    wout_in = din("wout_l", [64, 8, 128])
    bout_in = din("bout_c", [128, 8])
    wq_in = din("wq_t", [128, 8, D])
    wk_in = din("wk_t", [128, 8, D])
    wv_in = din("wv_t", [128, 8, D])
    bq_in = din("bq_c", [128, 8])
    bk_in = din("bk_c", [128, 8])
    bv_in = din("bv_r", [1, D])
    wqkvb_in = din("wqkvb2", [25, 128, 8, 128])
    woA_in = din("woA2", [8, 128, 8, 128])
    lngA_in = din("lng_c", [128, 8])
    lnbA_in = din("lnb_c", [128, 8])
    up_in = din("up_t", [8, 16, 128, 8, 128])
    dn_in = din("dn_t", [8, 8, 128, 8, 128])
    rw_in = din("rw_l", [8, 128, 8])
    wl_in = din("wl2", [16, 128, 8, 128])
    wd_in = din("wd2", [16, 128, 16, 128])
    wr_in = din("wr2", [16, 128, 8, 128])
    wo2_in = din("wo22", [8, 128, 16, 128])
    wb_in = din("wb_l", [16, 128, 16])
    wcc_in = din("wc_l", [16, 128, 16])
    convw2_in = din("convw2_c", [128, 16, 4])
    convb2_in = din("convb2_c", [128, 16])
    bdel_in = din("bdel_c", [128, 16])
    alog2_in = din("alog_r", [1, 16])
    lng2_in = din("lng2_c", [128, 16])
    lnb2_in = din("lnb2_c", [128, 16])

    sA = nc.dram_tensor("sA", [D, TOK], F32).ap()
    sB = nc.dram_tensor("sB", [D, TOK], F32).ap()
    scr_y = nc.dram_tensor("scr_y", [NCH, 16, 128, T], F32).ap()
    scr_ds = nc.dram_tensor("scr_ds", [NCH, 16, 128, T], F32).ap()
    scr_xi = nc.dram_tensor("scr_xi", [NCH, 16, 128, T], BF16).ap()
    scr_z = nc.dram_tensor("scr_z", [NCH, 16, 128, T], BF16).ap()
    scr_cr = nc.dram_tensor("scr_cr", [NCH, 16, 1, T], F32).ap()
    cc1_in = nc.dram_tensor("cc1_in", [1024, 1024], F32).ap()
    cc1_out = nc.dram_tensor("cc1_out", [1024, 1024], F32).ap()
    cc2_in = nc.dram_tensor("cc2_in", [1024, 1024], F32).ap()
    cc2_out = nc.dram_tensor("cc2_out", [1024, 1024], F32).ap()
    cc3_in = nc.dram_tensor("cc3_in", [1024, 48], F32).ap()
    cc3_out = nc.dram_tensor("cc3_out", [1024, 48], F32).ap()
    ip16s = nc.dram_tensor("ip16s", [32, 128, 8, 128], BF16).ap()
    op16s = nc.dram_tensor("op16s", [8, 128, 16, 128], BF16).ap()
    woA16s = nc.dram_tensor("woA16s", [8, 128, 8, 128], BF16).ap()
    wl16s = nc.dram_tensor("wl16s", [16, 128, 8, 128], BF16).ap()
    wr16s = nc.dram_tensor("wr16s", [16, 128, 8, 128], BF16).ap()
    wd16s = nc.dram_tensor("wd16s", [16, 128, 16, 128], BF16).ap()
    wo216s = nc.dram_tensor("wo216s", [8, 128, 16, 128], BF16).ap()
    qkvb16s = nc.dram_tensor("qkvb16s", [25, 128, 8, 128], BF16).ap()
    up16s = nc.dram_tensor("up16s", [8, 16, 128, 8, 128], BF16).ap()
    dn16s = nc.dram_tensor("dn16s", [8, 8, 128, 8, 128], BF16).ap()

    with contextlib.ExitStack() as stack:
        ctx = Ctx(nc, stack)
        p = P(ctx)
        kn = Konst(ctx, p)
        G = MB()
        G.ownm = sb(ctx, "ownm", [128, 8], F32)
        G.pm1 = sb(ctx, "pm1", [128, 8], F32)
        G.pm0 = sb(ctx, "pm0", [128, 8], F32)
        G.hsel = sb(ctx, "hsel", [128, 8], F32)
        eyef = sb(ctx, "eyef", [128, 128], F32)
        eye16 = sb(ctx, "eye16", [128, 128], BF16)
        ones_col = sb(ctx, "ones_col", [128, 1], F32)
        ones_row = sb(ctx, "ones_row", [1, 128], F32)
        sp_all = sb(ctx, "sp_all", [128, 1024], F32)
        Htile = sb(ctx, "Htile", [128, 512], F32)
        bsel = sb(ctx, "bsel", [128, 8, 6], F32)
        stg2 = [sb(ctx, f"stg2_{i}", [128, 8, 128], F32) for i in range(2)]
        w16b = [sb(ctx, f"w16b_{i}", [128, 8, 128], BF16) for i in range(2)]

        wl0 = [p.dma("sync", G.ownm[:], ownm_in),
               p.dma("sync", G.pm1[:], pm1_in),
               p.dma("sync", G.hsel[:], hsel_in),
               p.dma("sync", eyef[:], eye_in)]
        wc0 = [p.comp("vector", lambda E: E.tensor_copy(out=eye16[:], in_=eyef[:]), deps=wl0),
               p.comp("vector", lambda E: E.tensor_scalar(
                   out=G.pm0[:], in0=G.pm1[:], scalar1=-1.0, scalar2=1.0,
                   op0=ALU.mult, op1=ALU.add), deps=wl0),
               p.comp("vector", lambda E: E.memset(ones_col[:], 1.0)),
               p.comp("vector", lambda E: E.memset(ones_row[:], 1.0))]
        kzero = kn.ap(0.0)

        def spcol(q, dt, n):
            return sp_all[:, q * 256 + dt * 16 + n: q * 256 + dt * 16 + n + 1]

        def emit_l0_mamba():
            mb = MB()
            mb.is_l0 = True
            mb.kzero = kzero
            mb.eye16 = eye16
            mb.ones_row = ones_row
            mb.scr_y, mb.scr_ds = scr_y, scr_ds
            mb.scr_xi, mb.scr_z, mb.scr_cr = scr_xi, scr_z, scr_cr
            mb.spS = lambda b, dt, n: spcol(b, dt, n)
            mb.spP = lambda b, dt, n: spcol(2 + b, dt, n)
            mb.Hcol = lambda b, dt, n: Htile[:, b * 256 + dt * 16 + n:
                                             b * 256 + dt * 16 + n + 1]
            with ctx.scope():
                bar0 = p.barrier()
                xpst = sb(ctx, "xpst", [128, 16, 96], F32)
                xp16 = sb(ctx, "xp16", [128, 16, 96], BF16)
                dtwst = sb(ctx, "dtwst", [64, 16, 128], F32)
                dtw16 = sb(ctx, "dtw16", [64, 16, 128], BF16)
                rms_c = sb(ctx, "rms_c", [128, 8], F32)
                convw = sb(ctx, "convw", [128, 16, 4], F32)
                convb = sb(ctx, "convb", [128, 16], F32)
                dtb = sb(ctx, "dtb", [128, 16], F32)
                acol = sb(ctx, "acol", [128, 16, 16], F32)
                mb.d_c = sb(ctx, "d_c", [128, 16], F32)
                mb.acol_fn = lambda dt, n: acol[:, dt, n:n + 1]
                wl = [p.dma("sync", dst, src, deps=bar0) for dst, src in [
                    (xpst[:], xp_in), (dtwst[:], dtw_in), (rms_c[:], rms_in),
                    (convw[:], convw_in), (convb[:], convb_in), (dtb[:], dtb_in),
                    (acol[:], alog_in), (mb.d_c[:], d_in)]]
                wc = [p.comp("scalar", lambda E: E.activation(out=xp16[:], in_=xpst[:], func=AF.Copy), deps=wl),
                      p.comp("scalar", lambda E: E.activation(out=dtw16[:], in_=dtwst[:], func=AF.Copy), deps=wl)]
                t_a = p.comp("scalar", lambda E: E.activation(
                    out=acol[:], in_=acol[:], func=AF.Exp), deps=wl)
                t_a = p.comp("vector", lambda E: E.tensor_scalar(
                    out=acol[:], in0=acol[:], scalar1=-1.0, scalar2=None,
                    op0=ALU.mult), deps=[t_a])
                wc.append(t_a)
                wc += wc0

                def _phaseA():
                    xtc = sb(ctx, "xtc", [128, 8, TE], F32)
                    xin16 = sb(ctx, "xin16", [128, 8, T], BF16)
                    xn2 = sb(ctx, "xn2", [128, 8, T], BF16)
                    xn2e = sb(ctx, "xn2e", [128, 8, 3], BF16)
                    xipre = sb(ctx, "xipre", [128, 16, TE], BF16)
                    xi2 = sb(ctx, "xi2", [128, 16, T], BF16)
                    z16 = sb(ctx, "z16", [128, 2, T], BF16)
                    delta16 = sb(ctx, "delta16", [128, 16, T], BF16)
                    bbc16 = sb(ctx, "bbc16", [128, 16, T], BF16)
                    cbc16 = sb(ctx, "cbc16", [128, 16, T], BF16)
                    dtT16 = sb(ctx, "dtT16", [64, T], BF16)
                    bc1 = sb(ctx, "bc1", [1, 2, T], F32)
                    mb.bc1 = bc1
                    dsc = sb(ctx, "dsc", [128, 1, T], F32)
                    mb.du16 = sb(ctx, "du16", [128, 2, T], BF16)
                    mb.u16 = sb(ctx, "u16", [128, 2, T], BF16)
                    mb.tmp16 = sb(ctx, "tmp16", [128, NR, T], BF16)
                    mb.dA = sb(ctx, "dA", [128, NR, T], F32)
                    mb.hS = sb(ctx, "hS", [128, 2, T], F32)
                    mb.Ds = sb(ctx, "Ds", [128, 2, T], F32)
                    mb.z16zero = sb(ctx, "zz", [128, T], BF16)
                    mb.dstate = sb(ctx, "dstate", [128, 16], F32)
                    mb.ysb = sb(ctx, "ysb", [128, 2, T], F32)
                    ln = LnTmp(ctx, T)
                    cacc = ln.t1
                    bcst = ln.t1
                    wkps = [ps(ctx, f"wkps{i}", [128, T], F32) for i in range(2)]
                    mb.ybank = ps(ctx, "ybank", [128, T], F32)
                    mb.bcbank = ps(ctx, "bcbank", [128, T], F32)
                    mb.g = 0
                    mb.last_ya = []
                    mb.ysb_rd = {0: [], 1: []}
                    mb.ds_rd = {0: [], 1: []}
                    mb.du_rd = {0: [], 1: []}
                    mb.da_rd = {s: [] for s in range(NR)}
                    mb.u_rd = {0: [], 1: []}
                    mb.h_rd = {0: [], 1: []}
                    mb.tmp_rd = {s: [] for s in range(NR)}
                    ms = MS2()
                    wc2 = wc + [
                        p.comp("vector", lambda E: E.memset(mb.z16zero[:], 0.0)),
                        p.comp("vector", lambda E: E.memset(xipre[:, :, 0:3], 0.0))]

                    a_eps = kn.row(EPS)
                    a_mh = kn.row(-0.5)
                    a_i1024 = kn.row(1.0 / 1024.0)
                    kd = kn.dep()

                    def rms_pass(src16, nt, Tr, dst16, deps0):
                        sq_t, last_q = [], []
                        for i in range(nt):
                            tq = p.comp("scalar", lambda E, i=i: E.activation(
                                out=ln.sq[:, 0:Tr], in_=src16[:, i, 0:Tr], func=AF.Square),
                                deps=deps0 + last_q + sq_t[-1:])
                            sq_t.append(tq)
                            last_q = [p.comp("tensor", lambda E, i=i: E.matmul(
                                out=ln.ps_q[:, 0:Tr], lhsT=ones_col[:], rhs=ln.sq[:, 0:Tr],
                                start=(i == 0), stop=(i == nt - 1)), deps=[tq])]
                        tms = p.comp("scalar", lambda E: E.activation(
                            out=ln.r_msq[:, 0:Tr], in_=ln.ps_q[:, 0:Tr], func=AF.Copy,
                            scale=a_i1024), deps=last_q + kd)
                        tl2 = p.comp("scalar", lambda E: E.activation(
                            out=ln.r_tmp[:, 0:Tr], in_=ln.r_msq[:, 0:Tr], func=AF.Ln,
                            bias=a_eps), deps=[tms])
                        tr2 = p.comp("scalar", lambda E: E.activation(
                            out=ln.r_rstd[:, 0:Tr], in_=ln.r_tmp[:, 0:Tr], func=AF.Exp,
                            scale=a_mh), deps=[tl2])
                        trb = p.comp("tensor", lambda E: E.matmul(
                            out=ln.ps_rb[:, 0:Tr], lhsT=ones_row[:], rhs=ln.r_rstd[:, 0:Tr],
                            start=True, stop=True), deps=[tr2])
                        outd = []
                        for i in range(nt):
                            outd.append(p.comp("vector", lambda E, i=i: E.scalar_tensor_tensor(
                                out=dst16[:, i, 0:Tr], in0=src16[:, i, 0:Tr],
                                scalar=rms_c[:, i:i + 1], in1=ln.ps_rb[:, 0:Tr],
                                op0=ALU.mult, op1=ALU.mult), deps=[trb] + deps0 + wl + outd[-1:]))
                        return outd

                    zrd = {0: [], 1: []}
                    xist = {}
                    mb.t1_rd = []
                    mb.cbc_rd = []
                    mb.xtc_free = []

                    def frontA(ch):
                        b, j = ch // 2, ch % 2
                        cb = b * SEG + j * T
                        eb = b * (SEG + 3) + j * T
                        bar = p.barrier() if ch == 0 else []
                        ldx = [p.dma("sync", xtc[:], x0e[:, eb:eb + TE].rearrange(
                            "(a p) t -> p a t", p=128), deps=bar + mb.xtc_free)]
                        lnd, _ = emit_ln(p, ln, xtc, 8, ones_col[:], ones_row[:],
                                         xin16, ldx + mb.t1_rd, kn, T_=T, col0=3)
                        rdeps = []
                        stt_ = []
                        for i in range(8):
                            tr_ = p.comp("vector", lambda E, i=i: E.tensor_add(
                                out=xtc[:, i, 3:3 + T], in0=xtc[:, i, 3:3 + T],
                                in1=xin16[:, i, :]), deps=lnd)
                            rdeps.append(tr_)
                            stt_.append(p.dma("gpsimd",
                                              sA[i * 128:(i + 1) * 128, cb:cb + T],
                                              xtc[:, i, 3:3 + T], deps=[tr_]))
                        xnd = rms_pass(xin16, 8, T, xn2, lnd)
                        xne = None
                        lnde = []
                        if j == 0:
                            lnde, _ = emit_ln(p, ln, xtc, 8, ones_col[:], ones_row[:],
                                              xn2e, ldx + xnd + rdeps + mb.t1_rd,
                                              kn, T_=3, col0=0)
                            xne = rms_pass(xn2e, 8, 3, xn2e, lnde)
                        mb.xtc_free = rdeps + stt_ + lnd[-1:] + lnde[-1:]

                        def ip_evac(mt, bank, deps, ch=ch):
                            if mt < 16:
                                return p.comp("vector", lambda E, mt=mt, bank=bank: E.tensor_copy(
                                    out=xipre[:, mt, 3:3 + T], in_=bank[:, :]), deps=deps)
                            zi = mt - 16
                            t = p.comp("scalar", lambda E, zi=zi, bank=bank: E.activation(
                                out=z16[:, zi % 2, :], in_=bank[:, :], func=AF.Copy),
                                deps=deps + zrd[zi % 2])
                            td = p.dma("scalar", scr_z[ch, zi], z16[:, zi % 2, :],
                                       deps=[t])
                            zrd[zi % 2] = [td]
                            return t

                        def ipe_evac(mt, bank, deps):
                            return p.comp("scalar", lambda E, mt=mt, bank=bank: E.activation(
                                out=xipre[:, mt, 0:3], in_=bank[:, 0:3], func=AF.Copy), deps=deps)

                        ipd = mm_stream2(p, ms, stg2, w16b, lambda mt: ip_in[mt],
                                         8, 32, lambda kt: xn2[:, kt, :], ip_evac,
                                         wkps, xnd, bar,
                                         ext=((lambda kt: xn2e[:, kt, :]), ipe_evac,
                                              3, xne, 16) if j == 0 else None,
                                         scr=ip16s, first=(ch == 0))
                        return dict(ch=ch, b=b, j=j, cb=cb, ipd=ipd)

                    def midA(fa):
                        ch = fa["ch"]
                        ipd = fa["ipd"]
                        conv_d = []
                        for dt in range(16):
                            cap = cacc[:, 0:T] if dt % 2 == 0 else dsc[:, 0, 0:T]
                            tc_ = p.comp("vector", lambda E, dt=dt, cap=cap: E.tensor_scalar(
                                out=cap, in0=xipre[:, dt, 0:T],
                                scalar1=convw[:, dt, 0:1], scalar2=convb[:, dt:dt + 1],
                                op0=ALU.mult, op1=ALU.add), deps=ipd + conv_d[-2:])
                            for k in range(1, 4):
                                tc_ = p.comp("vector", lambda E, dt=dt, k=k, cap=cap: E.scalar_tensor_tensor(
                                    out=cap, in0=xipre[:, dt, k:k + T],
                                    scalar=convw[:, dt, k:k + 1], in1=cap,
                                    op0=ALU.mult, op1=ALU.add), deps=[tc_])
                            tsi = p.comp("scalar", lambda E, dt=dt, cap=cap: E.activation(
                                out=xi2[:, dt, :], in_=cap, func=AF.Silu),
                                deps=[tc_] + xist.get(dt, []))
                            conv_d.append(tsi)
                            xist[dt] = [p.dma("scalar", scr_xi[ch, dt], xi2[:, dt, :],
                                              deps=[tsi])]
                        t_carry = p.comp("vector", lambda E: E.tensor_copy(
                            out=xipre[:, :, 0:3], in_=xipre[:, :, T:T + 3]), deps=conv_d)
                        t_dbc = None
                        for kt in range(16):
                            t_dbc = p.comp("tensor", lambda E, kt=kt: E.matmul(
                                out=wkps[0][0:96, :], lhsT=xp16[:, kt, :], rhs=xi2[:, kt, :],
                                start=(kt == 0), stop=(kt == 15)),
                                deps=conv_d + wc2 + ipd + (ms.bank_rd[0] if kt == 0 else []))
                        t_dt = p.comp("vector", lambda E: E.tensor_copy(
                            out=dtT16[:, :], in_=wkps[0][0:64, :]), deps=[t_dbc])
                        t_bc = p.comp("vector", lambda E: E.tensor_copy(
                            out=bcst[64:96, 0:T], in_=wkps[0][64:96, :]), deps=[t_dbc])
                        t_crs = p.dma("sync", scr_cr[ch].rearrange("n o t -> (n o) t"),
                                      bcst[80:96, 0:T], deps=[t_bc])
                        del_d = []
                        bank_rd = {0: [t_bc, t_crs], 1: list(ms.bank_rd[1])}
                        dsc_rd = {0: []}
                        for mt in range(16):
                            bnk = wkps[mt % 2]
                            tm = p.comp("tensor", lambda E, mt=mt, bnk=bnk: E.matmul(
                                out=bnk[:, :], lhsT=dtw16[:, mt, :], rhs=dtT16[:, :],
                                start=True, stop=True), deps=[t_dt] + bank_rd[mt % 2])
                            ta1 = p.comp("scalar", lambda E, mt=mt, bnk=bnk: E.activation(
                                out=ln.sq[:, 0:T], in_=bnk[:, :], func=AF.Abs,
                                bias=dtb[:, mt:mt + 1]), deps=[tm] + del_d[-1:])
                            ta2 = p.comp("scalar", lambda E: E.activation(
                                out=ln.sq[:, 0:T], in_=ln.sq[:, 0:T], func=AF.Exp,
                                scale=-1.0), deps=[ta1])
                            ta3 = p.comp("scalar", lambda E: E.activation(
                                out=ln.sq[:, 0:T], in_=ln.sq[:, 0:T], func=AF.Ln,
                                bias=1.0), deps=[ta2])
                            ta4 = p.comp("scalar", lambda E, mt=mt, bnk=bnk: E.activation(
                                out=dsc[:, 0, :], in_=bnk[:, :], func=AF.Relu,
                                bias=dtb[:, mt:mt + 1]), deps=[ta3] + dsc_rd[0])
                            bank_rd[mt % 2] = [ta4]
                            t_dl = p.comp("vector", lambda E, mt=mt: E.tensor_add(
                                out=delta16[:, mt, :], in0=ln.sq[:, 0:T],
                                in1=dsc[:, 0, :]), deps=[ta4])
                            dsc_rd[0] = [t_dl]
                            del_d.append(t_dl)
                        ms.bank_rd[0] = bank_rd[0]
                        ms.bank_rd[1] = bank_rd[1]
                        bc_d = []
                        bmm = []
                        tmv = None
                        for n in range(32):
                            tmv = p.dma("sync", bc1[0:1, n % 2, 0:T],
                                        bcst[64 + n:65 + n, 0:T],
                                        deps=[t_bc] + bmm[-2:-1] + del_d[-1:])
                            tb = p.comp("tensor", lambda E, n=n: E.matmul(
                                out=mb.bcbank[:, :], lhsT=ones_row[:],
                                rhs=bc1[0:1, n % 2, 0:T], start=True, stop=True),
                                deps=[tmv] + bc_d[-1:] + (del_d[-1:] if n == 0 else []))
                            bmm.append(tb)
                            dst = bbc16 if n < 16 else cbc16
                            bc_d.append(p.comp("scalar", lambda E, n=n, dst=dst: E.activation(
                                out=dst[:, n % 16, :], in_=mb.bcbank[:, :], func=AF.Copy),
                                deps=[tb] + mb.cbc_rd))
                        mb.t1_rd = [t_crs, tmv]
                        return dict(ch=ch, del_d=del_d, t_carry=t_carry,
                                    bc_d=bc_d, conv_d=conv_d)

                    fas = [None] * NCH
                    mds = [None] * NCH
                    fas[0] = frontA(0)
                    mds[0] = midA(fas[0])
                    for ch in range(NCH):
                        if ch + 1 < NCH:
                            fas[ch + 1] = frontA(ch + 1)
                        md = mds[ch]
                        scan_A(p, ctx, mb, ch, delta16, xi2, bbc16, cbc16,
                               md["del_d"] + [md["t_carry"]], md["bc_d"],
                               md["conv_d"], wc2)
                        if ch + 1 < NCH:
                            mds[ch + 1] = midA(fas[ch + 1])
                with ctx.scope():
                    _phaseA()
                barx = p.barrier()
                tc1 = emit_gather(p, ctx, G, sp_all[:, :], cc1_in, cc1_out, 1024, barx)
                wcH = emit_fold(p, ctx, G, cc1_out, Htile, tc1)

                def _phaseB():
                    xtc = sb(ctx, "xtcB", [128, 8, T], F32)
                    cbc16 = sb(ctx, "cbcB", [128, 16, T], BF16)
                    y2 = sb(ctx, "y2B", [128, 16, T], BF16)
                    mb.dsb = sb(ctx, "dsb", [128, 2, T], F32)
                    mb.yacc = sb(ctx, "yacc", [128, 2, T], F32)
                    mb.xib = sb(ctx, "xib", [128, 2, T], BF16)
                    mb.zb = sb(ctx, "zbB", [128, 2, T], BF16)
                    mb.szb = sb(ctx, "szb", [128, 2, T], F32)
                    mb.dec = sb(ctx, "dec", [128, NR, T], F32)
                    mb.tmpc2 = sb(ctx, "tmpc2", [128, 4, T], F32)
                    bc1 = sb(ctx, "bc1B", [1, 2, T], F32)
                    mb.bc1 = bc1
                    wkps = [ps(ctx, f"wkpsB{i}", [128, T], F32) for i in range(2)]
                    mb.bcbank = ps(ctx, "bcbankB", [128, T], F32)
                    mb.g = 0
                    mb.dsb_rd = {0: [], 1: []}
                    mb.yacc_rd = {0: [], 1: []}
                    mb.xib_rd = {0: [], 1: []}
                    mb.zb_rd = {0: [], 1: []}
                    mb.szb_rd = {0: [], 1: []}
                    mb.dec_rd = {s: [] for s in range(NR)}
                    mb.tmpc_rd = {i: [] for i in range(4)}
                    ms = MS2()
                    stgB = [sb(ctx, f"stgB_{i}", [128, 16, 128], F32) for i in range(2)]
                    w16B = [sb(ctx, f"w16B_{i}", [128, 16, 128], BF16) for i in range(2)]
                    mb.xtcB_rd = []
                    for ch in range(NCH):
                        b, j = ch // 2, ch % 2
                        cb = b * SEG + j * T
                        bar = p.barrier()
                        bc_d = emit_cbc(p, ctx, mb, ch, cbc16, barx if ch == 0 else bar)
                        y2d = corr_B(p, ctx, mb, ch, cbc16, y2, bar, wcH, bc_d,
                                     ldbar=barx)
                        ldx2 = [p.dma("sync", xtc[:], sA[:, cb:cb + T].rearrange(
                            "(a p) t -> p a t", p=128),
                            deps=(barx if ch == 0 else bar) + mb.xtcB_rd)]
                        mb.xtcB_rd = []

                        def op_evac(mt, bank, deps):
                            t = p.comp("vector", lambda E, mt=mt, bank=bank: E.tensor_add(
                                out=xtc[:, mt, :], in0=bank[:, :], in1=xtc[:, mt, :]),
                                deps=deps + ldx2)
                            ts = p.dma("gpsimd", sA[mt * 128:(mt + 1) * 128, cb:cb + T],
                                       xtc[:, mt, :], deps=[t])
                            mb.xtcB_rd.append(ts)
                            return t

                        mm_stream2(p, ms, stgB, w16B, lambda mt: op_in[mt],
                                   16, 8, lambda kt: y2[:, kt, :], op_evac, wkps,
                                   y2d, bar, scr=op16s, first=(ch == 0))
                with ctx.scope():
                    _phaseB()

        def emit_l0ffn(src, dst):
            with ctx.scope():
                bar0 = p.barrier()
                winf = sb(ctx, "winf", [128, 8, 64], F32)
                win16 = sb(ctx, "win16", [128, 8, 64], BF16)
                woutf = sb(ctx, "woutf", [64, 8, 128], F32)
                wout16 = sb(ctx, "wout16", [64, 8, 128], BF16)
                bin_c = sb(ctx, "bin_c", [64, 1], F32)
                w1 = sb(ctx, "w1", [64, 16], F32)
                b1 = sb(ctx, "b1", [64, 16], F32)
                w2 = sb(ctx, "w2", [64, 16], F32)
                b2 = sb(ctx, "b2", [64, 1], F32)
                bout_c = sb(ctx, "bout", [128, 8], F32)
                xtc = sb(ctx, "xtc", [128, 8, T], F32)
                xn16 = sb(ctx, "xn16", [128, 8, T], BF16)
                s_sb = sb(ctx, "s_sb", [64, T], F32)
                h_sb = sb(ctx, "h_sb", [64, 2, T], F32)
                g_sb = sb(ctx, "g_sb", [64, T], F32)
                g16 = sb(ctx, "g16", [64, T], BF16)
                ln = LnTmp(ctx, T)
                s_ps = ps(ctx, "s_ps", [64, T], F32)
                out_ps = [ps(ctx, f"out_ps{i}", [128, T], F32) for i in range(2)]
                wl = [
                    p.dma("sync", winf[:], win_in, deps=bar0),
                    p.dma("sync", woutf[:], wout_in, deps=bar0),
                    p.dma("sync", bin_c[:], bin_in, deps=bar0),
                    p.dma("sync", w1[:], w1_in, deps=bar0),
                    p.dma("sync", b1[:], b1_in, deps=bar0),
                    p.dma("sync", w2[:], w2_in, deps=bar0),
                    p.dma("sync", b2[:], b2_in, deps=bar0),
                    p.dma("sync", bout_c[:], bout_in, deps=bar0),
                ]
                wc = [
                    p.comp("scalar", lambda E: E.activation(out=win16[:], in_=winf[:], func=AF.Copy), deps=wl),
                    p.comp("scalar", lambda E: E.activation(out=wout16[:], in_=woutf[:], func=AF.Copy), deps=wl),
                ] + wc0
                for ch in range(NCH):
                    c0 = ch * T
                    bar = p.barrier()
                    ldx = [p.dma("sync", xtc[:], src[:, c0:c0 + T].rearrange(
                        "(a p) t -> p a t", p=128), deps=bar)]
                    lnd, _ = emit_ln(p, ln, xtc, 8, ones_col[:], ones_row[:], xn16,
                                     ldx, kn)
                    t_mm = None
                    for k in range(8):
                        t_mm = p.comp("tensor", lambda E, k=k: E.matmul(
                            out=s_ps[:, :], lhsT=win16[:, k, :], rhs=xn16[:, k, :],
                            start=(k == 0), stop=(k == 7)), deps=lnd + wc)
                    t_s = p.comp("vector", lambda E: E.tensor_scalar(
                        out=s_sb[:], in0=s_ps[:, :], scalar1=bin_c[:, 0:1],
                        scalar2=None, op0=ALU.add), deps=[t_mm])
                    t_g = None
                    tg_hist = []
                    for i in range(16):
                        t_h = p.comp("scalar", lambda E, i=i: E.activation(
                            out=h_sb[:, i % 2, :], in_=s_sb[:], func=AF.Relu,
                            scale=w1[:, i:i + 1], bias=b1[:, i:i + 1]),
                            deps=[t_s] + tg_hist[-2:-1])
                        if i == 0:
                            t_g = p.comp("vector", lambda E: E.tensor_scalar(
                                out=g_sb[:], in0=h_sb[:, 0, :], scalar1=w2[:, 0:1],
                                op0=ALU.mult, scalar2=b2[:, 0:1], op1=ALU.add),
                                deps=[t_h])
                        else:
                            t_g = p.comp("vector", lambda E, i=i: E.scalar_tensor_tensor(
                                out=g_sb[:], in0=h_sb[:, i % 2, :], scalar=w2[:, i:i + 1],
                                in1=g_sb[:], op0=ALU.mult, op1=ALU.add), deps=[t_h, t_g])
                        tg_hist.append(t_g)
                    t_g16 = p.comp("vector", lambda E: E.tensor_copy(
                        out=g16[:], in_=g_sb[:]), deps=[t_g])
                    ev_toks = []
                    for dt in range(8):
                        t_o = p.comp("tensor", lambda E, dt=dt: E.matmul(
                            out=out_ps[dt % 2][:, :], lhsT=wout16[:, dt, :], rhs=g16[:],
                            start=True, stop=True), deps=[t_g16] + ev_toks[-2:-1])
                        t_ev = p.comp("vector", lambda E, dt=dt: E.scalar_tensor_tensor(
                            out=xtc[:, dt, :], in0=out_ps[dt % 2][:, :],
                            scalar=bout_c[:, dt:dt + 1], in1=xtc[:, dt, :],
                            op0=ALU.add, op1=ALU.add), deps=[t_o])
                        ev_toks.append(t_ev)
                        p.dma("gpsimd", dst[dt * 128:(dt + 1) * 128, c0:c0 + T],
                              xtc[:, dt, :], deps=[t_ev])

        def emit_l0attnb(src, dst):
            with ctx.scope():
                bar0 = p.barrier()
                wstage = sb(ctx, "wstage", [128, 8, D], F32)
                wq = sb(ctx, "wq16", [128, 8, D], BF16)
                wk = sb(ctx, "wk16", [128, 8, D], BF16)
                wv = sb(ctx, "wv16", [128, 8, D], BF16)
                bq = sb(ctx, "bq", [128, 8], F32)
                bk = sb(ctx, "bk", [128, 8], F32)
                bvf = sb(ctx, "bvf", [1, D], F32)
                bv = sb(ctx, "bv16", [1, D], BF16)
                ones_r16 = sb(ctx, "ones_r16", [1, 128], BF16)
                xtc = sb(ctx, "xtc", [128, 8, T], F32)
                xn16 = sb(ctx, "xn16", [128, 8, T], BF16)
                qT = sb(ctx, "qT", [128, 8, T], BF16)
                kT = sb(ctx, "kT", [128, 8, T], BF16)
                v16 = sb(ctx, "v16", [128, 4, D], BF16)
                pexp = sb(ctx, "pexp", [128, T], F32)
                p16 = sb(ctx, "p16", [128, T], BF16)
                pT16 = sb(ctx, "pT16", [128, 4, T], BF16)
                nmax = sb(ctx, "nmax", [128, 1], F32)
                ssum = sb(ctx, "ssum", [128, 1], F32)
                rsum = sb(ctx, "rsum", [128, 1], F32)
                ln = LnTmp(ctx, T)
                wkps = [ps(ctx, f"wkps{i}", [128, T], F32) for i in range(3)]
                tps = ps(ctx, "tps", [128, 128], BF16)
                wl = [p.dma("sync", bq[:], bq_in, deps=bar0),
                      p.dma("sync", bk[:], bk_in, deps=bar0),
                      p.dma("sync", bvf[:], bv_in, deps=bar0)]
                wc = [p.comp("scalar", lambda E: E.activation(out=bv[:], in_=bvf[:], func=AF.Copy), deps=wl),
                      p.comp("vector", lambda E: E.memset(ones_r16[:], 1.0))] + wc0
                prev_cv = []
                for win, wdst in [(wq_in, wq), (wk_in, wk), (wv_in, wv)]:
                    tl = p.dma("sync", wstage[:], win, deps=prev_cv + bar0)
                    prev_cv = [p.comp("scalar", lambda E, wdst=wdst: E.activation(
                        out=wdst[:], in_=wstage[:], func=AF.Copy), deps=[tl])]
                    wc += prev_cv
                for ch in range(NCH):
                    c0 = ch * T
                    bar = p.barrier()
                    ldx = [p.dma("sync", xtc[:], src[:, c0:c0 + T].rearrange(
                        "(a p) t -> p a t", p=128), deps=bar)]
                    lnd, _ = emit_ln(p, ln, xtc, 8, ones_col[:], ones_row[:], xn16,
                                     ldx, kn)
                    qk_toks = []
                    ev_hist = []
                    for which, (wt, bc, dstt) in enumerate([(wq, bq, qT), (wk, bk, kT)]):
                        for et in range(8):
                            bnk = wkps[(which * 8 + et) % 2]
                            tm = None
                            for k in range(8):
                                tm = p.comp("tensor", lambda E, wt=wt, et=et, k=k, bnk=bnk: E.matmul(
                                    out=bnk[:, :], lhsT=wt[:, k, et * 128:(et + 1) * 128],
                                    rhs=xn16[:, k, :], start=(k == 0), stop=(k == 7)),
                                    deps=lnd + wc + ev_hist[-2:-1])
                            tev = p.comp("vector", lambda E, dstt=dstt, et=et, bc=bc, bnk=bnk: E.tensor_scalar(
                                out=dstt[:, et, :], in0=bnk[:, :], scalar1=bc[:, et:et + 1],
                                scalar2=None, op0=ALU.add), deps=[tm])
                            ev_hist.append(tev)
                            qk_toks.append(tev)
                    v_toks = []
                    for ct in range(4):
                        for eh in range(2):
                            bnk = wkps[(ct * 2 + eh) % 2]
                            tm = None
                            for k in range(8):
                                tm = p.comp("tensor", lambda E, ct=ct, eh=eh, k=k, bnk=bnk: E.matmul(
                                    out=bnk[:, :], lhsT=xn16[:, k, ct * 128:(ct + 1) * 128],
                                    rhs=wv[:, k, eh * 512:(eh + 1) * 512], start=(k == 0),
                                    stop=False), deps=lnd + wc + ev_hist[-2:-1])
                            tm = p.comp("tensor", lambda E, eh=eh, bnk=bnk: E.matmul(
                                out=bnk[:, :], lhsT=ones_r16[:, :],
                                rhs=bv[:, eh * 512:(eh + 1) * 512],
                                start=False, stop=True), deps=[])
                            tev = p.comp("scalar", lambda E, ct=ct, eh=eh, bnk=bnk: E.activation(
                                out=v16[:, ct, eh * 512:(eh + 1) * 512], in_=bnk[:, :],
                                func=AF.Copy), deps=[tm])
                            ev_hist.append(tev)
                            v_toks.append(tev)
                    ot_deps = []
                    tr_hist = []
                    sb_rd = {0: v_toks[-2:-1], 1: v_toks[-1:]}
                    exp_prev = []
                    for cqt in range(4):
                        sbnk = wkps[cqt % 2]
                        tm = None
                        for et in range(8):
                            tm = p.comp("tensor", lambda E, cqt=cqt, et=et, sbnk=sbnk: E.matmul(
                                out=sbnk[:, :], lhsT=qT[:, et, cqt * 128:(cqt + 1) * 128],
                                rhs=kT[:, et, :], start=(et == 0), stop=(et == 7)),
                                deps=qk_toks + sb_rd[cqt % 2])
                        t_max = p.comp("vector", lambda E, sbnk=sbnk: E.reduce_max(
                            out=nmax[:], in_=sbnk[:, :], axis=mybir.AxisListType.X,
                            negate=True), deps=[tm] + exp_prev)
                        t_exp = p.comp("scalar", lambda E, sbnk=sbnk: E.activation(
                            out=pexp[:], in_=sbnk[:, :], func=AF.Exp, bias=nmax[:, 0:1],
                            accum_out=ssum[:]), deps=[t_max])
                        sb_rd[cqt % 2] = [t_exp]
                        exp_prev = [t_exp]
                        t_rec = p.comp("vector", lambda E: E.reciprocal(
                            out=rsum[:], in_=ssum[:]), deps=[t_exp])
                        t_p16 = p.comp("vector", lambda E: E.tensor_scalar(
                            out=p16[:], in0=pexp[:], scalar1=rsum[:, 0:1], scalar2=None,
                            op0=ALU.mult), deps=[t_rec] + tr_hist[-1:])
                        for ckt in range(4):
                            ttr = p.comp("tensor", lambda E, ckt=ckt: E.transpose(
                                out=tps[:, :], in_=p16[:, ckt * 128:(ckt + 1) * 128],
                                identity=eye16[:]), deps=[t_p16] + tr_hist[-1:])
                            tev = p.comp("vector", lambda E, ckt=ckt, cqt=cqt: E.tensor_copy(
                                out=pT16[:, ckt, cqt * 128:(cqt + 1) * 128], in_=tps[:, :]),
                                deps=[ttr])
                            tr_hist.append(tev)
                            ot_deps.append(tev)
                    oev = []
                    for et in range(8):
                        bnk = wkps[et % 2]
                        tm = None
                        for ckt in range(4):
                            tm = p.comp("tensor", lambda E, et=et, ckt=ckt, bnk=bnk: E.matmul(
                                out=bnk[:, :], lhsT=v16[:, ckt, et * 128:(et + 1) * 128],
                                rhs=pT16[:, ckt, :], start=(ckt == 0), stop=(ckt == 3)),
                                deps=ot_deps + v_toks + oev[-2:-1])
                        tev = p.comp("vector", lambda E, et=et, bnk=bnk: E.tensor_add(
                            out=xtc[:, et, :], in0=bnk[:, :], in1=xtc[:, et, :]),
                            deps=[tm])
                        oev.append(tev)
                        p.dma("gpsimd", dst[et * 128:(et + 1) * 128, c0:c0 + T],
                              xtc[:, et, :], deps=[tev])

        def emit_l1attna(src, dst):
            with ctx.scope():
                bar0 = p.barrier()
                lng = sb(ctx, "lng", [128, 8], F32)
                lnb = sb(ctx, "lnb", [128, 8], F32)
                seg = sb(ctx, "seg", [128, 8, 16], F32)
                xtc = sb(ctx, "xtc", [128, 8, T], F32)
                xn16 = sb(ctx, "xn16", [128, 8, 2, T], BF16)
                qT = sb(ctx, "qT", [128, 8, 2, T], BF16)
                kT = sb(ctx, "kT", [128, 8, 2, T], BF16)
                vT = sb(ctx, "vT", [128, 8, 2, T], BF16)
                o16 = sb(ctx, "o16", [128, 8, 2, T], BF16)
                eq = sb(ctx, "eq", [128, 8, T], F32)
                pr2 = sb(ctx, "pr2", [128, 8, T], F32)
                yv = sb(ctx, "yv", [128, 8, T], F32)
                betaR = sb(ctx, "betaR", [16, 2, T], F32)
                rows = sb(ctx, "rows", [16, 8, T], F32)
                bc1 = sb(ctx, "bc1", [1, 4, T], F32)
                ln = LnTmp(ctx, T)
                wkps = [ps(ctx, f"wkps{i}", [128, T], F32) for i in range(2)]
                segb = ps(ctx, "segb", [16, T], F32)
                cbank2 = wkps
                wl = [p.dma("sync", lng[:], lngA_in, deps=bar0),
                      p.dma("sync", lnb[:], lnbA_in, deps=bar0)]
                wc = [p.comp("vector", lambda E: E.memset(seg[:], 0.0))]
                for kt in range(8):
                    wc.append(p.comp("vector", lambda E, kt=kt: E.memset(
                        seg[0:64, kt, 2 * kt:2 * kt + 1], 1.0), deps=wc[0:1]))
                    wc.append(p.comp("vector", lambda E, kt=kt: E.memset(
                        seg[64:128, kt, 2 * kt + 1:2 * kt + 2], 1.0), deps=wc[0:1]))
                ones_c = kn.ap(1.0)
                prev_s_done = []
                p._bc_mm = []
                for pc in range(2):
                    bar = p.barrier()
                    lnd = []
                    for s in range(2):
                        ldx = p.dma("sync", xtc[:],
                                    src[:, s * 1024 + pc * T: s * 1024 + pc * T + T]
                                    .rearrange("(a p) t -> p a t", p=128),
                                    deps=bar + lnd + prev_s_done)
                        d, _ = emit_ln(p, ln, xtc, 8, ones_c, ones_row[:],
                                       xn16[:, :, s, :], [ldx] + prev_s_done,
                                       kn, T_=T, col0=0)
                        lnd += d
                    msA = MS2()
                    qkv_d = []
                    beta_d = []
                    if not hasattr(p, "_ms2cv"):
                        p._ms2cv = []
                        p._ms2rd = {0: [], 1: []}
                    if not hasattr(p, "_ms2st"):
                        p._ms2st = {}
                    for mt in range(25):
                        g = len(p._ms2cv)
                        q = g % 2
                        sl = g % 2
                        skey = (id(qkvb16s), mt)
                        if pc > 0:
                            tcv = p.dma(["sync", "scalar"][q], w16b[sl][:, 0:8, :],
                                        qkvb16s[mt],
                                        deps=p._ms2rd[sl] + p._ms2st[skey] +
                                        ([p._ms2cv[-2]] if len(p._ms2cv) >= 2 else bar))
                        else:
                            tld = p.dma(["sync", "scalar"][q], stg2[sl][:, 0:8, :],
                                        wqkvb_in[mt],
                                        deps=([p._ms2cv[-2]] if len(p._ms2cv) >= 2 else bar))
                            tcv = p.comp("vector", lambda E, sl=sl: E.tensor_copy(
                                out=w16b[sl][:, 0:8, :], in_=stg2[sl][:, 0:8, :]),
                                deps=[tld] + p._ms2rd[sl])
                            tst = p.dma(["sync", "scalar"][1 - q], qkvb16s[mt],
                                        w16b[sl][:, 0:8, :], deps=[tcv])
                            p._ms2st[skey] = [tst]
                            p._ms2rd[sl] = p._ms2rd[sl] + [tst]
                        p._ms2cv.append(tcv)
                        tmm = {0: None, 1: None}
                        for kt in range(8):
                            for s in range(2):
                                deps = [tcv]
                                if kt == 0:
                                    deps = deps + lnd + msA.bank_rd[s]
                                tmm[s] = p.comp("tensor", lambda E, sl=sl, kt=kt, s=s: E.matmul(
                                    out=wkps[s][:, :], lhsT=w16b[sl][:, kt, :],
                                    rhs=xn16[:, kt, s, :], start=(kt == 0), stop=(kt == 7)),
                                    deps=deps)
                        p._ms2rd[sl] = [tmm[0], tmm[1]]
                        for s in range(2):
                            if mt < 8:
                                tev = p.comp("vector", lambda E, mt=mt, s=s: E.tensor_copy(
                                    out=qT[:, mt, s, :], in_=wkps[s][:, :]), deps=[tmm[s]])
                            elif mt < 16:
                                tev = p.comp("vector", lambda E, mt=mt, s=s: E.tensor_copy(
                                    out=kT[:, mt - 8, s, :], in_=wkps[s][:, :]), deps=[tmm[s]])
                            elif mt < 24:
                                tev = p.comp("vector", lambda E, mt=mt, s=s: E.tensor_copy(
                                    out=vT[:, mt - 16, s, :], in_=wkps[s][:, :]), deps=[tmm[s]])
                            else:
                                tev = p.comp("vector", lambda E, s=s: E.tensor_copy(
                                    out=betaR[:, s, :], in_=wkps[s][0:16, :]), deps=[tmm[s]])
                                beta_d.append(tev)
                            msA.bank_rd[s] = [tev]
                            qkv_d.append(tev)
                    norm_d = []
                    seg_rd = []
                    for (srcq, ridx) in ((qT, 0), (kT, 1)):
                        for s in range(2):
                            t_m = p.comp("vector", lambda E, srcq=srcq, s=s: E.tensor_scalar(
                                out=eq[:], in0=srcq[:, :, s, :], scalar1=0.0, scalar2=None,
                                op0=ALU.min), deps=qkv_d + norm_d[-1:])
                            t_e = p.comp("scalar", lambda E: E.activation(
                                out=eq[:], in_=eq[:], func=AF.Exp), deps=[t_m])
                            t_r = p.comp("vector", lambda E, srcq=srcq, s=s: E.tensor_relu(
                                out=pr2[:], in_=srcq[:, :, s, :]), deps=qkv_d + norm_d[-1:])
                            t_a = p.comp("vector", lambda E: E.tensor_add(
                                out=eq[:], in0=eq[:], in1=pr2[:]), deps=[t_e, t_r])
                            t_sm = None
                            for kt in range(8):
                                t_sm = p.comp("tensor", lambda E, kt=kt: E.matmul(
                                    out=segb[:, :], lhsT=seg[:, kt, :], rhs=eq[:, kt, :],
                                    start=(kt == 0), stop=(kt == 7)),
                                    deps=[t_a] + wc + seg_rd)
                            ri = ridx * 2 + s
                            t_rc = p.comp("vector", lambda E, ri=ri: E.reciprocal(
                                out=rows[:, ri, :], in_=segb[:, :]), deps=[t_sm])
                            seg_rd = [t_rc]
                            bcm = p._bc_mm
                            for kt in range(8):
                                cbank = cbank2[kt % 2]
                                for hh in range(2):
                                    h = 2 * kt + hh
                                    tmv = p.dma("gpsimd", bc1[0:1, h % 4, 0:T],
                                                rows[h:h + 1, ri, :], deps=[t_rc] + bcm[-4:-3])
                                    bcm.append(p.comp("tensor", lambda E, hh=hh, h=h, cbank=cbank: E.matmul(
                                        out=cbank[64 * hh:64 * (hh + 1), :],
                                        lhsT=ones_row[0:1, 0:64], rhs=bc1[0:1, h % 4, 0:T],
                                        start=True, stop=True),
                                        deps=[tmv] + norm_d[-2:-1]))
                                norm_d.append(p.comp("vector", lambda E, srcq=srcq, s=s, kt=kt, cbank=cbank: E.tensor_mul(
                                    out=srcq[:, kt, s, :], in0=eq[:, kt, :], in1=cbank[:, :]),
                                    deps=bcm[-2:] + [t_a]))
                    dot_d = []
                    for di, (a_, sa, b_, sbi) in enumerate((
                            (qT, 0, kT, 0), (kT, 0, kT, 1), (qT, 1, kT, 0), (qT, 1, kT, 1))):
                        t_p = p.comp("vector", lambda E, a_=a_, sa=sa, b_=b_, sbi=sbi: E.tensor_mul(
                            out=pr2[:], in0=a_[:, :, sa, :], in1=b_[:, :, sbi, :]),
                            deps=norm_d + dot_d[-1:])
                        t_sm = None
                        for kt in range(8):
                            t_sm = p.comp("tensor", lambda E, kt=kt: E.matmul(
                                out=segb[:, :], lhsT=seg[:, kt, :], rhs=pr2[:, kt, :],
                                start=(kt == 0), stop=(kt == 7)),
                                deps=[t_p] + seg_rd)
                        t_ev = p.comp("vector", lambda E, di=di: E.tensor_copy(
                            out=rows[:, 4 + di, :], in_=segb[:, :]), deps=[t_sm])
                        seg_rd = [t_ev]
                        dot_d.append(t_ev)
                    t_b0 = p.comp("scalar", lambda E: E.activation(
                        out=rows[:, 3, :], in_=betaR[:, 0, :], func=AF.Sigmoid),
                        deps=beta_d + norm_d)
                    t_b1 = p.comp("scalar", lambda E: E.activation(
                        out=betaR[:, 1, :], in_=betaR[:, 1, :], func=AF.Sigmoid),
                        deps=beta_d + [t_b0])
                    t_c1 = p.comp("vector", lambda E: E.tensor_mul(
                        out=rows[:, 0, :], in0=rows[:, 4, :], in1=rows[:, 3, :]),
                        deps=[t_b0] + dot_d + norm_d)
                    t_c1 = p.comp("vector", lambda E: E.tensor_scalar(
                        out=rows[:, 0, :], in0=rows[:, 0, :], scalar1=0.125,
                        scalar2=None, op0=ALU.mult), deps=[t_c1])
                    t_cb = p.comp("vector", lambda E: E.tensor_mul(
                        out=rows[:, 2, :], in0=rows[:, 7, :], in1=betaR[:, 1, :]),
                        deps=[t_b1] + dot_d)
                    t_t = p.comp("vector", lambda E: E.tensor_mul(
                        out=rows[:, 1, :], in0=rows[:, 2, :], in1=rows[:, 5, :]),
                        deps=[t_cb])
                    t_t = p.comp("vector", lambda E: E.tensor_sub(
                        out=rows[:, 1, :], in0=rows[:, 6, :], in1=rows[:, 1, :]), deps=[t_t])
                    t_t = p.comp("vector", lambda E: E.tensor_mul(
                        out=rows[:, 1, :], in0=rows[:, 1, :], in1=rows[:, 3, :]),
                        deps=[t_t, t_b0])
                    t_ca = p.comp("vector", lambda E: E.tensor_scalar(
                        out=rows[:, 1, :], in0=rows[:, 1, :], scalar1=0.125,
                        scalar2=None, op0=ALU.mult), deps=[t_t])
                    t_cb = p.comp("vector", lambda E: E.tensor_scalar(
                        out=rows[:, 2, :], in0=rows[:, 2, :], scalar1=0.125,
                        scalar2=None, op0=ALU.mult), deps=[t_cb])
                    coef_d = [t_c1, t_ca, t_cb]
                    out_d = []
                    bcm = p._bc_mm
                    for (ri, s, vsrc, acc) in ((0, 0, 0, False), (1, 1, 0, False), (2, 1, 1, True)):
                        for kt in range(8):
                            cbank = cbank2[kt % 2]
                            for hh in range(2):
                                h = 2 * kt + hh
                                tmv = p.dma("gpsimd", bc1[0:1, h % 4, 0:T],
                                            rows[h:h + 1, ri, :], deps=coef_d + bcm[-4:-3])
                                bcm.append(p.comp("tensor", lambda E, hh=hh, cbank=cbank: E.matmul(
                                    out=cbank[64 * hh:64 * (hh + 1), :],
                                    lhsT=ones_row[0:1, 0:64], rhs=bc1[0:1, h % 4, 0:T],
                                    start=True, stop=True),
                                    deps=[tmv] + out_d[-2:-1]))
                            if not acc:
                                out_d.append(p.comp("vector", lambda E, s=s, kt=kt, vsrc=vsrc, cbank=cbank: E.tensor_mul(
                                    out=o16[:, kt, s, :] if s == 0 else yv[:, kt, :],
                                    in0=vT[:, kt, vsrc, :], in1=cbank[:, :]),
                                    deps=bcm[-2:] + norm_d + dot_d))
                            else:
                                t_m2 = p.comp("vector", lambda E, kt=kt, vsrc=vsrc, cbank=cbank: E.tensor_mul(
                                    out=pr2[:, kt, :], in0=vT[:, kt, vsrc, :], in1=cbank[:, :]),
                                    deps=bcm[-2:] + dot_d)
                                out_d.append(p.comp("vector", lambda E, kt=kt: E.tensor_add(
                                    out=o16[:, kt, 1, :], in0=yv[:, kt, :], in1=pr2[:, kt, :]),
                                    deps=[t_m2] + out_d))
                    for s in range(2):
                        def wo_evac(mt, bank, deps, s=s):
                            return p.comp("vector", lambda E, mt=mt, bank=bank, s=s: E.tensor_add(
                                out=yv[:, mt, :], in0=bank[:, :], in1=xn16[:, mt, s, :]),
                                deps=deps + out_d + prev_s_done)
                        yd = mm_stream2(p, msA, stg2, w16b, lambda mt: woA_in[mt],
                                        8, 8, lambda kt, s=s: o16[:, kt, s, :],
                                        wo_evac, wkps, out_d + prev_s_done, bar,
                                        scr=woA16s, first=(pc == 0 and s == 0))
                        sq_t, last_q, last_s = [], [], []
                        for i in range(8):
                            tq = p.comp("scalar", lambda E, i=i: E.activation(
                                out=ln.sq[:, 0:T], in_=yv[:, i, :], func=AF.Square),
                                deps=yd + last_q + sq_t[-1:])
                            sq_t.append(tq)
                            last_q = [p.comp("tensor", lambda E, i=i: E.matmul(
                                out=ln.ps_q[:, 0:T], lhsT=kn.ap(1.0), rhs=ln.sq[:, 0:T],
                                start=(i == 0), stop=(i == 7)), deps=[tq])]
                            last_s = [p.comp("tensor", lambda E, i=i: E.matmul(
                                out=ln.ps_s[:, 0:T], lhsT=kn.ap(1.0), rhs=yv[:, i, :],
                                start=(i == 0), stop=(i == 7)), deps=yd)]
                        a_inv = kn.row(1.0 / 1024.0)
                        t_mean = p.comp("scalar", lambda E: E.activation(
                            out=ln.r_mean[:, 0:T], in_=ln.ps_s[:, 0:T], func=AF.Copy,
                            scale=a_inv), deps=last_s + kn.dep())
                        t_msq = p.comp("scalar", lambda E: E.activation(
                            out=ln.r_msq[:, 0:T], in_=ln.ps_q[:, 0:T], func=AF.Copy,
                            scale=a_inv), deps=last_q + kn.dep())
                        t_m2_ = p.comp("scalar", lambda E: E.activation(
                            out=ln.r_tmp[:, 0:T], in_=ln.r_mean[:, 0:T], func=AF.Square),
                            deps=[t_mean])
                        t_var = p.comp("vector", lambda E: E.tensor_sub(
                            out=ln.r_msq[:, 0:T], in0=ln.r_msq[:, 0:T], in1=ln.r_tmp[:, 0:T]),
                            deps=[t_msq, t_m2_])
                        t_lnv = p.comp("scalar", lambda E: E.activation(
                            out=ln.r_tmp[:, 0:T], in_=ln.r_msq[:, 0:T], func=AF.Ln,
                            bias=kn.row(EPS)), deps=[t_var] + kn.dep())
                        t_rst = p.comp("scalar", lambda E: E.activation(
                            out=ln.r_rstd[:, 0:T], in_=ln.r_tmp[:, 0:T], func=AF.Exp,
                            scale=kn.row(-0.5)), deps=[t_lnv] + kn.dep())
                        t_mb = p.comp("tensor", lambda E: E.matmul(
                            out=ln.ps_mb[:, 0:T], lhsT=ones_row[:], rhs=ln.r_mean[:, 0:T],
                            start=True, stop=True), deps=[t_mean] + last_s)
                        t_rb = p.comp("tensor", lambda E: E.matmul(
                            out=ln.ps_rb[:, 0:T], lhsT=ones_row[:], rhs=ln.r_rstd[:, 0:T],
                            start=True, stop=True), deps=[t_rst])
                        ldr = [p.dma("sync", xtc[:],
                                     src[:, s * 1024 + pc * T: s * 1024 + pc * T + T]
                                     .rearrange("(a p) t -> p a t", p=128),
                                     deps=lnd + prev_s_done + [t_rb])]
                        st_d = []
                        for i in range(8):
                            ta = p.comp("vector", lambda E, i=i: E.tensor_sub(
                                out=ln.sq[:, 0:T], in0=yv[:, i, :], in1=ln.ps_mb[:, 0:T]),
                                deps=[t_mb] + sq_t + st_d[-1:])
                            tb2 = p.comp("vector", lambda E, i=i: E.tensor_mul(
                                out=ln.sq[:, 0:T], in0=ln.sq[:, 0:T], in1=ln.ps_rb[:, 0:T]),
                                deps=[t_rb, ta])
                            tc2 = p.comp("vector", lambda E, i=i: E.tensor_scalar(
                                out=ln.sq[:, 0:T], in0=ln.sq[:, 0:T], scalar1=lng[:, i:i + 1],
                                scalar2=lnb[:, i:i + 1], op0=ALU.mult, op1=ALU.add),
                                deps=[tb2] + wl)
                            tf = p.comp("vector", lambda E, i=i: E.tensor_add(
                                out=xtc[:, i, :], in0=xtc[:, i, :], in1=ln.sq[:, 0:T]),
                                deps=[tc2] + ldr)
                            st_d.append(tf)
                            st_d.append(p.dma("gpsimd", dst[i * 128:(i + 1) * 128,
                                                s * 1024 + pc * T: s * 1024 + pc * T + T],
                                  xtc[:, i, :], deps=[tf]))
                        prev_s_done = st_d + last_q
        def emit_l1ffn(src, dst, first=True):
            with ctx.scope():
                bar0 = p.barrier()
                rwst = sb(ctx, "rwst", [128, 8, 8], F32)
                rw16 = sb(ctx, "rw16", [128, 8, 8], BF16)
                wstg2 = [sb(ctx, f"wstg2_{i}", [128, 8, 128], F32) for i in range(2)]
                w16 = [sb(ctx, f"w16_{i}", [128, 8, 128], BF16) for i in range(2)]
                xtc = sb(ctx, "xtc", [128, 8, T], F32)
                xn16 = sb(ctx, "xn16", [128, 8, TOK], BF16)
                acc = sb(ctx, "acc", [128, 8, TOK], BF16)
                hid16 = sb(ctx, "hid16", [128, 8, TOK], BF16)
                s16 = sb(ctx, "s16", [128, TOK], BF16)
                logit = sb(ctx, "logit", [128, 16, 8], F32)
                prob = sb(ctx, "prob", [128, 16, 8], F32)
                gate = sb(ctx, "gate", [128, 16, 8], F32)
                gateT = sb(ctx, "gateT", [8, TOK], F32)
                gsb4 = sb(ctx, "gsb4", [128, NCH, T], BF16)
                t1sb = sb(ctx, "t1sb", [128, T], BF16)
                nmax = sb(ctx, "nmax", [128, 1], F32)
                ssum = sb(ctx, "ssum", [128, 1], F32)
                rsum = sb(ctx, "rsum", [128, 1], F32)
                m2p = sb(ctx, "m2p", [128, 1], F32)
                msk = sb(ctx, "msk", [128, 8], F32)
                bc1 = sb(ctx, "bc1", [1, 2, T], F32)
                ln = LnTmp(ctx, T)
                wkps = [ps(ctx, f"wkps{i}", [128, T], F32) for i in range(2)]
                gps = ps(ctx, "gps", [128, T], F32)
                tps = ps(ctx, "tps", [128, 128], F32)
                wl = [p.dma("sync", rwst[:], rw_in.rearrange("k p e -> p k e"), deps=bar0)]
                wc = [p.comp("scalar", lambda E: E.activation(out=rw16[:], in_=rwst[:], func=AF.Copy), deps=wl)] + wc0
                lnall = []
                ldxs = []
                for ch in range(NCH):
                    c0 = ch * T
                    ldx = [p.dma("sync", xtc[:], src[:, c0:c0 + T].rearrange(
                        "(a p) t -> p a t", p=128), deps=(lnall[-1:] or (wc + wl)) + bar0)]
                    ldxs.append(ldx)
                    lnd, _ = emit_ln(p, ln, xtc, 8, ones_col[:], ones_row[:], xn16, ldx,
                                     kn, T_=T, col0=0, ocol0=c0)
                    lnall += lnd
                gdone = []
                tT_rd = []
                for tt in range(16):
                    tm = None
                    for kt in range(8):
                        tm = p.comp("tensor", lambda E, tt=tt, kt=kt: E.matmul(
                            out=wkps[0][:, 0:8], lhsT=xn16[:, kt, tt * 128:(tt + 1) * 128],
                            rhs=rw16[:, kt, :], start=(kt == 0), stop=(kt == 7)),
                            deps=lnall + wc + gdone[-1:])
                    t_lg = p.comp("vector", lambda E, tt=tt: E.tensor_copy(
                        out=logit[:, tt, :], in_=wkps[0][:, 0:8]), deps=[tm])
                    t_mx = p.comp("vector", lambda E, tt=tt: E.reduce_max(
                        out=nmax[:], in_=logit[:, tt, :], axis=mybir.AxisListType.X,
                        negate=True), deps=[t_lg] + gdone[-1:])
                    t_ex = p.comp("scalar", lambda E, tt=tt: E.activation(
                        out=prob[:, tt, :], in_=logit[:, tt, :], func=AF.Exp,
                        bias=nmax[:, 0:1], accum_out=ssum[:]), deps=[t_mx])
                    t_rc = p.comp("vector", lambda E: E.reciprocal(
                        out=rsum[:], in_=ssum[:]), deps=[t_ex])
                    t_pr = p.comp("vector", lambda E, tt=tt: E.tensor_scalar(
                        out=prob[:, tt, :], in0=prob[:, tt, :], scalar1=rsum[:, 0:1],
                        scalar2=None, op0=ALU.mult), deps=[t_rc])
                    t_m1 = p.comp("vector", lambda E, tt=tt: E.reduce_max(
                        out=nmax[:], in_=prob[:, tt, :], axis=mybir.AxisListType.X),
                        deps=[t_pr])
                    t_eq = p.comp("vector", lambda E, tt=tt: E.tensor_scalar(
                        out=msk[:], in0=prob[:, tt, :], scalar1=nmax[:, 0:1],
                        scalar2=None, op0=ALU.is_lt), deps=[t_m1])
                    t_mk = p.comp("vector", lambda E, tt=tt: E.tensor_mul(
                        out=msk[:], in0=msk[:], in1=prob[:, tt, :]), deps=[t_eq])
                    t_m2 = p.comp("vector", lambda E: E.reduce_max(
                        out=m2p[:], in_=msk[:], axis=mybir.AxisListType.X), deps=[t_mk])
                    t_dn = p.comp("vector", lambda E: E.tensor_add(
                        out=ssum[:], in0=nmax[:], in1=m2p[:]), deps=[t_m2])
                    t_rc2 = p.comp("vector", lambda E: E.reciprocal(
                        out=rsum[:], in_=ssum[:]), deps=[t_dn])
                    t_ge = p.comp("vector", lambda E, tt=tt: E.tensor_scalar(
                        out=msk[:], in0=prob[:, tt, :], scalar1=m2p[:, 0:1],
                        scalar2=None, op0=ALU.is_ge), deps=[t_m2, t_mk])
                    t_g1 = p.comp("vector", lambda E, tt=tt: E.tensor_mul(
                        out=gate[:, tt, :], in0=msk[:], in1=prob[:, tt, :]), deps=[t_ge])
                    t_g2 = p.comp("vector", lambda E, tt=tt: E.tensor_scalar(
                        out=gate[:, tt, :], in0=gate[:, tt, :], scalar1=rsum[:, 0:1],
                        scalar2=None, op0=ALU.mult), deps=[t_g1, t_rc2])
                    t_tr = p.comp("tensor", lambda E, tt=tt: E.transpose(
                        out=tps[0:8, :], in_=gate[:, tt, 0:8], identity=eyef[:]),
                        deps=[t_g2] + tT_rd)
                    t_gt = p.comp("vector", lambda E, tt=tt: E.tensor_copy(
                        out=gateT[0:8, tt * 128:(tt + 1) * 128], in_=tps[0:8, :]),
                        deps=[t_tr])
                    tT_rd = [t_gt]
                    gdone.append(t_g2)
                    gdone.append(t_gt)
                accd = []
                ev_all = list(gdone)
                p._moe_cv = []
                p._moe_mm = []
                p._gsb_rd = []
                p._tgb_hist = []
                qg = [0]

                if not hasattr(p, "_ms2st"):
                    p._ms2st = {}
                p._moe_wg = {0: [], 1: []}

                def load_w(src_ap, scr_ap, skey):
                    q = qg[0] % 2
                    qg[0] += 1
                    if not first:
                        tcv = p.dma(["sync", "gpsimd"][q], w16[q][:], scr_ap,
                                    deps=(p._moe_cv[-2:-1] or (wl + lnall)) +
                                    p._moe_mm[-5:-4] + p._ms2st[skey])
                    else:
                        tld = p.dma(["sync", "gpsimd"][q], wstg2[q][:], src_ap,
                                    deps=(p._moe_cv[-2:-1] or (wl + lnall)))
                        tcv = p.comp("scalar", lambda E, q=q: E.activation(
                            out=w16[q][:], in_=wstg2[q][:], func=AF.Copy),
                            deps=[tld] + p._moe_mm[-5:-4] + p._moe_wg[q])
                        tst = p.dma(["gpsimd", "sync"][q], scr_ap, w16[q][:],
                                    deps=[tcv])
                        p._ms2st[skey] = [tst]
                        p._moe_wg[q] = [tst]
                    p._moe_cv.append(tcv)
                    return q, tcv

                for e in range(8):
                    hidd = []
                    sils = {}
                    for jj in range(8):
                        for half in range(2):
                            mt = jj + 8 * half
                            q, tcv = load_w(up_in[e, mt], up16s[e, mt],
                                            ("up", e, mt))
                            for ch in range(NCH):
                                c0 = ch * T
                                tm = None
                                for kt in range(8):
                                    tm = p.comp("tensor", lambda E, q=q, kt=kt, ch=ch, c0=c0: E.matmul(
                                        out=wkps[ch % 2][:, :], lhsT=w16[q][:, kt, :],
                                        rhs=xn16[:, kt, c0:c0 + T], start=(kt == 0),
                                        stop=(kt == 7)),
                                        deps=[tcv] + lnall + wc + ev_all[-2:-1] + accd[-1:])
                                p._moe_mm.append(tm)
                                if half == 0:
                                    tev = p.comp("scalar", lambda E, ch=ch, c0=c0: E.activation(
                                        out=s16[:, c0:c0 + T], in_=wkps[ch % 2][:, :],
                                        func=AF.Silu), deps=[tm] + hidd[-1:])
                                    sils[ch] = tev
                                else:
                                    tev = p.comp("vector", lambda E, jj=jj, ch=ch, c0=c0: E.tensor_mul(
                                        out=hid16[:, jj, c0:c0 + T], in0=s16[:, c0:c0 + T],
                                        in1=wkps[ch % 2][:, :]),
                                        deps=[tm, sils[ch]] + hidd[-4:-3])
                                    hidd.append(tev)
                                ev_all.append(tev)
                    gds = []
                    for ch in range(NCH):
                        c0 = ch * T
                        tmv = p.dma("gpsimd", bc1[0:1, ch % 2, 0:T],
                                    gateT[e:e + 1, c0:c0 + T],
                                    deps=gdone + (gds[-2:-1] or p._tgb_hist[-2:]))
                        tgb = p.comp("tensor", lambda E, ch=ch: E.matmul(
                            out=gps[:, :], lhsT=ones_row[:], rhs=bc1[0:1, ch % 2, 0:T],
                            start=True, stop=True), deps=[tmv] + gds[-1:] + p._gsb_rd)
                        tgs = p.comp("scalar", lambda E, ch=ch: E.activation(
                            out=gsb4[:, ch, :], in_=gps[:, :], func=AF.Copy),
                            deps=[tgb] + p._gsb_rd)
                        gds.append(tgs)
                        p._tgb_hist.append(tgb)
                    p._gsb_rd = []
                    for mt in range(8):
                        q, tcv = load_w(dn_in[e, mt], dn16s[e, mt],
                                        ("dn", e, mt))
                        for ch in range(NCH):
                            c0 = ch * T
                            tm = None
                            for kt in range(8):
                                tm = p.comp("tensor", lambda E, q=q, kt=kt, ch=ch, c0=c0: E.matmul(
                                    out=wkps[ch % 2][:, :], lhsT=w16[q][:, kt, :],
                                    rhs=hid16[:, kt, c0:c0 + T], start=(kt == 0),
                                    stop=(kt == 7)), deps=[tcv] + hidd + ev_all[-2:-1])
                            p._moe_mm.append(tm)
                            t_t1 = p.comp("vector", lambda E, ch=ch: E.tensor_mul(
                                out=t1sb[:, :], in0=wkps[ch % 2][:, :], in1=gsb4[:, ch, :]),
                                deps=[tm] + gds + accd[-1:])
                            p._gsb_rd.append(t_t1)
                            ev_all.append(t_t1)
                            if e == 0:
                                ta = p.comp("vector", lambda E, mt=mt, c0=c0: E.tensor_copy(
                                    out=acc[:, mt, c0:c0 + T], in_=t1sb[:, :]), deps=[t_t1])
                            else:
                                ta = p.comp("vector", lambda E, mt=mt, c0=c0: E.tensor_add(
                                    out=acc[:, mt, c0:c0 + T], in0=acc[:, mt, c0:c0 + T],
                                    in1=t1sb[:, :]), deps=[t_t1] + accd[-32:-31])
                            accd.append(ta)
                st_prev = []
                for ch in range(NCH):
                    c0 = ch * T
                    ldx2 = p.dma("sync", xtc[:], src[:, c0:c0 + T].rearrange(
                        "(a p) t -> p a t", p=128), deps=accd + ldxs[-1] + st_prev)
                    st_prev = []
                    for mt in range(8):
                        tf = p.comp("vector", lambda E, mt=mt, c0=c0: E.tensor_add(
                            out=xtc[:, mt, :], in0=xtc[:, mt, :], in1=acc[:, mt, c0:c0 + T]),
                            deps=[ldx2] + accd)
                        st_prev.append(p.dma("gpsimd", dst[mt * 128:(mt + 1) * 128, c0:c0 + T],
                                             xtc[:, mt, :], deps=[tf]))

        def emit_l1_mamba(src, dst):
            mb = MB()
            mb.is_l0 = False
            mb.kzero = kzero
            mb.eye16 = eye16
            mb.ones_row = ones_row
            mb.scr_y, mb.scr_ds = scr_y, scr_ds
            mb.scr_xi, mb.scr_z, mb.scr_cr = scr_xi, scr_z, scr_cr
            mb.spS = lambda b, dt, n: spcol(b, dt, n)
            mb.spP = lambda b, dt, n: spcol(2 + b, dt, n)
            mb.Hcol = lambda b, dt, n: Htile[:, b * 256 + dt * 16 + n:
                                             b * 256 + dt * 16 + n + 1]
            with ctx.scope():
                bar0 = p.barrier()
                wbst = sb(ctx, "wbst", [128, 16, 32], F32)
                wb16 = sb(ctx, "wb16", [128, 16, 32], BF16)
                convw = sb(ctx, "convw", [128, 16, 4], F32)
                convb = sb(ctx, "convb", [128, 16], F32)
                bdel = sb(ctx, "bdel", [128, 16], F32)
                lng = sb(ctx, "lng", [128, 16], F32)
                lnb = sb(ctx, "lnb", [128, 16], F32)
                arow = sb(ctx, "arow", [1, 16], F32)
                ar1 = sb(ctx, "ar1", [1, 16], F32)
                ar2 = sb(ctx, "ar2", [1, 16], F32)
                acol = sb(ctx, "acol", [128, 16], F32)
                mb.acol_fn = lambda dt, n: acol[:, n:n + 1]
                wl = [p.dma("sync", dst_, src_, deps=bar0) for dst_, src_ in [
                    (convw[:], convw2_in), (convb[:], convb2_in), (bdel[:], bdel_in),
                    (arow[:], alog2_in), (lng[:], lng2_in), (lnb[:], lnb2_in),
                    (wbst[:, :, 0:16], wb_in.rearrange("k p n -> p k n")),
                    (wbst[:, :, 16:32], wcc_in.rearrange("k p n -> p k n"))]]
                wc = [p.comp("scalar", lambda E: E.activation(out=wb16[:], in_=wbst[:],
                             func=AF.Copy), deps=wl)] + wc0
                t_a = p.comp("scalar", lambda E: E.activation(
                    out=ar1[:], in_=arow[:], func=AF.Abs), deps=wl)
                t_a = p.comp("scalar", lambda E: E.activation(
                    out=ar1[:], in_=ar1[:], func=AF.Exp, scale=-1.0), deps=[t_a])
                t_a = p.comp("scalar", lambda E: E.activation(
                    out=ar1[:], in_=ar1[:], func=AF.Ln, bias=1.0), deps=[t_a])
                t_a2 = p.comp("scalar", lambda E: E.activation(
                    out=ar2[:], in_=arow[:], func=AF.Relu), deps=wl)
                t_arow = p.comp("vector", lambda E: E.tensor_add(
                    out=ar1[:], in0=ar1[:], in1=ar2[:]), deps=[t_a, t_a2])
                bnds = sb(ctx, "bnds", [128, 8, 6], F32)
                tb1 = p.dma("sync", bnds[:, :, 0:3],
                            src[:, SEG - 3:SEG].rearrange("(a p) t -> p a t", p=128),
                            deps=bar0)
                tb2 = p.dma("sync", bnds[:, :, 3:6],
                            src[:, TOK - 3:TOK].rearrange("(a p) t -> p a t", p=128),
                            deps=bar0)
                tc3 = emit_gather(p, ctx, G, bnds.rearrange("p a t -> p (a t)"),
                                  cc3_in, cc3_out, 48, [tb1, tb2])
                bsf = [sb(ctx, f"bsf{i}", [128, 8, 6], F32) for i in range(2)]
                tsel = p.comp("vector", lambda E: E.memset(bsel[:], 0.0), deps=[tc3])
                bsf_rd = {0: [], 1: []}
                for r in range(8):
                    sl = r % 2
                    tl = p.dma("sync", bsf[sl][:],
                               cc3_out[r * 128:(r + 1) * 128, :].rearrange(
                                   "p (a t) -> p a t", a=8), deps=[tc3] + bsf_rd[sl])
                    tsel = p.comp("vector", lambda E, r=r, sl=sl: E.scalar_tensor_tensor(
                        out=bsel[:], in0=bsf[sl][:], scalar=G.hsel[:, r:r + 1],
                        in1=bsel[:], op0=ALU.mult, op1=ALU.add), deps=[tl, tsel])
                    bsf_rd[sl] = [tsel]
                bseld = [tsel]

                def _phaseA():
                    xtc = sb(ctx, "xtc", [128, 8, TE], F32)
                    xn16 = sb(ctx, "xn16", [128, 8, T], BF16)
                    xn16e = sb(ctx, "xn16e", [128, 8, 3], BF16)
                    lpre16 = sb(ctx, "lpre16", [128, 16, TE], BF16)
                    left16 = sb(ctx, "left16", [128, 16, T], BF16)
                    g16 = sb(ctx, "g16", [128, 2, T], BF16)
                    delta16 = sb(ctx, "delta16", [128, 16, T], BF16)
                    bbc16 = sb(ctx, "bbc16", [128, 16, T], BF16)
                    cbc16 = sb(ctx, "cbc16", [128, 16, T], BF16)
                    bc1 = sb(ctx, "bc1", [1, 2, T], F32)
                    mb.bc1 = bc1
                    dsc = sb(ctx, "dsc", [128, 2, T], F32)
                    mb.du16 = sb(ctx, "du16", [128, 2, T], BF16)
                    mb.u16 = sb(ctx, "u16", [128, 2, T], BF16)
                    mb.tmp16 = sb(ctx, "tmp16", [128, NR, T], BF16)
                    mb.dA = sb(ctx, "dA", [128, NR, T], F32)
                    mb.hS = sb(ctx, "hS", [128, 2, T], F32)
                    mb.Ds = sb(ctx, "Ds", [128, 2, T], F32)
                    mb.z16zero = sb(ctx, "zz", [128, T], BF16)
                    mb.dstate = sb(ctx, "dstate", [128, 16], F32)
                    mb.ysb = sb(ctx, "ysb", [128, 2, T], F32)
                    ln = LnTmp(ctx, T)
                    cacc = ln.t1
                    bcst = ln.t1
                    wkps = [ps(ctx, f"wkps{i}", [128, T], F32) for i in range(2)]
                    mb.ybank = ps(ctx, "ybank", [128, T], F32)
                    mb.bcbank = ps(ctx, "bcbank", [128, T], F32)
                    mb.g = 0
                    mb.last_ya = []
                    mb.ysb_rd = {0: [], 1: []}
                    mb.ds_rd = {0: [], 1: []}
                    mb.du_rd = {0: [], 1: []}
                    mb.da_rd = {s: [] for s in range(NR)}
                    mb.u_rd = {0: [], 1: []}
                    mb.h_rd = {0: [], 1: []}
                    mb.tmp_rd = {s: [] for s in range(NR)}
                    ms = MS2()
                    stgL = [sb(ctx, f"stgL_{i}", [128, 16, 128], F32) for i in range(2)]
                    w16L = [sb(ctx, f"w16L_{i}", [128, 16, 128], BF16) for i in range(2)]
                    t_ab = p.comp("tensor", lambda E: E.matmul(
                        out=mb.bcbank[:, 0:16], lhsT=ones_row[:], rhs=ar1[:],
                        start=True, stop=True), deps=[t_arow])
                    t_ac = p.comp("vector", lambda E: E.tensor_scalar(
                        out=acol[:], in0=mb.bcbank[:, 0:16], scalar1=-1.0,
                        scalar2=None, op0=ALU.mult), deps=[t_ab])
                    wc2 = wc + [
                        t_ac,
                        p.comp("vector", lambda E: E.memset(mb.z16zero[:], 0.0)),
                        p.comp("vector", lambda E: E.memset(lpre16[:, :, 0:3], 0.0))]
                    mb.t1_rd = []
                    mb.cbc_rd = []
                    mb.xtc_free = []
                    grd = {0: [], 1: []}

                    def frontA(ch):
                        b, j = ch // 2, ch % 2
                        cb = b * SEG + j * T
                        bar = p.barrier() if ch == 0 else []
                        if j == 0:
                            ldx = [p.dma("sync", xtc[:, :, 3:3 + T],
                                         src[:, cb:cb + T].rearrange("(a p) t -> p a t", p=128),
                                         deps=bar + mb.xtc_free)]
                            tcp = p.comp("vector", lambda E, b=b: E.tensor_copy(
                                out=xtc[:, :, 0:3], in_=bsel[:, :, b * 3:(b + 1) * 3]),
                                deps=bseld + bar + mb.xtc_free)
                            ldx = ldx + [tcp]
                        else:
                            ldx = [p.dma("sync", xtc[:],
                                         src[:, cb - 3:cb + T].rearrange("(a p) t -> p a t", p=128),
                                         deps=bar + mb.xtc_free)]
                        lnd, _ = emit_ln(p, ln, xtc, 8, ones_col[:], ones_row[:], xn16,
                                         ldx + mb.t1_rd, kn, T_=T, col0=3)
                        if j == 0:
                            lnde, _ = emit_ln(p, ln, xtc, 8, ones_col[:], ones_row[:],
                                              xn16e, ldx + lnd + mb.t1_rd, kn, T_=3, col0=0)
                            lnd = lnd + lnde
                        mb.xtc_free = lnd[-1:]

                        def lp_evac(mt, bank, deps):
                            return p.comp("vector", lambda E, mt=mt, bank=bank: E.tensor_copy(
                                out=lpre16[:, mt, 3:3 + T], in_=bank[:, :]), deps=deps)

                        def lpe_evac(mt, bank, deps):
                            return p.comp("scalar", lambda E, mt=mt, bank=bank: E.activation(
                                out=lpre16[:, mt, 0:3], in_=bank[:, 0:3], func=AF.Copy), deps=deps)

                        lpd = mm_stream2(p, ms, stgL, w16L, lambda mt: wl_in[mt],
                                         8, 16, lambda kt: xn16[:, kt, :], lp_evac,
                                         wkps, lnd, bar,
                                         ext=((lambda kt: xn16e[:, kt, :]), lpe_evac,
                                              3, lnd, 16) if j == 0 else None,
                                         scr=wl16s, first=(ch == 0))

                        def g_evac(mt, bank, deps, ch=ch):
                            t = p.comp("scalar", lambda E, mt=mt, bank=bank: E.activation(
                                out=g16[:, mt % 2, :], in_=bank[:, :], func=AF.Silu),
                                deps=deps + grd[mt % 2])
                            td = p.dma("scalar", scr_z[ch, mt], g16[:, mt % 2, :],
                                       deps=[t])
                            grd[mt % 2] = [td]
                            return t
                        gd = mm_stream2(p, ms, stgL, w16L, lambda mt: wr_in[mt],
                                        8, 16, lambda kt: xn16[:, kt, :], g_evac,
                                        wkps, lnd, bar + lpd,
                                        scr=wr16s, first=(ch == 0))
                        return dict(ch=ch, b=b, j=j, cb=cb, lpd=lpd, gd=gd)

                    def midA(fa):
                        ch = fa["ch"]
                        lpd = fa["lpd"]
                        gd = fa["gd"]
                        conv_d = []
                        for dt in range(16):
                            cap = cacc[:, 0:T] if dt % 2 == 0 else dsc[:, 0, 0:T]
                            tc_ = p.comp("vector", lambda E, dt=dt, cap=cap: E.tensor_scalar(
                                out=cap, in0=lpre16[:, dt, 0:T],
                                scalar1=convw[:, dt, 0:1], scalar2=convb[:, dt:dt + 1],
                                op0=ALU.mult, op1=ALU.add), deps=lpd + conv_d[-2:])
                            for k in range(1, 4):
                                tc_ = p.comp("vector", lambda E, dt=dt, k=k, cap=cap: E.scalar_tensor_tensor(
                                    out=cap, in0=lpre16[:, dt, k:k + T],
                                    scalar=convw[:, dt, k:k + 1], in1=cap,
                                    op0=ALU.mult, op1=ALU.add), deps=[tc_])
                            conv_d.append(p.comp("scalar", lambda E, dt=dt, cap=cap: E.activation(
                                out=left16[:, dt, :], in_=cap, func=AF.Silu),
                                deps=[tc_]))
                        t_carry = p.comp("vector", lambda E: E.tensor_copy(
                            out=lpre16[:, :, 0:3], in_=lpre16[:, :, T:T + 3]), deps=conv_d)
                        t_b = None
                        t_c = None
                        for kt in range(16):
                            t_b = p.comp("tensor", lambda E, kt=kt: E.matmul(
                                out=wkps[0][0:16, :], lhsT=wb16[:, kt, 0:16],
                                rhs=left16[:, kt, :], start=(kt == 0), stop=(kt == 15)),
                                deps=conv_d + wc2 + lpd + [t_carry] +
                                (ms.bank_rd[0] if kt == 0 else []))
                        for kt in range(16):
                            t_c = p.comp("tensor", lambda E, kt=kt: E.matmul(
                                out=wkps[1][0:16, :], lhsT=wb16[:, kt, 16:32],
                                rhs=left16[:, kt, :], start=(kt == 0), stop=(kt == 15)),
                                deps=conv_d + wc2 + (ms.bank_rd[1] if kt == 0 else []))
                        t_bs = p.comp("vector", lambda E: E.tensor_copy(
                            out=bcst[0:16, 0:T], in_=wkps[0][0:16, :]), deps=[t_b])
                        t_cs = p.comp("vector", lambda E: E.tensor_copy(
                            out=bcst[32:48, 0:T], in_=wkps[1][0:16, :]), deps=[t_c])
                        ms.bank_rd[0] = ms.bank_rd[0] + [t_bs]
                        ms.bank_rd[1] = ms.bank_rd[1] + [t_cs]
                        t_crs = p.dma("sync", scr_cr[ch].rearrange("n o t -> (n o) t"),
                                      bcst[32:48, 0:T], deps=[t_cs])
                        deld = []
                        dsc_rd = {0: [], 1: []}

                        def d_evac(mt, bank, deps):
                            ds2 = mt % 2
                            ta1 = p.comp("scalar", lambda E, mt=mt, bank=bank: E.activation(
                                out=ln.sq[:, 0:T], in_=bank[:, :], func=AF.Abs,
                                bias=bdel[:, mt:mt + 1]), deps=deps + deld[-1:])
                            ta2 = p.comp("scalar", lambda E: E.activation(
                                out=ln.sq[:, 0:T], in_=ln.sq[:, 0:T], func=AF.Exp,
                                scale=-1.0), deps=[ta1])
                            ta3 = p.comp("scalar", lambda E: E.activation(
                                out=ln.sq[:, 0:T], in_=ln.sq[:, 0:T], func=AF.Ln,
                                bias=1.0), deps=[ta2])
                            ta4 = p.comp("scalar", lambda E, mt=mt, bank=bank, ds2=ds2: E.activation(
                                out=dsc[:, ds2, :], in_=bank[:, :], func=AF.Relu,
                                bias=bdel[:, mt:mt + 1]), deps=[ta3] + dsc_rd[ds2])
                            t5 = p.comp("vector", lambda E, ds2=ds2: E.tensor_add(
                                out=dsc[:, ds2, :], in0=ln.sq[:, 0:T],
                                in1=dsc[:, ds2, :]), deps=[ta4])
                            t6 = p.comp("vector", lambda E, mt=mt, ds2=ds2: E.tensor_scalar(
                                out=delta16[:, mt, :], in0=dsc[:, ds2, :], scalar1=1e-4,
                                scalar2=10.0, op0=ALU.max, op1=ALU.min), deps=[t5])
                            dsc_rd[ds2] = [t6]
                            deld.append(t6)
                            return ta4

                        mm_stream2(p, ms, stgL, w16L, lambda mt: wd_in[mt],
                                   16, 16, lambda kt: left16[:, kt, :], d_evac,
                                   wkps, conv_d + [t_bs, t_cs, t_crs], gd,
                                   scr=wd16s, first=(ch == 0))
                        bc_d = []
                        bmm = []
                        tmv = None
                        for n in range(32):
                            row = n if n < 16 else n + 16
                            tmv = p.dma("sync", bc1[0:1, n % 2, 0:T], bcst[row:row + 1, 0:T],
                                        deps=[t_bs, t_cs] + bmm[-2:-1] + deld[-1:])
                            tb = p.comp("tensor", lambda E, n=n: E.matmul(
                                out=mb.bcbank[:, :], lhsT=ones_row[:],
                                rhs=bc1[0:1, n % 2, 0:T], start=True, stop=True),
                                deps=[tmv] + bc_d[-1:])
                            bmm.append(tb)
                            dstt = bbc16 if n < 16 else cbc16
                            bc_d.append(p.comp("scalar", lambda E, n=n, dstt=dstt: E.activation(
                                out=dstt[:, n % 16, :], in_=mb.bcbank[:, :], func=AF.Copy),
                                deps=[tb] + mb.cbc_rd))
                        mb.t1_rd = [t_crs, tmv]
                        return dict(ch=ch, deld=deld, t_carry=t_carry,
                                    bc_d=bc_d, conv_d=conv_d)

                    fas = [None] * NCH
                    mds = [None] * NCH
                    fas[0] = frontA(0)
                    mds[0] = midA(fas[0])
                    for ch in range(NCH):
                        if ch + 1 < NCH:
                            fas[ch + 1] = frontA(ch + 1)
                        md = mds[ch]
                        scan_A(p, ctx, mb, ch, delta16, left16, bbc16, cbc16,
                               md["deld"] + [md["t_carry"]], md["bc_d"],
                               md["conv_d"], wc2)
                        if ch + 1 < NCH:
                            mds[ch + 1] = midA(fas[ch + 1])
                with ctx.scope():
                    _phaseA()
                barx = p.barrier()
                tc2 = emit_gather(p, ctx, G, sp_all[:, :], cc2_in, cc2_out, 1024, barx)
                wcH = emit_fold(p, ctx, G, cc2_out, Htile, tc2)

                def _phaseB():
                    xtc = sb(ctx, "xtcB", [128, 8, T], F32)
                    cbc16 = sb(ctx, "cbcB", [128, 16, T], BF16)
                    y2 = sb(ctx, "y2B", [128, 16, T], BF16)
                    y3 = sb(ctx, "y3B", [128, 16, T], BF16)
                    mb.dsb = sb(ctx, "dsb", [128, 2, T], F32)
                    mb.yacc = sb(ctx, "yacc", [128, 2, T], F32)
                    mb.xib = sb(ctx, "xib", [128, 2, T], BF16)
                    mb.dec = sb(ctx, "dec", [128, NR, T], F32)
                    mb.tmpc2 = sb(ctx, "tmpc2", [128, 4, T], F32)
                    ones_c16 = sb(ctx, "ones_c16", [128, 1], BF16)
                    bc1 = sb(ctx, "bc1B", [1, 2, T], F32)
                    mb.bc1 = bc1
                    ln = LnTmp(ctx, T)
                    wkps = [ps(ctx, f"wkpsB{i}", [128, T], F32) for i in range(2)]
                    mb.bcbank = ps(ctx, "bcbankB", [128, T], F32)
                    mb.g = 0
                    mb.dsb_rd = {0: [], 1: []}
                    mb.yacc_rd = {0: [], 1: []}
                    mb.xib_rd = {0: [], 1: []}
                    mb.dec_rd = {s: [] for s in range(NR)}
                    mb.tmpc_rd = {i: [] for i in range(4)}
                    ms = MS2()
                    stgB2 = [sb(ctx, f"stgB2_{i}", [128, 16, 128], F32) for i in range(2)]
                    w16B2 = [sb(ctx, f"w16B2_{i}", [128, 16, 128], BF16) for i in range(2)]
                    wc3 = [p.comp("vector", lambda E: E.memset(ones_c16[:], 1.0))]
                    a_eps = kn.row(EPS)
                    a_mh = kn.row(-0.5)
                    for ch in range(NCH):
                        b, j = ch // 2, ch % 2
                        cb = b * SEG + j * T
                        bar = p.barrier()
                        bc_d = emit_cbc(p, ctx, mb, ch, cbc16, barx if ch == 0 else bar)
                        y2d = corr_B(p, ctx, mb, ch, cbc16, y2, bar, wcH, bc_d,
                                     ldbar=barx)
                        sq_t, last_q, last_s = [], [], []
                        for i in range(16):
                            tq = p.comp("scalar", lambda E, i=i: E.activation(
                                out=ln.sq[:, 0:T], in_=y2[:, i, :], func=AF.Square),
                                deps=y2d + last_q + sq_t[-1:])
                            sq_t.append(tq)
                            last_q = [p.comp("tensor", lambda E, i=i: E.matmul(
                                out=ln.ps_q[:, 0:T], lhsT=ones_col[:], rhs=ln.sq[:, 0:T],
                                start=(i == 0), stop=(i == 15)), deps=[tq])]
                            last_s = [p.comp("tensor", lambda E, i=i: E.matmul(
                                out=ln.ps_s[:, 0:T], lhsT=ones_c16[:], rhs=y2[:, i, :],
                                start=(i == 0), stop=(i == 15)), deps=y2d + wc3)]
                        a_inv2 = kn.row(1.0 / 2048.0)
                        t_mean = p.comp("scalar", lambda E: E.activation(
                            out=ln.r_mean[:, 0:T], in_=ln.ps_s[:, 0:T], func=AF.Copy,
                            scale=a_inv2), deps=last_s + kn.dep())
                        t_msq = p.comp("scalar", lambda E: E.activation(
                            out=ln.r_msq[:, 0:T], in_=ln.ps_q[:, 0:T], func=AF.Copy,
                            scale=a_inv2), deps=last_q + kn.dep())
                        t_m2 = p.comp("scalar", lambda E: E.activation(
                            out=ln.r_tmp[:, 0:T], in_=ln.r_mean[:, 0:T], func=AF.Square),
                            deps=[t_mean])
                        t_var = p.comp("vector", lambda E: E.tensor_sub(
                            out=ln.r_msq[:, 0:T], in0=ln.r_msq[:, 0:T],
                            in1=ln.r_tmp[:, 0:T]), deps=[t_msq, t_m2])
                        t_lnv = p.comp("scalar", lambda E: E.activation(
                            out=ln.r_tmp[:, 0:T], in_=ln.r_msq[:, 0:T], func=AF.Ln,
                            bias=a_eps), deps=[t_var])
                        t_rst = p.comp("scalar", lambda E: E.activation(
                            out=ln.r_rstd[:, 0:T], in_=ln.r_tmp[:, 0:T], func=AF.Exp,
                            scale=a_mh), deps=[t_lnv])
                        t_mb = p.comp("tensor", lambda E: E.matmul(
                            out=ln.ps_mb[:, 0:T], lhsT=ones_row[:], rhs=ln.r_mean[:, 0:T],
                            start=True, stop=True), deps=[t_mean] + last_s)
                        t_rb = p.comp("tensor", lambda E: E.matmul(
                            out=ln.ps_rb[:, 0:T], lhsT=ones_row[:], rhs=ln.r_rstd[:, 0:T],
                            start=True, stop=True), deps=[t_rst])
                        y3d = []
                        for i in range(16):
                            ta = p.comp("vector", lambda E, i=i: E.tensor_sub(
                                out=ln.sq[:, 0:T], in0=y2[:, i, :], in1=ln.ps_mb[:, 0:T]),
                                deps=[t_mb] + y2d + sq_t + y3d[-1:])
                            tb2 = p.comp("vector", lambda E, i=i: E.tensor_mul(
                                out=ln.sq[:, 0:T], in0=ln.sq[:, 0:T], in1=ln.ps_rb[:, 0:T]),
                                deps=[t_rb, ta])
                            y3d.append(p.comp("vector", lambda E, i=i: E.tensor_scalar(
                                out=y3[:, i, :], in0=ln.sq[:, 0:T], scalar1=lng[:, i:i + 1],
                                scalar2=lnb[:, i:i + 1], op0=ALU.mult, op1=ALU.add),
                                deps=[tb2] + wl))
                        ldx2 = [p.dma("sync", xtc[:], src[:, cb:cb + T].rearrange(
                            "(a p) t -> p a t", p=128), deps=bar)]

                        def o_evac(mt, bank, deps):
                            t = p.comp("vector", lambda E, mt=mt, bank=bank: E.tensor_add(
                                out=xtc[:, mt, :], in0=bank[:, :], in1=xtc[:, mt, :]),
                                deps=deps + ldx2)
                            p.dma("gpsimd", dst[mt * 128:(mt + 1) * 128, cb:cb + T],
                                  xtc[:, mt, :], deps=[t])
                            return t

                        mm_stream2(p, ms, stgB2, w16B2, lambda mt: wo2_in[mt],
                                   16, 8, lambda kt: y3[:, kt, :], o_evac, wkps,
                                   y3d, bar, scr=wo216s, first=(ch == 0))
                with ctx.scope():
                    _phaseB()

        stages = [
            (1, lambda: emit_l0_mamba(), sA),
            (3, lambda: emit_l0ffn(sA, sB), sB),
            (4, lambda: emit_l0attnb(sB, sA), sA),
            (5, lambda: emit_l0ffn(sA, sB), sB),
            (6, lambda: emit_l1attna(sB, sA), sA),
            (7, lambda: emit_l1ffn(sA, sB, first=True), sB),
            (8, lambda: emit_l1_mamba(sB, sA), sA),
            (10, lambda: emit_l1ffn(sA, xt_out, first=False), xt_out),
        ]
        cur = None
        for (sid, fn, buf) in stages:
            if sid > stop_after:
                break
            fn()
            cur = buf
        if dbg is not None and cur is not None and cur is not xt_out:
            fin = p.barrier()
            p.dma("gpsimd", dbg, cur, deps=fin)
        p.emit()
    return nc



_EYE = np.eye(128, dtype=np.float32)


def prep_weights(inputs):
    f32 = np.float32
    A = np.asarray

    def pc(v, n):
        return np.ascontiguousarray(A(v, f32).reshape(n, 128).T)

    w = {}
    w["ip2"] = np.ascontiguousarray(
        A(inputs["L0_attn_a_in_proj"], f32).reshape(8, 128, 32, 128).transpose(2, 1, 0, 3))
    w["op2"] = np.ascontiguousarray(
        A(inputs["L0_attn_a_out_proj"], f32).reshape(16, 128, 8, 128).transpose(2, 1, 0, 3))
    w["xp_l"] = np.ascontiguousarray(A(inputs["L0_attn_a_x_proj"], f32).reshape(16, 128, 96).transpose(1, 0, 2))
    w["dtw_l"] = np.ascontiguousarray(A(inputs["L0_attn_a_dt_w"], f32).reshape(64, 16, 128))
    w["rms_c"] = pc(inputs["L0_attn_a_rms_w"], 8)
    w["convw_c"] = np.ascontiguousarray(A(inputs["L0_attn_a_conv_w"], f32).reshape(16, 128, 4).transpose(1, 0, 2))
    w["convb_c"] = pc(inputs["L0_attn_a_conv_b"], 16)
    w["dtb_c"] = pc(inputs["L0_attn_a_dt_b"], 16)
    w["alog_c"] = np.ascontiguousarray(A(inputs["L0_attn_a_A_log"], f32).reshape(16, 128, 16).transpose(1, 0, 2))
    w["d_c"] = pc(inputs["L0_attn_a_D"], 16)
    w["win_l"] = np.ascontiguousarray(A(inputs["L0_ffn_Win"], f32).reshape(8, 128, 64).transpose(1, 0, 2))
    w["bin_c"] = A(inputs["L0_ffn_bin_"], f32).reshape(64, 1)
    w["w1"] = A(inputs["L0_ffn_W1"], f32)
    w["b1"] = A(inputs["L0_ffn_b1"], f32)
    w["w2"] = A(inputs["L0_ffn_W2"], f32)
    w["b2"] = A(inputs["L0_ffn_b2"], f32).reshape(64, 1)
    w["wout_l"] = np.ascontiguousarray(A(inputs["L0_ffn_Wout"], f32).reshape(64, 8, 128))
    w["bout_c"] = np.ascontiguousarray(A(inputs["L0_ffn_bout"], f32).reshape(8, 128).T)
    def tiler(wt):
        return np.ascontiguousarray(A(wt, f32).T.reshape(8, 128, D).transpose(1, 0, 2))
    w["wq_t"] = tiler(inputs["L0_attn_b_Wq"])
    w["wk_t"] = tiler(inputs["L0_attn_b_Wk"])
    w["wv_t"] = tiler(inputs["L0_attn_b_Wv"])
    w["bq_c"] = np.ascontiguousarray(A(inputs["L0_attn_b_bq"], f32).reshape(8, 128).T)
    w["bk_c"] = np.ascontiguousarray(A(inputs["L0_attn_b_bk"], f32).reshape(8, 128).T)
    w["bv_r"] = A(inputs["L0_attn_b_bv"], f32).reshape(1, D)
    W = A(inputs["L1_attn_a_W_qkvb"], f32)
    perm = []
    for comp in range(3):
        for h in range(16):
            perm.extend(range(h * 193 + comp * 64, h * 193 + comp * 64 + 64))
    perm.extend(h * 193 + 192 for h in range(16))
    Wp = W[np.array(perm)]
    WpT = np.zeros((1024, 3200), f32)
    WpT[:, 0:3088] = Wp.T
    w["wqkvb2"] = np.ascontiguousarray(
        WpT.reshape(8, 128, 25, 128).transpose(2, 1, 0, 3))
    w["woA2"] = np.ascontiguousarray(
        A(inputs["L1_attn_a_W_o"], f32).T.reshape(8, 128, 8, 128).transpose(2, 1, 0, 3))
    w["lng_c"] = np.ascontiguousarray(A(inputs["L1_attn_a_ln_g"], f32).reshape(8, 128).T)
    w["lnb_c"] = np.ascontiguousarray(A(inputs["L1_attn_a_ln_b"], f32).reshape(8, 128).T)
    up = A(inputs["L1_ffn_up_proj"], f32)
    dn = A(inputs["L1_ffn_down_proj"], f32)
    w["up_t"] = np.ascontiguousarray(up.reshape(8, 8, 128, 16, 128).transpose(0, 3, 2, 1, 4))
    w["dn_t"] = np.ascontiguousarray(dn.reshape(8, 8, 128, 8, 128).transpose(0, 3, 2, 1, 4))
    w["rw_l"] = np.ascontiguousarray(A(inputs["L1_ffn_router_w"], f32).T.reshape(8, 128, 8))
    w["wl2"] = np.ascontiguousarray(
        A(inputs["L1_attn_b_W_left"], f32).T.reshape(8, 128, 16, 128).transpose(2, 1, 0, 3))
    w["wd2"] = np.ascontiguousarray(
        A(inputs["L1_attn_b_W_delta"], f32).T.reshape(16, 128, 16, 128).transpose(2, 1, 0, 3))
    w["wr2"] = np.ascontiguousarray(
        A(inputs["L1_attn_b_W_right"], f32).T.reshape(8, 128, 16, 128).transpose(2, 1, 0, 3))
    w["wo22"] = np.ascontiguousarray(
        A(inputs["L1_attn_b_W_out"], f32).T.reshape(16, 128, 8, 128).transpose(2, 1, 0, 3))
    w["wb_l"] = np.ascontiguousarray(A(inputs["L1_attn_b_W_B"], f32).T.reshape(16, 128, 16))
    w["wc_l"] = np.ascontiguousarray(A(inputs["L1_attn_b_W_C"], f32).T.reshape(16, 128, 16))
    w["convw2_c"] = np.ascontiguousarray(A(inputs["L1_attn_b_conv_w"], f32).reshape(16, 128, 4).transpose(1, 0, 2))
    w["convb2_c"] = pc(inputs["L1_attn_b_conv_b"], 16)
    w["bdel_c"] = pc(inputs["L1_attn_b_b_delta"], 16)
    w["alog_r"] = A(inputs["L1_attn_b_A_log"], f32).reshape(1, 16)
    w["lng2_c"] = pc(inputs["L1_attn_b_ln_g"], 16)
    w["lnb2_c"] = pc(inputs["L1_attn_b_ln_b"], 16)
    w["eye"] = _EYE
    return w


def make_inmaps(inputs):
    w = prep_weights(inputs)
    x = np.asarray(inputs["L0_attn_a_x"], np.float32)
    xT = np.ascontiguousarray(x.reshape(B * L, D).T)
    maps = []
    for c in range(NCORES):
        m = dict(w)
        x0e = np.zeros((D, 2 * (SEG + 3)), np.float32)
        for b in range(B):
            lo = b * L + c * SEG
            x0e[:, b * (SEG + 3) + 3: (b + 1) * (SEG + 3)] = xT[:, lo:lo + SEG]
            if c > 0:
                x0e[:, b * (SEG + 3): b * (SEG + 3) + 3] = xT[:, lo - 3:lo]
        m["x0e"] = x0e
        om = np.zeros((128, 8), np.float32)
        om[:, c] = 1.0
        m["ownm"] = om
        pm1 = np.zeros((128, 8), np.float32)
        pm1[:, :c] = 1.0
        m["pm1"] = pm1
        hs = np.zeros((128, 8), np.float32)
        if c > 0:
            hs[:, c - 1] = 1.0
        m["hsel"] = hs
        maps.append(m)
    return maps


def assemble(results, key="xt_new"):
    xT = np.empty((D, B * L), np.float32)
    for c in range(NCORES):
        r = results[c][key]
        for b in range(B):
            xT[:, b * L + c * SEG:(b * L) + (c + 1) * SEG] = \
                r[:, b * SEG:(b + 1) * SEG]
    return np.ascontiguousarray(xT.T).reshape(B, L, D)


def kernel(**inputs):
    if "nc" not in _CACHE:
        _CACHE["nc"] = build_fused()
    nc = _CACHE["nc"]
    maps = make_inmaps(inputs)
    res = run_bass_kernel_spmd(nc, maps, core_ids=list(range(NCORES)))
    return assemble(res.results)
